# version 16
# baseline (speedup 1.0000x reference)
import numpy as np
import ml_dtypes
import concourse.bass as bass
import concourse.mybir as mybir
from concourse.bass_utils import run_bass_kernel_spmd

F32 = mybir.dt.float32
BF16 = mybir.dt.bfloat16
AF = mybir.ActivationFunctionType
ALU = mybir.AluOpType
AX = mybir.AxisListType

BF = ml_dtypes.bfloat16

D = 2048
HID = 2048
NH = 16
T = 1024
NT = 2
NBLK = 4
DC = 16
E = 8
RK = 16
SCALING = 2.0
LN_EPS = 1e-5
RMS_EPS = 1e-6
DDBIG = -1.0e5

ENGS = ("tensor", "vector", "scalar", "gpsimd", "sync")
ESEM = {"tensor": "PE", "vector": "DV", "scalar": "AC", "gpsimd": "PL", "sync": None}
PAGE = 8192


def _ap_ranges(ap):
    t = ap.tensor
    if t.name.startswith("pb"):
        return (t.name, 0, 2048)
    esz = mybir.dt.size(ap.dtype)
    rowbytes = None
    shp = list(t.shape)
    if len(shp) == 2 and shp[0] <= 128 and t.name.startswith("arena"):
        rowbytes = shp[1] * mybir.dt.size(t.dtype)
    lo = ap.offset * esz
    span = 0
    for step, cnt in ap.ap:
        sb = step * esz
        if rowbytes is not None and cnt <= 128 and sb == rowbytes:
            continue
        span += sb * (cnt - 1)
    return (t.name, lo, lo + span + esz)


class Prog:

    def __init__(self):
        self.ops = {e: [] for e in ENGS}
        self.cnt = {}
        self.K = {e: {} for e in ENGS}
        self.SNAP = {}
        self.pages = {}
        self.hist = {}
        self.hidx = 0
        self.dma_slots = set()

    def _entries_overlapping(self, name, lo, hi):
        out = set()
        for pg in range(lo // PAGE, hi // PAGE + 1):
            for i in self.pages.get((name, pg), ()):
                ent = self.hist[i]
                if ent[5] and ent[1] < hi and lo < ent[2]:
                    out.add(i)
        return out

    def _record(self, name, lo, hi, mark, is_write):
        i = self.hidx
        self.hidx += 1
        self.hist[i] = [name, lo, hi, mark, is_write, True]
        for pg in range(lo // PAGE, hi // PAGE + 1):
            self.pages.setdefault((name, pg), set()).add(i)
        return i

    def _auto_waits(self, eng, r, w):
        need = {}
        kill = []
        for ap in r:
            name, lo, hi = _ap_ranges(ap)
            for i in self._entries_overlapping(name, lo, hi):
                ent = self.hist[i]
                if ent[4]:
                    s, v = ent[3]
                    if need.get(s, 0) < v:
                        need[s] = v
        for ap in w:
            name, lo, hi = _ap_ranges(ap)
            for i in self._entries_overlapping(name, lo, hi):
                ent = self.hist[i]
                s, v = ent[3]
                if need.get(s, 0) < v:
                    need[s] = v
                if ent[1] >= lo and ent[2] <= hi:
                    kill.append(i)
        for i in kill:
            self.hist[i][5] = False
        own = ESEM[eng]
        out = []
        for s, v in need.items():
            if eng == "tensor" and s == "PE":
                continue
            if self.K[eng].get(s, 0) >= v:
                continue
            out.append((s, v))
        return out

    def _apply_waits(self, eng, waits):
        K = self.K[eng]
        for s, v in waits:
            if K.get(s, 0) >= v:
                continue
            snap = self.SNAP.get((s, v))
            if snap:
                for s2, v2 in snap.items():
                    if K.get(s2, 0) < v2:
                        K[s2] = v2
            if K.get(s, 0) < v:
                K[s] = v

    def _push(self, eng, fn, waits, incs):
        self.ops[eng].append({"waits": waits, "fn": fn, "incs": incs})

    def op(self, eng, fn, r=(), w=(), extra=(), mark_sem=None, mark_amt=1):
        ws = dict()
        for m in extra:
            if m is None:
                continue
            s, v = m
            if v > 0 and ws.get(s, 0) < v:
                ws[s] = v
        for s, v in self._auto_waits(eng, r, w):
            if ws.get(s, 0) < v:
                ws[s] = v
        waits = [(s, v) for s, v in ws.items() if not (eng == "tensor" and s == "PE")
                 and self.K[eng].get(s, 0) < v]
        self._apply_waits(eng, waits)
        if mark_sem is None:
            mark_sem = ESEM[eng]
        self.cnt[mark_sem] = self.cnt.get(mark_sem, 0) + (1 if mark_amt is None else mark_amt)
        mark = (mark_sem, self.cnt[mark_sem])
        self._push(eng, fn, waits, [(mark_sem, mark_amt)])
        snap = dict(self.K[eng])
        if ESEM[eng] is not None and mark_sem == ESEM[eng]:
            snap[mark_sem] = mark[1]
        else:
            snap[mark_sem] = mark[1]
        self.SNAP[mark] = snap
        for ap in r:
            name, lo, hi = _ap_ranges(ap)
            self._record(name, lo, hi, mark, False)
        for ap in w:
            name, lo, hi = _ap_ranges(ap)
            self._record(name, lo, hi, mark, True)
        return mark

    def gop(self, eng, items, extra=()):
        ws = dict()
        for m in extra:
            if m is None:
                continue
            s, v = m
            if v > 0 and ws.get(s, 0) < v:
                ws[s] = v
        for fn, r, w in items:
            for s, v in self._auto_waits(eng, r, w):
                if ws.get(s, 0) < v:
                    ws[s] = v
        waits = [(s, v) for s, v in ws.items() if not (eng == "tensor" and s == "PE")
                 and self.K[eng].get(s, 0) < v]
        self._apply_waits(eng, waits)
        sem = ESEM[eng]
        self.cnt[sem] = self.cnt.get(sem, 0) + 1
        mark = (sem, self.cnt[sem])
        for i, (fn, r, w) in enumerate(items):
            self._push(eng, fn, waits if i == 0 else [],
                       [(sem, 1)] if i == len(items) - 1 else [])
        snap = dict(self.K[eng])
        snap[sem] = mark[1]
        self.SNAP[mark] = snap
        for fn, r, w in items:
            for ap in r:
                name, lo, hi = _ap_ranges(ap)
                self._record(name, lo, hi, mark, False)
            for ap in w:
                name, lo, hi = _ap_ranges(ap)
                self._record(name, lo, hi, mark, True)
        return mark

    def dma(self, eng, slot, out, in_, extra=(), track_read=True, track_write=True):
        sem = "dma_" + slot
        self.dma_slots.add(sem)
        r = [in_] if track_read else []
        w = [out] if track_write else []
        return self.op(eng, lambda e, o=out, i=in_: e.dma_start(out=o, in_=i),
                       r=r, w=w, extra=extra, mark_sem=sem, mark_amt=16)

    def barrier_wait(self, eng, extra):
        self.op(eng, lambda e: e.nop(), extra=extra)

    def replay(self, eng_name, engine, sems):
        hi = {}
        for ent in self.ops[eng_name]:
            for s, v in ent["waits"]:
                if hi.get(s, 0) >= v:
                    continue
                engine.wait_ge(sems[s], v)
                hi[s] = v
            inst = ent["fn"](engine)
            for s, a in ent["incs"]:
                if a is None:
                    inst.then_inc(sems[s])
                else:
                    inst.then_inc(sems[s], a)


def build_nc(dbg=False):
    nc = bass.Bass("TRN2", target_bir_lowering=False, debug=False)

    d_x = nc.dram_tensor("x", [D, T], F32, kind="ExternalInput").ap()
    d_qkvu = nc.dram_tensor("qkvuS", [64, 128, DC, 128], F32, kind="ExternalInput").ap()
    d_wout = nc.dram_tensor("woutS", [16, 128, DC, 128], F32, kind="ExternalInput").ap()
    d_wbase = nc.dram_tensor("wbaseS", [16, 128, DC, 128], F32, kind="ExternalInput").ap()
    d_aT = nc.dram_tensor("aT", [128, DC, 128], F32, kind="ExternalInput").ap()
    d_rT = nc.dram_tensor("rT", [128, DC, 8], F32, kind="ExternalInput").ap()
    d_bw = nc.dram_tensor("bwF", [128, HID], F32, kind="ExternalInput").ap()
    d_bT = nc.dram_tensor("bT", [128, 16], F32, kind="ExternalInput").ap()
    d_slope = nc.dram_tensor("slopeB", [128, 16], F32, kind="ExternalInput").ap()
    d_mask8 = nc.dram_tensor("maskv8", [128, 8], F32, kind="ExternalInput").ap()
    d_qdi = nc.dram_tensor("QDI", [128, 256], F32, kind="ExternalInput").ap()
    d_ddi = nc.dram_tensor("DDI", [128, 512], F32, kind="ExternalInput").ap()
    d_kdi = nc.dram_tensor("KDI", [128, 2], F32, kind="ExternalInput").ap()
    d_bdi = nc.dram_tensor("BDIc", [128, 4], F32, kind="ExternalInput").ap()
    d_idb = nc.dram_tensor("identB", [128, 128], BF16, kind="ExternalInput").ap()
    d_e2 = nc.dram_tensor("E2", [8, 128], BF16, kind="ExternalInput").ap()
    d_idf8 = nc.dram_tensor("IDF8", [8, 8], F32, kind="ExternalInput").ap()
    d_y = nc.dram_tensor("y", [D, T], F32, kind="ExternalOutput").ap()

    d_part = nc.dram_tensor("part", [D, T], F32).ap()
    d_slocal = nc.dram_tensor("s_local", [128, NH * 128], F32).ap()
    d_sall = nc.dram_tensor("s_all", [8 * 128, NH * 128], F32, addr_space="Shared").ap()

    dbg_o = {}
    dbg_n = [0]
    if dbg:
        for nm, shp, dt in [("dbg_xn", [128, DC * T], BF16), ("dbg_q", [128, T], BF16),
                            ("dbg_k", [128, T], BF16), ("dbg_v", [128, T], BF16),
                            ("dbg_o", [128, NH * T], BF16), ("dbg_kv", [128, NH * 128], F32),
                            ("dbg_u", [128, NH * T], BF16), ("dbg_g", [128, NH * T], BF16),
                            ("dbg_hm", [128, T], BF16), ("dbg_kvin", [128, NH * 128], F32),
                            ("dbg_qd", [128, NH * T], BF16)]:
            dbg_o[nm] = nc.dram_tensor(nm, shp, dt, kind="ExternalOutput").ap()

    P = Prog()

    from contextlib import ExitStack
    with ExitStack() as ctx:
        AR = ctx.enter_context(nc.sbuf_tensor("arena", [128, 53196], F32))
        pb = [ctx.enter_context(nc.psum_tensor(f"pb{i}", [128, 512], F32)) for i in range(8)]
        pb0, pb1, pb2, pb3, pb4, pb5, pb6, pb7 = pb

        pos = [0]

        def alloc(n, align=64):
            p = (pos[0] + align - 1) // align * align
            pos[0] = p + n
            assert pos[0] <= 53196, f"arena overflow at {pos[0]}"
            return p

        def c(off, n):
            return AR[:, off:off + n]

        def cb(off, n):
            return AR[:, off:off + n].bitcast(BF16)

        o_xn = alloc(8192)
        XN = cb(o_xn, 8192).rearrange("p (c t) -> p c t", c=DC)
        o_o = alloc(8192)
        O_ = cb(o_o, 8192).rearrange("p (h t) -> p h t", h=NH)
        o_qd = alloc(8192)
        QDl = cb(o_qd, 8192).rearrange("p (h t) -> p h t", h=NH)
        Xf = c(o_o, 16384).rearrange("p (c t) -> p c t", c=DC)
        o_u = alloc(8192)
        U_ = cb(o_u, 8192).rearrange("p (h t) -> p h t", h=NH)
        o_qdec = alloc(512)
        o_ddec = alloc(1024)
        o_kdec = alloc(8)
        QDEC = None
        BDP = c(alloc(64), 64)
        SLP = c(alloc(16), 16)
        o_qdi = alloc(256)
        QDI = c(o_qdi, 256)
        o_ddi = alloc(512)
        DDI = c(o_ddi, 512)
        KDI = c(alloc(2), 2)
        BDIc = c(alloc(4), 4)
        HMSK = cb(alloc(512), 512)
        HSB = cb(o_qdec, 512)
        o_kv = alloc(2048)
        KV = c(o_kv, 2048).rearrange("p (h e) -> p h e", h=NH)
        KVINf = c(o_kv, 2048)
        IDN = cb(alloc(64), 64)
        o_e2 = alloc(64)
        E2C = AR[0:8, o_e2:o_e2 + 64].bitcast(BF16)
        o_idf = alloc(8)
        IDF8 = AR[0:8, o_idf:o_idf + 8]
        ALH = cb(alloc(1024), 1024).rearrange("p (c j) -> p c j", c=DC)
        RTR = cb(alloc(64), 64).rearrange("p (c j) -> p c j", c=DC)
        BWL = cb(alloc(1024), 1024)
        BBT = c(alloc(16), 16)
        MSK8 = c(alloc(8), 8)
        ONE1 = c(alloc(1), 1)
        ONEB = cb(alloc(1), 1)
        o_oner = alloc(128)
        ONER = AR[0:1, o_oner:o_oner + 128]
        RROW = c(alloc(512), 512)
        R2ROW = c(alloc(512), 512)
        NSTG = 2
        o_stg = alloc(NSTG * 1024)
        STG = [c(o_stg + i * 1024, 1024).rearrange("p (c j) -> p c j", c=8) for i in range(NSTG)]
        o_wq = alloc(1024)
        WQB = cb(o_wq, 1024).rearrange("p (c j) -> p c j", c=DC)
        o_wk = alloc(1024)
        WKB = cb(o_wk, 1024).rearrange("p (c j) -> p c j", c=DC)
        o_wv = alloc(1024)
        WVB = cb(o_wv, 1024).rearrange("p (c j) -> p c j", c=DC)
        o_qs = alloc(512)
        QS2 = [cb(o_qs, 512)]
        o_ks = alloc(512)
        KS2 = [cb(o_ks, 512)]
        o_vs = alloc(512)
        VS2 = [cb(o_vs, 512)]
        o_qs1 = alloc(512)
        QS2.append(cb(o_qs1, 512))
        o_ks1 = alloc(512)
        KS2.append(cb(o_ks1, 512))
        o_vs1 = alloc(512)
        VS2.append(cb(o_vs1, 512))
        o_ov = alloc(576)
        KVB = cb(o_ov + 512, 64)
        XB = [cb(o_ov + i * 256, 256) for i in range(2)]
        OSC4 = [c(o_qs + i * 256, 256) for i in range(4)]
        SQB4 = [cb(o_qs + 1024 + i * 128, 128) for i in range(4)]
        KVS4 = [cb(o_qs + 1536 + i * 64, 64) for i in range(4)]
        o_d = alloc(2560)
        MROW = c(o_d, 512)
        XRS = [c(o_d + i * 512, 512) for i in range(3)]
        OST = [c(o_d + 1536 + i * 512, 512) for i in range(2)]
        GBUF = c(o_wk, 2048)
        GTMP = c(o_qs, 2048)
        LTF = AR[0:8, o_stg:o_stg + 1024]
        LTT = c(o_d, 8)
        MXC = c(o_d + 16, 1)
        OHT = cb(o_d + 24, 4)
        o_mfb = o_d + 64
        MFB = AR[0:8, o_mfb:o_mfb + 512].bitcast(BF16)
        WOB = cb(o_wq, 1024).rearrange("p (c j) -> p c j", c=DC)
        WBB = cb(o_wk, 1024).rearrange("p (c j) -> p c j", c=DC)
        PST = [c(o_qdec + i * 512, 512) for i in range(2)]
        PRD = [c(o_qs1 + i * 512, 512) for i in range(3)]
        RROW4 = c(o_d + 1536, 1024)
        QDEC2 = [cb(o_qdec, 512).rearrange("p (l t) -> p l t", l=4),
                 cb(o_d + 576, 512).rearrange("p (l t) -> p l t", l=4)]
        DDEC2 = [cb(o_ddec, 1024).rearrange("p (l t) -> p l t", l=4),
                 cb(o_d + 1088, 1024).rearrange("p (l t) -> p l t", l=4)]
        KDEC2 = [c(o_kdec, 8), c(o_d + 2112, 8)]
        VTD = [[cb(b + i * 64, 64) for i in range(2)] for b in (o_ov, o_d)]
        KTD = [[cb(b + 128 + i * 64, 64) for i in range(2)] for b in (o_ov, o_d)]
        QKPD = [[cb(b + 256 + i * 128, 128) for i in range(2)] for b in (o_ov, o_d)]


        for xh in range(2):
            for ch in range(2):
                P.dma("sync", f"x{xh}{ch}",
                      Xf[:, ch * 8:(ch + 1) * 8, xh * 512:(xh + 1) * 512],
                      d_x.rearrange("(c p) t -> p c t", p=128)[:, ch * 8:(ch + 1) * 8,
                                                              xh * 512:(xh + 1) * 512],
                      track_read=False)
        for key, dst, src in [("QDI", QDI, d_qdi), ("DDI", DDI, d_ddi), ("KDI", KDI, d_kdi),
                              ("BDI", BDIc, d_bdi), ("SLP", SLP, d_slope), ("IDN", IDN, d_idb),
                              ("E2", E2C, d_e2), ("IDF8", IDF8, d_idf8),
                              ("BBT", BBT, d_bT), ("MSK8", MSK8, d_mask8)]:
            P.dma("sync", "c_" + key, dst, src, track_read=False)
        P.dma("sync", "bw", GBUF[:, 0:2048], d_bw, track_read=False)
        P.op("vector", lambda e: e.tensor_copy(BWL, GBUF[:, 0:2048]),
             r=[GBUF[:, 0:2048]], w=[BWL])
        P.dma("sync", "aT", GBUF[:, 0:2048].rearrange("p (c j) -> p c j", c=DC),
              d_aT, track_read=False)
        P.dma("sync", "rT", GTMP[:, 0:128].rearrange("p (c j) -> p c j", c=DC),
              d_rT, track_read=False)
        P.op("vector", lambda e: e.tensor_copy(ALH, GBUF[:, 0:2048].rearrange("p (c j) -> p c j", c=DC)),
             r=[GBUF[:, 0:2048]], w=[ALH])
        P.op("vector", lambda e: e.tensor_copy(RTR, GTMP[:, 0:128].rearrange("p (c j) -> p c j", c=DC)),
             r=[GTMP[:, 0:128]], w=[RTR])
        P.op("vector", lambda e: e.memset(ONE1, 1.0), w=[ONE1])
        P.op("vector", lambda e: e.memset(ONEB[:, 0:1], 1.0), w=[ONEB[:, 0:1]])
        P.op("vector", lambda e: e.memset(ONER, 1.0), w=[ONER])

        def ln_half(nt):
            tsl = slice(nt * 512, (nt + 1) * 512)
            for dc in range(DC):
                sl = dc % 2
                P.op("scalar", lambda e, dc=dc, tsl=tsl, sl=sl: e.activation(
                    XB[sl], Xf[:, dc, tsl], AF.Square),
                    r=[Xf[:, dc, tsl]], w=[XB[sl]])
                P.op("tensor", lambda e, dc=dc, tsl=tsl: e.matmul(
                    pb4[0:1, 0:512], ONE1, Xf[:, dc, tsl],
                    start=(dc == 0), stop=(dc == DC - 1), skip_group_check=True),
                    r=[ONE1, Xf[:, dc, tsl]], w=[pb4[0:1, 0:512]])
                P.op("tensor", lambda e, dc=dc, sl=sl: e.matmul(
                    pb5[0:1, 0:512], ONEB[:, 0:1], XB[sl],
                    start=(dc == 0), stop=(dc == DC - 1), skip_group_check=True),
                    r=[ONEB[:, 0:1], XB[sl]], w=[pb5[0:1, 0:512]])
            P.op("vector", lambda e: e.tensor_scalar(MROW[0:1, 0:512], pb4[0:1, 0:512],
                                                     1.0 / D, None, ALU.mult),
                 r=[pb4[0:1, 0:512]], w=[MROW[0:1, 0:512]])
            P.op("vector", lambda e: e.tensor_scalar(R2ROW[0:1, 0:512], pb5[0:1, 0:512],
                                                     1.0 / D, LN_EPS, ALU.mult, ALU.add),
                 r=[pb5[0:1, 0:512]], w=[R2ROW[0:1, 0:512]])
            P.op("vector", lambda e: e.tensor_tensor(RROW[0:1, 0:512], MROW[0:1, 0:512],
                                                     MROW[0:1, 0:512], ALU.mult),
                 r=[MROW[0:1, 0:512]], w=[RROW[0:1, 0:512]])
            P.op("vector", lambda e: e.tensor_tensor(R2ROW[0:1, 0:512], R2ROW[0:1, 0:512],
                                                     RROW[0:1, 0:512], ALU.subtract),
                 r=[R2ROW[0:1, 0:512], RROW[0:1, 0:512]], w=[R2ROW[0:1, 0:512]])
            P.op("scalar", lambda e: e.activation(RROW[0:1, 0:512], R2ROW[0:1, 0:512], AF.Sqrt),
                 r=[R2ROW[0:1, 0:512]], w=[RROW[0:1, 0:512]])
            P.op("vector", lambda e: e.reciprocal(R2ROW[0:1, 0:512], RROW[0:1, 0:512]),
                 r=[RROW[0:1, 0:512]], w=[R2ROW[0:1, 0:512]])
            P.op("tensor", lambda e: e.matmul(pb3[:, 0:512], ONER, MROW[0:1, 0:512],
                                              start=True, stop=True, skip_group_check=True),
                 r=[ONER, MROW[0:1, 0:512]], w=[pb3[:, 0:512]])
            P.op("tensor", lambda e: e.matmul(pb2[:, 0:512], ONER, R2ROW[0:1, 0:512],
                                              start=True, stop=True, skip_group_check=True),
                 r=[ONER, R2ROW[0:1, 0:512]], w=[pb2[:, 0:512]])
            for dc in range(DC):
                P.op("vector", lambda e, dc=dc, tsl=tsl: e.tensor_tensor(
                    Xf[:, dc, tsl], Xf[:, dc, tsl], pb3[:, 0:512], ALU.subtract),
                    r=[Xf[:, dc, tsl], pb3[:, 0:512]], w=[Xf[:, dc, tsl]])
                P.op("vector", lambda e, dc=dc, tsl=tsl: e.tensor_tensor(
                    XN[:, dc, tsl], Xf[:, dc, tsl], pb2[:, 0:512], ALU.mult),
                    r=[Xf[:, dc, tsl], pb2[:, 0:512]], w=[XN[:, dc, tsl]])
        stg_i = [0]

        def load_stripe(dram_ap, dstW, eng="vector"):
            for half in range(2):
                sl = stg_i[0] % NSTG
                stg_i[0] += 1
                P.dma("sync", f"stg{sl}", STG[sl],
                      dram_ap[:, half * 8:(half + 1) * 8, :], track_read=False)
                if eng == "vector":
                    mk = P.op("vector", lambda e, sl=sl, half=half, dstW=dstW: e.tensor_copy(
                        dstW[:, half * 8:(half + 1) * 8, :], STG[sl]),
                        r=[STG[sl]], w=[dstW[:, half * 8:(half + 1) * 8, :]])
                else:
                    mk = P.op("scalar", lambda e, sl=sl, half=half, dstW=dstW: e.copy(
                        dstW[:, half * 8:(half + 1) * 8, :], STG[sl]),
                        r=[STG[sl]], w=[dstW[:, half * 8:(half + 1) * 8, :]])
            return mk

        def mm_chain(bank, dstW, nt, half):
            return [
                (lambda e, dc=dc, nt=nt, bank=bank, dstW=dstW: e.matmul(
                    bank[:, 0:512], dstW[:, dc, :], XN[:, dc, nt * 512:(nt + 1) * 512],
                    start=(dc == 0), stop=(dc == DC - 1), skip_group_check=True),
                 [dstW[:, dc, :], XN[:, dc, nt * 512:(nt + 1) * 512]],
                 [bank[:, 0:512]])
                for dc in range(half * 8, half * 8 + 8)]

        def proj_group(dstW, out_sb, silu, nt):
            bank = pb[nt]
            P.gop("tensor", mm_chain(bank, dstW, nt, 0))
            P.gop("tensor", mm_chain(bank, dstW, nt, 1))
            if silu:
                P.op("scalar", lambda e, nt=nt, bank=bank, out_sb=out_sb: e.activation(
                    out_sb[:, nt * 512:(nt + 1) * 512], bank[:, 0:512], AF.Silu),
                    r=[bank[:, 0:512]], w=[out_sb[:, nt * 512:(nt + 1) * 512]])
            else:
                P.op("scalar", lambda e, nt=nt, bank=bank, out_sb=out_sb: e.copy(
                    out_sb[:, nt * 512:(nt + 1) * 512], bank[:, 0:512]),
                    r=[bank[:, 0:512]], w=[out_sb[:, nt * 512:(nt + 1) * 512]])

        def emit_masks(g):
            par = g % 2
            QDEC, DDEC, KDEC = QDEC2[par], DDEC2[par], KDEC2[par]
            for hl2 in range(4):
                h2 = g * 4 + hl2
                P.op("scalar", lambda e, hl2=hl2, h2=h2, QDEC=QDEC: e.activation(
                    QDEC[:, hl2, :], QDI, AF.Exp, scale=SLP[:, h2:h2 + 1]),
                    r=[QDI, SLP[:, h2:h2 + 1]], w=[QDEC[:, hl2, :]])
                P.op("scalar", lambda e, hl2=hl2, h2=h2, DDEC=DDEC: e.activation(
                    DDEC[:, hl2, :], DDI, AF.Exp, scale=SLP[:, h2:h2 + 1]),
                    r=[DDI, SLP[:, h2:h2 + 1]], w=[DDEC[:, hl2, :]])
                P.op("scalar", lambda e, hl2=hl2, h2=h2, KDEC=KDEC: e.activation(
                    KDEC[:, hl2 * 2:hl2 * 2 + 2], KDI, AF.Exp, scale=SLP[:, h2:h2 + 1]),
                    r=[KDI, SLP[:, h2:h2 + 1]], w=[KDEC[:, hl2 * 2:hl2 * 2 + 2]])
                P.op("scalar", lambda e, h2=h2: e.activation(
                    BDP[:, h2 * 4:(h2 + 1) * 4], BDIc, AF.Exp, scale=SLP[:, h2:h2 + 1]),
                    r=[BDIc, SLP[:, h2:h2 + 1]], w=[BDP[:, h2 * 4:(h2 + 1) * 4]])

        def b_produce(h, hl, hp, blk):
            base = blk * 256
            bp = blk % 2
            mpar = (h // 4) % 2
            QDEC, DDEC, KDEC = QDEC2[mpar], DDEC2[mpar], KDEC2[mpar]
            VT, KT, QKP = VTD[bp], KTD[bp], QKPD[bp]
            QS, KS, VS = QS2[hp], KS2[hp], VS2[hp]
            for sc in range(2):
                P.op("tensor", lambda e, sc=sc, base=base, KS=KS, QS=QS: e.matmul(
                    pb[4 + sc][:, 0:256], KS[:, base + sc * 128:base + sc * 128 + 128],
                    QS[:, base:base + 256], start=True, stop=True, skip_group_check=True),
                    r=[KS[:, base + sc * 128:base + sc * 128 + 128], QS[:, base:base + 256]],
                    w=[pb[4 + sc][:, 0:256]])
            for sc in range(2):
                P.op("tensor", lambda e, sc=sc, base=base, KS=KS: e.transpose(
                    pb2[:, sc * 64:sc * 64 + 64].bitcast(BF16),
                    KS[:, base + sc * 128:base + sc * 128 + 128], IDN),
                    r=[KS[:, base + sc * 128:base + sc * 128 + 128], IDN],
                    w=[pb2[:, sc * 64:sc * 64 + 64]])
                P.op("tensor", lambda e, sc=sc, base=base, VS=VS: e.transpose(
                    pb3[:, sc * 64:sc * 64 + 64].bitcast(BF16),
                    VS[:, base + sc * 128:base + sc * 128 + 128], IDN),
                    r=[VS[:, base + sc * 128:base + sc * 128 + 128], IDN],
                    w=[pb3[:, sc * 64:sc * 64 + 64]])
            for sc in range(2):
                P.op("vector", lambda e, sc=sc, hl=hl, QKP=QKP: e.tensor_tensor(
                    QKP[sc], pb[4 + sc][:, 0:256], DDEC[:, hl, sc * 256:(sc + 1) * 256],
                    ALU.mult),
                    r=[pb[4 + sc][:, 0:256], DDEC[:, hl, sc * 256:(sc + 1) * 256]],
                    w=[QKP[sc]])
            P.op("vector", lambda e, h=h, hl=hl, base=base, QS=QS: e.tensor_tensor(
                QDl[:, h, base:base + 256], QS[:, base:base + 256], QDEC[:, hl, :],
                ALU.mult),
                r=[QS[:, base:base + 256], QDEC[:, hl, :]], w=[QDl[:, h, base:base + 256]])
            for sc in range(2):
                P.op("scalar", lambda e, sc=sc, hl=hl, KT=KT: e.activation(
                    KT[sc], pb2[:, sc * 64:sc * 64 + 64].bitcast(BF16), AF.Copy,
                    scale=KDEC[:, hl * 2 + sc:hl * 2 + sc + 1]),
                    r=[pb2[:, sc * 64:sc * 64 + 64], KDEC[:, hl * 2 + sc:hl * 2 + sc + 1]],
                    w=[KT[sc]])
                P.op("scalar", lambda e, sc=sc, VT=VT: e.copy(
                    VT[sc], pb3[:, sc * 64:sc * 64 + 64].bitcast(BF16)),
                    r=[pb3[:, sc * 64:sc * 64 + 64]], w=[VT[sc]])

        def b_consume(h, hl, hp, blk):
            base = blk * 256
            bp = blk % 2
            VT, KT, QKP = VTD[bp], KTD[bp], QKPD[bp]
            P.op("tensor", lambda e, VT=VT, QKP=QKP: e.matmul(
                pb6[:, 0:256], VT[0], QKP[0],
                start=True, stop=False, skip_group_check=True),
                r=[VT[0], QKP[0]], w=[pb6[:, 0:256]])
            P.op("tensor", lambda e, blk=blk, VT=VT, QKP=QKP: e.matmul(
                pb6[:, 0:256], VT[1], QKP[1],
                start=False, stop=(blk == 0), skip_group_check=True),
                r=[VT[1], QKP[1]], w=[pb6[:, 0:256]])
            if blk > 0:
                P.op("tensor", lambda e, h=h, base=base: e.matmul(
                    pb6[:, 0:256], KVB, QDl[:, h, base:base + 256],
                    start=False, stop=True, skip_group_check=True),
                    r=[KVB, QDl[:, h, base:base + 256]], w=[pb6[:, 0:256]])
            P.op("tensor", lambda e, KT=KT, VT=VT: e.matmul(
                pb7[:, 0:128], KT[0], VT[0],
                start=True, stop=False, skip_group_check=True),
                r=[KT[0], VT[0]], w=[pb7[:, 0:128]])
            P.op("tensor", lambda e, KT=KT, VT=VT: e.matmul(
                pb7[:, 0:128], KT[1], VT[1],
                start=False, stop=True, skip_group_check=True),
                r=[KT[1], VT[1]], w=[pb7[:, 0:128]])
            if blk == 0:
                P.op("vector", lambda e, h=h: e.tensor_copy(KV[:, h, :], pb7[:, 0:128]),
                     r=[pb7[:, 0:128]], w=[KV[:, h, :]])
            else:
                P.op("vector", lambda e, h=h: e.tensor_scalar(
                    KV[:, h, :], KV[:, h, :], BDP[:, h * 4 + 1:h * 4 + 2], None, ALU.mult),
                    r=[KV[:, h, :], BDP[:, h * 4 + 1:h * 4 + 2]], w=[KV[:, h, :]])
                P.op("vector", lambda e, h=h: e.tensor_tensor(
                    KV[:, h, :], KV[:, h, :], pb7[:, 0:128], ALU.add),
                    r=[KV[:, h, :], pb7[:, 0:128]], w=[KV[:, h, :]])
            if blk < NBLK - 1:
                P.op("vector", lambda e, h=h: e.tensor_copy(KVB, KV[:, h, :]),
                     r=[KV[:, h, :]], w=[KVB])
            P.op("scalar", lambda e, h=h, base=base: e.copy(
                O_[:, h, base:base + 256], pb6[:, 0:256]),
                r=[pb6[:, 0:256]], w=[O_[:, h, base:base + 256]])

        emit_masks(0)
        load_stripe(d_qkvu[0], WQB, eng="scalar")
        load_stripe(d_qkvu[16], WKB, eng="scalar")
        load_stripe(d_qkvu[32], WVB, eng="scalar")
        ln_half(0)
        proj_group(WQB, QS2[0], True, 0)
        proj_group(WKB, KS2[0], True, 0)
        ln_half(1)
        proj_group(WVB, VS2[0], False, 0)
        proj_group(WQB, QS2[0], True, 1)
        proj_group(WKB, KS2[0], True, 1)
        proj_group(WVB, VS2[0], False, 1)

        for h in range(NH):
            hl = h % 4
            hp = h % 2
            if hl == 1 and h // 4 + 1 < 4:
                emit_masks(h // 4 + 1)
            nxt = []
            if h + 1 < NH:
                nxt = [(d_qkvu[h + 1], WQB, QS2[1 - hp], True),
                       (d_qkvu[16 + h + 1], WKB, KS2[1 - hp], True),
                       (d_qkvu[32 + h + 1], WVB, VS2[1 - hp], False)]
            b_produce(h, hl, hp, 0)
            b_produce(h, hl, hp, 1)
            b_consume(h, hl, hp, 0)
            if nxt:
                load_stripe(nxt[0][0], nxt[0][1], eng="scalar")
            b_consume(h, hl, hp, 1)
            if nxt:
                proj_group(nxt[0][1], nxt[0][2], nxt[0][3], 0)
            b_produce(h, hl, hp, 2)
            if nxt:
                load_stripe(nxt[1][0], nxt[1][1], eng="scalar")
                proj_group(nxt[0][1], nxt[0][2], nxt[0][3], 1)
            b_consume(h, hl, hp, 2)
            if nxt:
                proj_group(nxt[1][1], nxt[1][2], nxt[1][3], 0)
            b_produce(h, hl, hp, 3)
            if nxt:
                load_stripe(nxt[2][0], nxt[2][1])
                proj_group(nxt[1][1], nxt[1][2], nxt[1][3], 1)
            b_consume(h, hl, hp, 3)
            if nxt:
                proj_group(nxt[2][1], nxt[2][2], nxt[2][3], 0)
                proj_group(nxt[2][1], nxt[2][2], nxt[2][3], 1)

            if dbg and h == 1:
                dbg_n[0] += 1; P.dma("scalar", f"dbg{dbg_n[0]}", dbg_o["dbg_q"], QS2[1], track_write=False)
                dbg_n[0] += 1; P.dma("scalar", f"dbg{dbg_n[0]}", dbg_o["dbg_k"], KS2[1], track_write=False)
                dbg_n[0] += 1; P.dma("scalar", f"dbg{dbg_n[0]}", dbg_o["dbg_v"], VS2[1], track_write=False)

        P.dma("gpsimd", "sl", d_slocal, KVINf)
        m_cc = P.op("gpsimd", lambda e: e.collective_compute(
            "AllGather", ALU.bypass, replica_groups=[[0, 1, 2, 3, 4, 5, 6, 7]],
            ins=[d_slocal.opt()], outs=[d_sall.opt()]),
            r=[d_slocal], w=[d_sall], mark_sem="CC", mark_amt=None)

        for nt in range(NT):
            P.gop("tensor", [
                (lambda e, dc=dc, nt=nt: e.matmul(
                    pb5[:, 0:512], ALH[:, dc, :], XN[:, dc, nt * 512:(nt + 1) * 512],
                    start=(dc == 0), stop=(dc == DC - 1), skip_group_check=True),
                 [ALH[:, dc, :], XN[:, dc, nt * 512:(nt + 1) * 512]], [pb5[:, 0:512]])
                for dc in range(DC)])
            P.op("scalar", lambda e, nt=nt: e.copy(HSB[:, nt * 512:(nt + 1) * 512], pb5[:, 0:512]),
                 r=[pb5[:, 0:512]], w=[HSB[:, nt * 512:(nt + 1) * 512]])
        for nt in range(NT):
            P.gop("tensor", [
                (lambda e, dc=dc, nt=nt: e.matmul(
                    pb3[0:8, 0:512], RTR[:, dc, :], XN[:, dc, nt * 512:(nt + 1) * 512],
                    start=(dc == 0), stop=(dc == DC - 1), skip_group_check=True),
                 [RTR[:, dc, :], XN[:, dc, nt * 512:(nt + 1) * 512]], [pb3[0:8, 0:512]])
                for dc in range(DC)])
            P.op("vector", lambda e, nt=nt: e.tensor_copy(
                LTF[:, nt * 512:(nt + 1) * 512], pb3[0:8, 0:512]),
                r=[pb3[0:8, 0:512]], w=[LTF[:, nt * 512:(nt + 1) * 512]])
        for i in range(8):
            P.op("tensor", lambda e, i=i: e.transpose(
                pb2[:, 0:8], LTF[:, i * 128:(i + 1) * 128], IDF8),
                r=[LTF[:, i * 128:(i + 1) * 128], IDF8], w=[pb2[:, 0:8]])
            P.op("vector", lambda e: e.tensor_copy(LTT, pb2[:, 0:8]),
                 r=[pb2[:, 0:8]], w=[LTT])
            P.op("vector", lambda e: e.tensor_reduce(MXC, LTT, axis=AX.X, op=ALU.max),
                 r=[LTT], w=[MXC])
            P.op("vector", lambda e: e.tensor_scalar(OHT, LTT, MXC, None, ALU.is_equal),
                 r=[LTT, MXC], w=[OHT])
            P.op("tensor", lambda e: e.transpose(
                pb2[0:8, 128:192].bitcast(BF16), OHT, IDN),
                r=[OHT, IDN], w=[pb2[0:8, 128:192]])
            P.op("vector", lambda e, i=i: e.tensor_copy(
                MFB[:, i * 128:(i + 1) * 128], pb2[0:8, 128:192].bitcast(BF16)),
                r=[pb2[0:8, 128:192]], w=[MFB[:, i * 128:(i + 1) * 128]])
        for nt in range(NT):
            P.op("tensor", lambda e, nt=nt: e.matmul(
                pb6[:, 0:512], E2C, MFB[:, nt * 512:(nt + 1) * 512],
                start=True, stop=True, skip_group_check=True),
                r=[E2C, MFB[:, nt * 512:(nt + 1) * 512]], w=[pb6[:, 0:512]])
            P.op("vector", lambda e, nt=nt: e.tensor_tensor(
                HMSK[:, nt * 512:(nt + 1) * 512], HSB[:, nt * 512:(nt + 1) * 512],
                pb6[:, 0:512], ALU.mult),
                r=[HSB[:, nt * 512:(nt + 1) * 512], pb6[:, 0:512]],
                w=[HMSK[:, nt * 512:(nt + 1) * 512]])

        if dbg:
            dbg_n[0] += 1; P.dma("scalar", f"dbg{dbg_n[0]}", dbg_o["dbg_hm"], HMSK, track_write=False)


        WPAR = [WBB, WQB]
        for oc in range(16):
            WBc = WPAR[oc % 2]
            load_stripe(d_wbase[oc], WBc)
            for nt in range(NT):
                ti = oc * 2 + nt
                bank = pb[ti % 2]
                tsl = slice(nt * 512, (nt + 1) * 512)
                P.gop("tensor", [
                    (lambda e, dc=dc, nt=nt, bank=bank, WBc=WBc: e.matmul(
                        bank[:, 0:512], WBc[:, dc, :], XN[:, dc, nt * 512:(nt + 1) * 512],
                        start=(dc == 0), stop=False, skip_group_check=True),
                     [WBc[:, dc, :], XN[:, dc, nt * 512:(nt + 1) * 512]], [bank[:, 0:512]])
                    for dc in range(8)])
                items = [
                    (lambda e, dc=dc, nt=nt, bank=bank, WBc=WBc: e.matmul(
                        bank[:, 0:512], WBc[:, dc, :], XN[:, dc, nt * 512:(nt + 1) * 512],
                        start=False, stop=False, skip_group_check=True),
                     [WBc[:, dc, :], XN[:, dc, nt * 512:(nt + 1) * 512]], [bank[:, 0:512]])
                    for dc in range(8, DC)]
                items.append(
                    (lambda e, oc=oc, nt=nt, bank=bank: e.matmul(
                        bank[:, 0:512], BWL[:, oc * 128:(oc + 1) * 128],
                        HMSK[:, nt * 512:(nt + 1) * 512],
                        start=False, stop=True, skip_group_check=True),
                     [BWL[:, oc * 128:(oc + 1) * 128], HMSK[:, nt * 512:(nt + 1) * 512]],
                     [bank[:, 0:512]]))
                P.gop("tensor", items)
                psl = ti % 2
                P.op("scalar", lambda e, oc=oc, psl=psl, bank=bank: e.activation(
                    PST[psl], bank[:, 0:512], AF.Identity, bias=BBT[:, oc:oc + 1]),
                    r=[bank[:, 0:512], BBT[:, oc:oc + 1]], w=[PST[psl]])
                P.dma("scalar", f"pp{psl}", d_part[oc * 128:(oc + 1) * 128, tsl], PST[psl])

        GSC = c(o_d, 2048)
        GBH = [c(o_qs + i * 1024, 1024) for i in range(2)]
        for cidx in range(8):
            for hf in range(2):
                gbb = GBH[hf]
                csl = slice(hf * 1024, (hf + 1) * 1024)
                P.dma("sync", f"gb{hf}", gbb, d_sall[cidx * 128:(cidx + 1) * 128, csl])
                if cidx == 0:
                    P.op("vector", lambda e, gbb=gbb, csl=csl: e.tensor_scalar(
                        KVINf[:, csl], gbb, MSK8[:, 0:1], None, ALU.mult),
                        r=[gbb, MSK8[:, 0:1]], w=[KVINf[:, csl]])
                else:
                    P.op("vector", lambda e, cidx=cidx, gbb=gbb, csl=csl: e.tensor_scalar(
                        GSC[:, csl], gbb, MSK8[:, cidx:cidx + 1], None, ALU.mult),
                        r=[gbb, MSK8[:, cidx:cidx + 1]], w=[GSC[:, csl]])
                    P.op("vector", lambda e, csl=csl: e.tensor_tensor(
                        KVINf[:, csl], KVINf[:, csl], GSC[:, csl], ALU.add),
                        r=[KVINf[:, csl], GSC[:, csl]], w=[KVINf[:, csl]])

        if dbg:
            dbg_n[0] += 1; P.dma("scalar", f"dbg{dbg_n[0]}", dbg_o["dbg_kvin"], KVINf, track_write=False)

        RSB = c(o_qs + 1792, 256)

        def emit_umult(blk, heads):
            base = blk * 256
            for h in heads:
                eng = "vector" if h % 4 < 2 else "gpsimd"
                P.op(eng, lambda e, h=h, base=base: e.tensor_tensor(
                    O_[:, h, base:base + 256], O_[:, h, base:base + 256],
                    U_[:, h, base:base + 256], ALU.mult),
                    r=[O_[:, h, base:base + 256], U_[:, h, base:base + 256]],
                    w=[O_[:, h, base:base + 256]])

        def emit_rmult(blk):
            base = blk * 256
            P.op("tensor", lambda e, base=base: e.matmul(
                pb3[:, 0:256], ONER, RROW4[0:1, base:base + 256],
                start=True, stop=True, skip_group_check=True),
                r=[ONER, RROW4[0:1, base:base + 256]], w=[pb3[:, 0:256]])
            P.op("scalar", lambda e: e.copy(RSB, pb3[:, 0:256]),
                 r=[pb3[:, 0:256]], w=[RSB])
            for h in range(NH):
                eng = "vector" if h % 4 < 2 else "gpsimd"
                P.op(eng, lambda e, h=h, base=base: e.tensor_tensor(
                    O_[:, h, base:base + 256], O_[:, h, base:base + 256], RSB,
                    ALU.mult),
                    r=[O_[:, h, base:base + 256], RSB], w=[O_[:, h, base:base + 256]])

        ugen = []

        def u_head(h):
            WUc = [WQB, WBB][h % 2]
            load_stripe(d_qkvu[48 + h], WUc, eng="scalar")
            for nt in range(NT):
                bank = pb[nt]
                P.gop("tensor", mm_chain(bank, WUc, nt, 0))
                P.gop("tensor", mm_chain(bank, WUc, nt, 1))
                P.op("scalar", lambda e, nt=nt, h=h, bank=bank: e.copy(
                    U_[:, h, nt * 512:(nt + 1) * 512], bank[:, 0:512]),
                    r=[bank[:, 0:512]], w=[U_[:, h, nt * 512:(nt + 1) * 512]])

        cbank = [pb2, pb5, pb6, pb7]
        uh = [0]
        pend_um = []
        rm_done = [0]

        def flush_sched():
            keep = []
            for b, heads in pend_um:
                if max(heads) < uh[0]:
                    emit_umult(b, heads)
                else:
                    keep.append((b, heads))
            pend_um[:] = keep

        while uh[0] < 3:
            u_head(uh[0]); uh[0] += 1
        for blk in range(NBLK):
            base = blk * 256
            for h0 in range(0, NH, 4):
                if blk > 0 and h0 == 4 and rm_done[0] < blk:
                    emit_rmult(rm_done[0]); rm_done[0] += 1
                if uh[0] < NH:
                    u_head(uh[0]); uh[0] += 1
                for i in range(4):
                    h = h0 + i
                    P.op("vector", lambda e, h=h, blk=blk, i=i: e.tensor_scalar(
                        KVS4[i], KVINf[:, h * 128:(h + 1) * 128],
                        BDP[:, h * 4 + blk:h * 4 + blk + 1], None, ALU.mult),
                        r=[KVINf[:, h * 128:(h + 1) * 128], BDP[:, h * 4 + blk:h * 4 + blk + 1]],
                        w=[KVS4[i]])
                for i in range(4):
                    h = h0 + i
                    P.op("tensor", lambda e, h=h, base=base, i=i: e.matmul(
                        cbank[i][:, 0:256], KVS4[i], QDl[:, h, base:base + 256],
                        start=True, stop=True, skip_group_check=True),
                        r=[KVS4[i], QDl[:, h, base:base + 256]], w=[cbank[i][:, 0:256]])
                for i in range(4):
                    h = h0 + i
                    P.op("vector", lambda e, h=h, base=base, i=i: e.tensor_tensor(
                        OSC4[i], O_[:, h, base:base + 256], cbank[i][:, 0:256], ALU.add),
                        r=[O_[:, h, base:base + 256], cbank[i][:, 0:256]], w=[OSC4[i]])
                for i in range(4):
                    P.op("scalar", lambda e, i=i: e.activation(SQB4[i], OSC4[i], AF.Square),
                         r=[OSC4[i]], w=[SQB4[i]])
                for i in range(4):
                    h = h0 + i
                    P.op("tensor", lambda e, h=h, i=i: e.matmul(
                        pb4[0:1, 0:256], ONEB[:, 0:1], SQB4[i],
                        start=(h == 0), stop=(h == NH - 1), skip_group_check=True),
                        r=[ONEB[:, 0:1], SQB4[i]], w=[pb4[0:1, 0:256]])
                for i in range(4):
                    h = h0 + i
                    P.op("vector", lambda e, h=h, base=base, i=i: e.tensor_copy(
                        O_[:, h, base:base + 256], OSC4[i]),
                        r=[OSC4[i]], w=[O_[:, h, base:base + 256]])
                pend_um.append((blk, list(range(h0, h0 + 4))))
                flush_sched()
            P.op("vector", lambda e: e.tensor_scalar(R2ROW[0:1, 0:256], pb4[0:1, 0:256],
                                                     1.0 / HID, RMS_EPS, ALU.mult, ALU.add),
                 r=[pb4[0:1, 0:256]], w=[R2ROW[0:1, 0:256]])
            P.op("scalar", lambda e: e.activation(RROW[0:1, 0:256], R2ROW[0:1, 0:256], AF.Sqrt),
                 r=[R2ROW[0:1, 0:256]], w=[RROW[0:1, 0:256]])
            P.op("vector", lambda e, base=base: e.reciprocal(
                RROW4[0:1, base:base + 256], RROW[0:1, 0:256]),
                r=[RROW[0:1, 0:256]], w=[RROW4[0:1, base:base + 256]])
        while uh[0] < NH:
            u_head(uh[0]); uh[0] += 1
            flush_sched()
        flush_sched()
        assert not pend_um
        while rm_done[0] < NBLK:
            emit_rmult(rm_done[0]); rm_done[0] += 1
        if dbg:
            dbg_n[0] += 1; P.dma("scalar", f"dbg{dbg_n[0]}", dbg_o["dbg_u"], U_.rearrange("p h t -> p (h t)"), track_write=False)

        if dbg:
            dbg_n[0] += 1; P.dma("scalar", f"dbg{dbg_n[0]}", dbg_o["dbg_g"], O_.rearrange("p h t -> p (h t)"), track_write=False)

        for oc in range(16):
            WOc = [WOB, WBB][oc % 2]
            load_stripe(d_wout[oc], WOc, eng="scalar")
            for nt in range(NT):
                ti = oc * 2 + nt
                bank = pb[ti % 2]
                tsl = slice(nt * 512, (nt + 1) * 512)
                xsl = ti % 3
                P.dma("sync", f"xr{xsl}", XRS[xsl], d_x[oc * 128:(oc + 1) * 128, tsl],
                      track_read=False)
                prs = ti % 3
                P.dma("sync", f"pr{prs}", PRD[prs], d_part[oc * 128:(oc + 1) * 128, tsl])
                for half in range(2):
                    P.gop("tensor", [
                        (lambda e, hc=hc, nt=nt, bank=bank, WOc=WOc: e.matmul(
                            bank[:, 0:512], WOc[:, hc, :], O_[:, hc, nt * 512:(nt + 1) * 512],
                            start=(hc == 0), stop=(hc == 15), skip_group_check=True),
                         [WOc[:, hc, :], O_[:, hc, nt * 512:(nt + 1) * 512]], [bank[:, 0:512]])
                        for hc in range(half * 8, half * 8 + 8)])
                osl = ti % 2
                P.op("scalar", lambda e, osl=osl, bank=bank: e.copy(OST[osl], bank[:, 0:512]),
                     r=[bank[:, 0:512]], w=[OST[osl]])
                P.op("vector", lambda e, osl=osl, prs=prs: e.tensor_tensor(
                    OST[osl], OST[osl], PRD[prs], ALU.add),
                    r=[OST[osl], PRD[prs]], w=[OST[osl]])
                P.op("vector", lambda e, osl=osl, xsl=xsl: e.tensor_tensor(
                    OST[osl], OST[osl], XRS[xsl], ALU.add),
                    r=[OST[osl], XRS[xsl]], w=[OST[osl]])
                P.dma("scalar", f"y{osl}", d_y[oc * 128:(oc + 1) * 128, tsl], OST[osl],
                      track_write=False)

        final_waits = [(s, P.cnt[s]) for s in P.cnt
                       if s.startswith("dma_y") or s.startswith("dma_dbg")]
        P.barrier_wait("scalar", final_waits)

        sem_names = ["PE", "DV", "AC", "PL", "CC"] + sorted(
            s for s in P.cnt if s.startswith("dma_"))
        sems = {n: ctx.enter_context(nc.semaphore(n)) for n in sem_names}
        block = ctx.enter_context(nc.Block())

        @block.tensor
        def _(eng):
            P.replay("tensor", eng, sems)

        @block.vector
        def _(eng):
            P.replay("vector", eng, sems)

        @block.scalar
        def _(eng):
            P.replay("scalar", eng, sems)

        @block.gpsimd
        def _(eng):
            P.replay("gpsimd", eng, sems)

        @block.sync
        def _(eng):
            P.replay("sync", eng, sems)

    return nc


def _host_constants():
    QDI = np.broadcast_to(-(np.arange(256, dtype=np.float32) + 1.0), (128, 256)).copy()
    DDI = np.empty((128, 512), np.float32)
    m = np.arange(256, dtype=np.float32)
    for sc in range(2):
        for p in range(128):
            s = sc * 128 + p
            DDI[p, sc * 256:(sc + 1) * 256] = np.where(m >= s, -(m - s), DDBIG)
    KDI = np.empty((128, 2), np.float32)
    for j in range(2):
        KDI[:, j] = -(255.0 - (j * 128 + np.arange(128, dtype=np.float32)))
    BDIc = np.broadcast_to(np.array([0.0, -256.0, -512.0, -768.0], np.float32), (128, 4)).copy()
    identB = np.eye(128, dtype=BF)
    E2 = np.zeros((8, 128), BF)
    for e in range(8):
        E2[e, e * 16:(e + 1) * 16] = BF(SCALING)
    IDF8 = np.eye(8, dtype=np.float32)
    return QDI, DDI, KDI, BDIc, identB, E2, IDF8


def kernel(L0_attn_x, L0_attn_slope_rate, L0_attn_W_qkvu, L0_attn_W_out,
           L0_ffn_W_base, L0_ffn_b_base, L0_ffn_router_W, L0_ffn_A, L0_ffn_Bw):
    x = np.ascontiguousarray(np.asarray(L0_attn_x, dtype=np.float32)).reshape(8192, D)
    Wqkvu = np.asarray(L0_attn_W_qkvu, np.float32)
    Wout = np.asarray(L0_attn_W_out, np.float32)
    Wbase = np.asarray(L0_ffn_W_base, np.float32)
    bbase = np.asarray(L0_ffn_b_base, np.float32)
    rW = np.asarray(L0_ffn_router_W, np.float32)
    A = np.asarray(L0_ffn_A, np.float32)
    Bw = np.asarray(L0_ffn_Bw, np.float32)
    slope = np.asarray(L0_attn_slope_rate, np.float32).reshape(NH)

    qkvuS = np.ascontiguousarray(Wqkvu.reshape(64, 128, DC, 128).transpose(0, 3, 2, 1))
    woutS = np.ascontiguousarray(Wout.reshape(16, 128, DC, 128).transpose(0, 3, 2, 1))
    wbaseS = np.ascontiguousarray(Wbase.reshape(16, 128, DC, 128).transpose(0, 3, 2, 1))
    aT = np.ascontiguousarray(A.reshape(E * RK, DC, 128).transpose(2, 1, 0))
    rT = np.ascontiguousarray(rW.reshape(E, DC, 128).transpose(2, 1, 0))
    bwF = np.ascontiguousarray(Bw.transpose(0, 2, 1).reshape(E * RK, HID))
    bT = np.ascontiguousarray(bbase.reshape(16, 128).T)
    slopeB = np.broadcast_to(slope.reshape(1, NH), (128, NH)).copy()
    QDI, DDI, KDI, BDIc, identB, E2, IDF8 = _host_constants()

    shared = dict(qkvuS=qkvuS, woutS=woutS, wbaseS=wbaseS, aT=aT, rT=rT, bwF=bwF,
                  bT=bT, slopeB=slopeB, QDI=QDI, DDI=DDI, KDI=KDI, BDIc=BDIc,
                  identB=identB, E2=E2, IDF8=IDF8)
    in_maps = []
    for cc in range(8):
        m8 = np.zeros((128, 8), np.float32)
        if cc % 2 == 1:
            m8[:, cc - 1] = 1.0
        xc = np.ascontiguousarray(x[cc * T:(cc + 1) * T].T)
        in_maps.append(dict(x=xc, maskv8=m8, **shared))

    nc = build_nc()
    res = run_bass_kernel_spmd(nc, in_maps, core_ids=list(range(8)))
    out = np.empty((8192, D), np.float32)
    for cc in range(8):
        out[cc * T:(cc + 1) * T] = res.results[cc]["y"].T
    return out.reshape(4, 2048, D)


if __name__ == "__main__":
    import reference as _r
    inp = _r.setup_inputs()
    got = kernel(**{k: np.asarray(v) for k, v in inp.items()})
    want = np.asarray(_r.reference(**inp))
    num = float(((got - want) ** 2).mean())
    den = float((want ** 2).mean())
    print("res-var ratio:", num / den)


# version 19
# speedup vs baseline: 1.0842x; 1.0842x over previous
import numpy as np
import ml_dtypes
import concourse.bass as bass
import concourse.mybir as mybir
from concourse.bass_utils import run_bass_kernel_spmd

F32 = mybir.dt.float32
BF16 = mybir.dt.bfloat16
AF = mybir.ActivationFunctionType
ALU = mybir.AluOpType
AX = mybir.AxisListType

BF = ml_dtypes.bfloat16

D = 2048
HID = 2048
NH = 16
T = 1024
NT = 2
NBLK = 4
DC = 16
E = 8
RK = 16
SCALING = 2.0
LN_EPS = 1e-5
RMS_EPS = 1e-6
DDBIG = -1.0e5

ENGS = ("tensor", "vector", "scalar", "gpsimd", "sync")
ESEM = {"tensor": "PE", "vector": "DV", "scalar": "AC", "gpsimd": "PL", "sync": None}
PAGE = 8192


def _ap_ranges(ap):
    t = ap.tensor
    if t.name.startswith("pb"):
        return (t.name, 0, 2048)
    esz = mybir.dt.size(ap.dtype)
    rowbytes = None
    shp = list(t.shape)
    if len(shp) == 2 and shp[0] <= 128 and t.name.startswith("arena"):
        rowbytes = shp[1] * mybir.dt.size(t.dtype)
    lo = ap.offset * esz
    span = 0
    for step, cnt in ap.ap:
        sb = step * esz
        if rowbytes is not None and cnt <= 128 and sb == rowbytes:
            continue
        span += sb * (cnt - 1)
    return (t.name, lo, lo + span + esz)


class Prog:

    def __init__(self):
        self.ops = {e: [] for e in ENGS}
        self.cnt = {}
        self.K = {e: {} for e in ENGS}
        self.SNAP = {}
        self.pages = {}
        self.hist = {}
        self.hidx = 0
        self.dma_slots = set()

    def _entries_overlapping(self, name, lo, hi):
        out = set()
        for pg in range(lo // PAGE, hi // PAGE + 1):
            for i in self.pages.get((name, pg), ()):
                ent = self.hist[i]
                if ent[5] and ent[1] < hi and lo < ent[2]:
                    out.add(i)
        return out

    def _record(self, name, lo, hi, mark, is_write):
        i = self.hidx
        self.hidx += 1
        self.hist[i] = [name, lo, hi, mark, is_write, True]
        for pg in range(lo // PAGE, hi // PAGE + 1):
            self.pages.setdefault((name, pg), set()).add(i)
        return i

    def _auto_waits(self, eng, r, w):
        need = {}
        kill = []
        for ap in r:
            name, lo, hi = _ap_ranges(ap)
            for i in self._entries_overlapping(name, lo, hi):
                ent = self.hist[i]
                if ent[4]:
                    s, v = ent[3]
                    if need.get(s, 0) < v:
                        need[s] = v
        for ap in w:
            name, lo, hi = _ap_ranges(ap)
            for i in self._entries_overlapping(name, lo, hi):
                ent = self.hist[i]
                s, v = ent[3]
                if need.get(s, 0) < v:
                    need[s] = v
                if ent[1] >= lo and ent[2] <= hi:
                    kill.append(i)
        for i in kill:
            self.hist[i][5] = False
        own = ESEM[eng]
        out = []
        for s, v in need.items():
            if eng == "tensor" and s == "PE":
                continue
            if self.K[eng].get(s, 0) >= v:
                continue
            out.append((s, v))
        return out

    def _apply_waits(self, eng, waits):
        K = self.K[eng]
        for s, v in waits:
            if K.get(s, 0) >= v:
                continue
            snap = self.SNAP.get((s, v))
            if snap:
                for s2, v2 in snap.items():
                    if K.get(s2, 0) < v2:
                        K[s2] = v2
            if K.get(s, 0) < v:
                K[s] = v

    def _push(self, eng, fn, waits, incs):
        self.ops[eng].append({"waits": waits, "fn": fn, "incs": incs})

    def op(self, eng, fn, r=(), w=(), extra=(), mark_sem=None, mark_amt=1):
        ws = dict()
        for m in extra:
            if m is None:
                continue
            s, v = m
            if v > 0 and ws.get(s, 0) < v:
                ws[s] = v
        for s, v in self._auto_waits(eng, r, w):
            if ws.get(s, 0) < v:
                ws[s] = v
        waits = [(s, v) for s, v in ws.items() if not (eng == "tensor" and s == "PE")
                 and self.K[eng].get(s, 0) < v]
        self._apply_waits(eng, waits)
        if mark_sem is None:
            mark_sem = ESEM[eng]
        self.cnt[mark_sem] = self.cnt.get(mark_sem, 0) + (1 if mark_amt is None else mark_amt)
        mark = (mark_sem, self.cnt[mark_sem])
        self._push(eng, fn, waits, [(mark_sem, mark_amt)])
        snap = dict(self.K[eng])
        if ESEM[eng] is not None and mark_sem == ESEM[eng]:
            snap[mark_sem] = mark[1]
        else:
            snap[mark_sem] = mark[1]
        self.SNAP[mark] = snap
        for ap in r:
            name, lo, hi = _ap_ranges(ap)
            self._record(name, lo, hi, mark, False)
        for ap in w:
            name, lo, hi = _ap_ranges(ap)
            self._record(name, lo, hi, mark, True)
        return mark

    def gop(self, eng, items, extra=()):
        ws = dict()
        for m in extra:
            if m is None:
                continue
            s, v = m
            if v > 0 and ws.get(s, 0) < v:
                ws[s] = v
        for fn, r, w in items:
            for s, v in self._auto_waits(eng, r, w):
                if ws.get(s, 0) < v:
                    ws[s] = v
        waits = [(s, v) for s, v in ws.items() if not (eng == "tensor" and s == "PE")
                 and self.K[eng].get(s, 0) < v]
        self._apply_waits(eng, waits)
        sem = ESEM[eng]
        self.cnt[sem] = self.cnt.get(sem, 0) + 1
        mark = (sem, self.cnt[sem])
        for i, (fn, r, w) in enumerate(items):
            self._push(eng, fn, waits if i == 0 else [],
                       [(sem, 1)] if i == len(items) - 1 else [])
        snap = dict(self.K[eng])
        snap[sem] = mark[1]
        self.SNAP[mark] = snap
        for fn, r, w in items:
            for ap in r:
                name, lo, hi = _ap_ranges(ap)
                self._record(name, lo, hi, mark, False)
            for ap in w:
                name, lo, hi = _ap_ranges(ap)
                self._record(name, lo, hi, mark, True)
        return mark

    def dma(self, eng, slot, out, in_, extra=(), track_read=True, track_write=True):
        sem = "dma_" + slot
        self.dma_slots.add(sem)
        r = [in_] if track_read else []
        w = [out] if track_write else []
        return self.op(eng, lambda e, o=out, i=in_: e.dma_start(out=o, in_=i),
                       r=r, w=w, extra=extra, mark_sem=sem, mark_amt=16)

    def barrier_wait(self, eng, extra):
        self.op(eng, lambda e: e.nop(), extra=extra)

    def replay(self, eng_name, engine, sems):
        hi = {}
        for ent in self.ops[eng_name]:
            for s, v in ent["waits"]:
                if hi.get(s, 0) >= v:
                    continue
                engine.wait_ge(sems[s], v)
                hi[s] = v
            inst = ent["fn"](engine)
            for s, a in ent["incs"]:
                if a is None:
                    inst.then_inc(sems[s])
                else:
                    inst.then_inc(sems[s], a)


def build_nc(dbg=False):
    nc = bass.Bass("TRN2", target_bir_lowering=False, debug=False)

    d_x = nc.dram_tensor("x", [D, T], F32, kind="ExternalInput").ap()
    d_qkvu = nc.dram_tensor("qkvuS", [64, 128, DC, 128], F32, kind="ExternalInput").ap()
    d_wout = nc.dram_tensor("woutS", [16, 128, DC, 128], F32, kind="ExternalInput").ap()
    d_wbase = nc.dram_tensor("wbaseS", [16, 128, DC, 128], F32, kind="ExternalInput").ap()
    d_aT = nc.dram_tensor("aT", [128, DC, 128], F32, kind="ExternalInput").ap()
    d_rT = nc.dram_tensor("rT", [128, DC, 8], F32, kind="ExternalInput").ap()
    d_bw = nc.dram_tensor("bwF", [128, HID], F32, kind="ExternalInput").ap()
    d_bT = nc.dram_tensor("bT", [128, 16], F32, kind="ExternalInput").ap()
    d_slope = nc.dram_tensor("slopeB", [128, 16], F32, kind="ExternalInput").ap()
    d_mask8 = nc.dram_tensor("maskv8", [128, 8], F32, kind="ExternalInput").ap()
    d_qdi = nc.dram_tensor("QDI", [128, 256], F32, kind="ExternalInput").ap()
    d_ddi = nc.dram_tensor("DDI", [128, 512], F32, kind="ExternalInput").ap()
    d_kdi = nc.dram_tensor("KDI", [128, 2], F32, kind="ExternalInput").ap()
    d_bdi = nc.dram_tensor("BDIc", [128, 4], F32, kind="ExternalInput").ap()
    d_idb = nc.dram_tensor("identB", [128, 128], BF16, kind="ExternalInput").ap()
    d_e2 = nc.dram_tensor("E2", [8, 128], BF16, kind="ExternalInput").ap()
    d_idf8 = nc.dram_tensor("IDF8", [8, 8], F32, kind="ExternalInput").ap()
    d_y = nc.dram_tensor("y", [D, T], F32, kind="ExternalOutput").ap()

    d_part = nc.dram_tensor("part", [D, T], F32).ap()
    d_slocal = nc.dram_tensor("s_local", [128, NH * 128], F32).ap()
    d_sall = nc.dram_tensor("s_all", [8 * 128, NH * 128], F32, addr_space="Shared").ap()

    dbg_o = {}
    dbg_n = [0]
    if dbg:
        for nm, shp, dt in [("dbg_xn", [128, DC * T], BF16), ("dbg_q", [128, T], BF16),
                            ("dbg_k", [128, T], BF16), ("dbg_v", [128, T], BF16),
                            ("dbg_o", [128, NH * T], BF16), ("dbg_kv", [128, NH * 128], F32),
                            ("dbg_u", [128, NH * T], BF16), ("dbg_g", [128, NH * T], BF16),
                            ("dbg_hm", [128, T], BF16), ("dbg_kvin", [128, NH * 128], F32),
                            ("dbg_qd", [128, NH * T], BF16)]:
            dbg_o[nm] = nc.dram_tensor(nm, shp, dt, kind="ExternalOutput").ap()

    P = Prog()

    from contextlib import ExitStack
    with ExitStack() as ctx:
        AR = ctx.enter_context(nc.sbuf_tensor("arena", [128, 53196], F32))
        pb = [ctx.enter_context(nc.psum_tensor(f"pb{i}", [128, 512], F32)) for i in range(8)]
        pb0, pb1, pb2, pb3, pb4, pb5, pb6, pb7 = pb

        pos = [0]

        def alloc(n, align=64):
            p = (pos[0] + align - 1) // align * align
            pos[0] = p + n
            assert pos[0] <= 53196, f"arena overflow at {pos[0]}"
            return p

        def c(off, n):
            return AR[:, off:off + n]

        def cb(off, n):
            return AR[:, off:off + n].bitcast(BF16)

        o_xn = alloc(8192)
        XN = cb(o_xn, 8192).rearrange("p (c t) -> p c t", c=DC)
        o_o = alloc(8192)
        O_ = cb(o_o, 8192).rearrange("p (h t) -> p h t", h=NH)
        o_qd = alloc(8192)
        QDl = cb(o_qd, 8192).rearrange("p (h t) -> p h t", h=NH)
        Xf = c(o_o, 16384).rearrange("p (c t) -> p c t", c=DC)
        o_u = alloc(8192)
        U_ = cb(o_u, 8192).rearrange("p (h t) -> p h t", h=NH)
        o_qdec = alloc(512)
        o_ddec = alloc(1024)
        o_kdec = alloc(8)
        QDEC = None
        BDP = c(alloc(64), 64)
        SLP = c(alloc(16), 16)
        o_qdi = alloc(256)
        QDI = c(o_qdi, 256)
        o_ddi = alloc(512)
        DDI = c(o_ddi, 512)
        KDI = c(alloc(2), 2)
        BDIc = c(alloc(4), 4)
        HMSK = cb(alloc(512), 512)
        HSB = cb(o_qdec, 512)
        o_kv = alloc(2048)
        KV = c(o_kv, 2048).rearrange("p (h e) -> p h e", h=NH)
        KVINf = c(o_kv, 2048)
        IDN = cb(alloc(64), 64)
        o_e2 = alloc(64)
        E2C = AR[0:8, o_e2:o_e2 + 64].bitcast(BF16)
        o_idf = alloc(8)
        IDF8 = AR[0:8, o_idf:o_idf + 8]
        ALH = cb(alloc(1024), 1024).rearrange("p (c j) -> p c j", c=DC)
        RTR = cb(alloc(64), 64).rearrange("p (c j) -> p c j", c=DC)
        BWL = cb(alloc(1024), 1024)
        BBT = c(alloc(16), 16)
        MSK8 = c(alloc(8), 8)
        ONE1 = c(alloc(1), 1)
        ONEB = cb(alloc(1), 1)
        o_oner = alloc(128)
        ONER = AR[0:1, o_oner:o_oner + 128]
        RROW = c(alloc(512), 512)
        R2ROW = c(alloc(512), 512)
        NSTG = 2
        o_stg = alloc(NSTG * 1024)
        STG = [c(o_stg + i * 1024, 1024).rearrange("p (c j) -> p c j", c=8) for i in range(NSTG)]
        o_wq = alloc(1024)
        WQB = cb(o_wq, 1024).rearrange("p (c j) -> p c j", c=DC)
        o_wk = alloc(1024)
        WKB = cb(o_wk, 1024).rearrange("p (c j) -> p c j", c=DC)
        o_wv = alloc(1024)
        WVB = cb(o_wv, 1024).rearrange("p (c j) -> p c j", c=DC)
        o_qs = alloc(512)
        QS2 = [cb(o_qs, 512)]
        o_ks = alloc(512)
        KS2 = [cb(o_ks, 512)]
        o_vs = alloc(512)
        VS2 = [cb(o_vs, 512)]
        o_qs1 = alloc(512)
        QS2.append(cb(o_qs1, 512))
        o_ks1 = alloc(512)
        KS2.append(cb(o_ks1, 512))
        o_vs1 = alloc(512)
        VS2.append(cb(o_vs1, 512))
        XB4 = [cb(o_qs1 + i * 256, 256) for i in range(4)]
        o_ov = alloc(576)
        KVB = cb(o_ov + 512, 64)
        OSC4 = [c(o_qs + i * 256, 256) for i in range(4)]
        SQB4 = [cb(o_qs + 1024 + i * 128, 128) for i in range(4)]
        KVS4 = [cb(o_qs + 1536 + i * 64, 64) for i in range(4)]
        o_d = alloc(2560)
        MROW = c(o_d, 512)
        XRS = [c(o_d + i * 512, 512) for i in range(3)]
        OST = [c(o_d + 1536 + i * 512, 512) for i in range(2)]
        GBUF = c(o_wk, 2048)
        GTMP = c(o_qs, 2048)
        LTF = AR[0:8, o_stg:o_stg + 1024]
        LTT = c(o_d, 8)
        MXC = c(o_d + 16, 1)
        OHT = cb(o_d + 24, 4)
        o_mfb = o_d + 64
        MFB = AR[0:8, o_mfb:o_mfb + 512].bitcast(BF16)
        WOB = cb(o_wq, 1024).rearrange("p (c j) -> p c j", c=DC)
        WBB = cb(o_wk, 1024).rearrange("p (c j) -> p c j", c=DC)
        PST = [c(o_qdec + i * 512, 512) for i in range(2)]
        PRD = [c(o_qs1 + i * 512, 512) for i in range(3)]
        RROW4 = c(o_d + 1536, 1024)
        QDEC2 = [cb(o_qdec, 512).rearrange("p (l t) -> p l t", l=4),
                 cb(o_d + 576, 512).rearrange("p (l t) -> p l t", l=4)]
        DDEC2 = [cb(o_ddec, 1024).rearrange("p (l t) -> p l t", l=4),
                 cb(o_d + 1088, 1024).rearrange("p (l t) -> p l t", l=4)]
        KDEC2 = [c(o_kdec, 8), c(o_d + 2112, 8)]
        VTD = [[cb(b + i * 64, 64) for i in range(2)] for b in (o_ov, o_d)]
        KTD = [[cb(b + 128 + i * 64, 64) for i in range(2)] for b in (o_ov, o_d)]
        QKPD = [[cb(b + 256 + i * 128, 128) for i in range(2)] for b in (o_ov, o_d)]


        xq = ["sync", "scalar", "sync", "scalar"]
        for xh in range(2):
            for ch in range(2):
                P.dma(xq[xh * 2 + ch], f"x{xh}{ch}",
                      Xf[:, ch * 8:(ch + 1) * 8, xh * 512:(xh + 1) * 512],
                      d_x.rearrange("(c p) t -> p c t", p=128)[:, ch * 8:(ch + 1) * 8,
                                                              xh * 512:(xh + 1) * 512],
                      track_read=False)
        for key, dst, src in [("QDI", QDI, d_qdi), ("DDI", DDI, d_ddi), ("KDI", KDI, d_kdi),
                              ("BDI", BDIc, d_bdi), ("SLP", SLP, d_slope), ("IDN", IDN, d_idb),
                              ("E2", E2C, d_e2), ("IDF8", IDF8, d_idf8),
                              ("BBT", BBT, d_bT), ("MSK8", MSK8, d_mask8)]:
            P.dma("sync", "c_" + key, dst, src, track_read=False)
        P.dma("gpsimd", "bw", BWL, d_bw, track_read=False)
        P.dma("gpsimd", "aT", ALH, d_aT, track_read=False)
        P.dma("gpsimd", "rT", RTR, d_rT, track_read=False)
        P.op("vector", lambda e: e.memset(ONE1, 1.0), w=[ONE1])
        P.op("vector", lambda e: e.memset(ONEB[:, 0:1], 1.0), w=[ONEB[:, 0:1]])
        P.op("vector", lambda e: e.memset(ONER, 1.0), w=[ONER])

        def ln_half(nt):
            tsl = slice(nt * 512, (nt + 1) * 512)
            for dc in range(DC):
                sl = dc % 4
                if dc % 2 == 0:
                    P.op("scalar", lambda e, dc=dc, tsl=tsl, sl=sl: e.activation(
                        XB4[sl], Xf[:, dc, tsl], AF.Square),
                        r=[Xf[:, dc, tsl]], w=[XB4[sl]])
                else:
                    P.op("gpsimd", lambda e, dc=dc, tsl=tsl, sl=sl: e.tensor_tensor(
                        XB4[sl], Xf[:, dc, tsl], Xf[:, dc, tsl], ALU.mult),
                        r=[Xf[:, dc, tsl]], w=[XB4[sl]])
                P.op("tensor", lambda e, dc=dc, tsl=tsl: e.matmul(
                    pb4[0:1, 0:512], ONE1, Xf[:, dc, tsl],
                    start=(dc == 0), stop=(dc == DC - 1), skip_group_check=True),
                    r=[ONE1, Xf[:, dc, tsl]], w=[pb4[0:1, 0:512]])
                P.op("tensor", lambda e, dc=dc, sl=sl: e.matmul(
                    pb5[0:1, 0:512], ONEB[:, 0:1], XB4[sl],
                    start=(dc == 0), stop=(dc == DC - 1), skip_group_check=True),
                    r=[ONEB[:, 0:1], XB4[sl]], w=[pb5[0:1, 0:512]])
            P.op("vector", lambda e: e.tensor_scalar(MROW[0:1, 0:512], pb4[0:1, 0:512],
                                                     1.0 / D, None, ALU.mult),
                 r=[pb4[0:1, 0:512]], w=[MROW[0:1, 0:512]])
            P.op("vector", lambda e: e.tensor_scalar(R2ROW[0:1, 0:512], pb5[0:1, 0:512],
                                                     1.0 / D, LN_EPS, ALU.mult, ALU.add),
                 r=[pb5[0:1, 0:512]], w=[R2ROW[0:1, 0:512]])
            P.op("vector", lambda e: e.tensor_tensor(RROW[0:1, 0:512], MROW[0:1, 0:512],
                                                     MROW[0:1, 0:512], ALU.mult),
                 r=[MROW[0:1, 0:512]], w=[RROW[0:1, 0:512]])
            P.op("vector", lambda e: e.tensor_tensor(R2ROW[0:1, 0:512], R2ROW[0:1, 0:512],
                                                     RROW[0:1, 0:512], ALU.subtract),
                 r=[R2ROW[0:1, 0:512], RROW[0:1, 0:512]], w=[R2ROW[0:1, 0:512]])
            P.op("scalar", lambda e: e.activation(RROW[0:1, 0:512], R2ROW[0:1, 0:512], AF.Sqrt),
                 r=[R2ROW[0:1, 0:512]], w=[RROW[0:1, 0:512]])
            P.op("vector", lambda e: e.reciprocal(R2ROW[0:1, 0:512], RROW[0:1, 0:512]),
                 r=[RROW[0:1, 0:512]], w=[R2ROW[0:1, 0:512]])
            P.op("tensor", lambda e: e.matmul(pb3[:, 0:512], ONER, MROW[0:1, 0:512],
                                              start=True, stop=True, skip_group_check=True),
                 r=[ONER, MROW[0:1, 0:512]], w=[pb3[:, 0:512]])
            P.op("tensor", lambda e: e.matmul(pb2[:, 0:512], ONER, R2ROW[0:1, 0:512],
                                              start=True, stop=True, skip_group_check=True),
                 r=[ONER, R2ROW[0:1, 0:512]], w=[pb2[:, 0:512]])
            for dc in range(DC):
                P.op("vector", lambda e, dc=dc, tsl=tsl: e.tensor_tensor(
                    Xf[:, dc, tsl], Xf[:, dc, tsl], pb3[:, 0:512], ALU.subtract),
                    r=[Xf[:, dc, tsl], pb3[:, 0:512]], w=[Xf[:, dc, tsl]])
                P.op("vector", lambda e, dc=dc, tsl=tsl: e.tensor_tensor(
                    XN[:, dc, tsl], Xf[:, dc, tsl], pb2[:, 0:512], ALU.mult),
                    r=[Xf[:, dc, tsl], pb2[:, 0:512]], w=[XN[:, dc, tsl]])
        def load_stripe(dram_ap, dstW, eng=None):
            return P.dma("gpsimd", f"wst{dstW.offset}", dstW, dram_ap,
                         track_read=False)

        def mm_chain(bank, dstW, nt, half):
            return [
                (lambda e, dc=dc, nt=nt, bank=bank, dstW=dstW: e.matmul(
                    bank[:, 0:512], dstW[:, dc, :], XN[:, dc, nt * 512:(nt + 1) * 512],
                    start=(dc == 0), stop=(dc == DC - 1), skip_group_check=True),
                 [dstW[:, dc, :], XN[:, dc, nt * 512:(nt + 1) * 512]],
                 [bank[:, 0:512]])
                for dc in range(half * 8, half * 8 + 8)]

        def proj_group(dstW, out_sb, silu, nt):
            bank = pb[nt]
            P.gop("tensor", mm_chain(bank, dstW, nt, 0))
            P.gop("tensor", mm_chain(bank, dstW, nt, 1))
            if silu:
                P.op("scalar", lambda e, nt=nt, bank=bank, out_sb=out_sb: e.activation(
                    out_sb[:, nt * 512:(nt + 1) * 512], bank[:, 0:512], AF.Silu),
                    r=[bank[:, 0:512]], w=[out_sb[:, nt * 512:(nt + 1) * 512]])
            else:
                P.op("scalar", lambda e, nt=nt, bank=bank, out_sb=out_sb: e.copy(
                    out_sb[:, nt * 512:(nt + 1) * 512], bank[:, 0:512]),
                    r=[bank[:, 0:512]], w=[out_sb[:, nt * 512:(nt + 1) * 512]])

        def emit_masks(g):
            par = g % 2
            QDEC, DDEC, KDEC = QDEC2[par], DDEC2[par], KDEC2[par]
            for hl2 in range(4):
                h2 = g * 4 + hl2
                P.op("scalar", lambda e, hl2=hl2, h2=h2, QDEC=QDEC: e.activation(
                    QDEC[:, hl2, :], QDI, AF.Exp, scale=SLP[:, h2:h2 + 1]),
                    r=[QDI, SLP[:, h2:h2 + 1]], w=[QDEC[:, hl2, :]])
                P.op("scalar", lambda e, hl2=hl2, h2=h2, DDEC=DDEC: e.activation(
                    DDEC[:, hl2, :], DDI, AF.Exp, scale=SLP[:, h2:h2 + 1]),
                    r=[DDI, SLP[:, h2:h2 + 1]], w=[DDEC[:, hl2, :]])
                P.op("scalar", lambda e, hl2=hl2, h2=h2, KDEC=KDEC: e.activation(
                    KDEC[:, hl2 * 2:hl2 * 2 + 2], KDI, AF.Exp, scale=SLP[:, h2:h2 + 1]),
                    r=[KDI, SLP[:, h2:h2 + 1]], w=[KDEC[:, hl2 * 2:hl2 * 2 + 2]])
                P.op("scalar", lambda e, h2=h2: e.activation(
                    BDP[:, h2 * 4:(h2 + 1) * 4], BDIc, AF.Exp, scale=SLP[:, h2:h2 + 1]),
                    r=[BDIc, SLP[:, h2:h2 + 1]], w=[BDP[:, h2 * 4:(h2 + 1) * 4]])

        def b_produce(h, hl, hp, blk):
            base = blk * 256
            bp = blk % 2
            mpar = (h // 4) % 2
            QDEC, DDEC, KDEC = QDEC2[mpar], DDEC2[mpar], KDEC2[mpar]
            VT, KT, QKP = VTD[bp], KTD[bp], QKPD[bp]
            QS, KS, VS = QS2[hp], KS2[hp], VS2[hp]
            for sc in range(2):
                P.op("tensor", lambda e, sc=sc, base=base, KS=KS, QS=QS: e.matmul(
                    pb[4 + sc][:, 0:256], KS[:, base + sc * 128:base + sc * 128 + 128],
                    QS[:, base:base + 256], start=True, stop=True, skip_group_check=True),
                    r=[KS[:, base + sc * 128:base + sc * 128 + 128], QS[:, base:base + 256]],
                    w=[pb[4 + sc][:, 0:256]])
            for sc in range(2):
                P.op("tensor", lambda e, sc=sc, base=base, KS=KS: e.transpose(
                    pb2[:, sc * 64:sc * 64 + 64].bitcast(BF16),
                    KS[:, base + sc * 128:base + sc * 128 + 128], IDN),
                    r=[KS[:, base + sc * 128:base + sc * 128 + 128], IDN],
                    w=[pb2[:, sc * 64:sc * 64 + 64]])
                P.op("tensor", lambda e, sc=sc, base=base, VS=VS: e.transpose(
                    pb3[:, sc * 64:sc * 64 + 64].bitcast(BF16),
                    VS[:, base + sc * 128:base + sc * 128 + 128], IDN),
                    r=[VS[:, base + sc * 128:base + sc * 128 + 128], IDN],
                    w=[pb3[:, sc * 64:sc * 64 + 64]])
            for sc in range(2):
                P.op("vector", lambda e, sc=sc, hl=hl, QKP=QKP: e.tensor_tensor(
                    QKP[sc], pb[4 + sc][:, 0:256], DDEC[:, hl, sc * 256:(sc + 1) * 256],
                    ALU.mult),
                    r=[pb[4 + sc][:, 0:256], DDEC[:, hl, sc * 256:(sc + 1) * 256]],
                    w=[QKP[sc]])
            P.op("vector", lambda e, h=h, hl=hl, base=base, QS=QS: e.tensor_tensor(
                QDl[:, h, base:base + 256], QS[:, base:base + 256], QDEC[:, hl, :],
                ALU.mult),
                r=[QS[:, base:base + 256], QDEC[:, hl, :]], w=[QDl[:, h, base:base + 256]])
            for sc in range(2):
                P.op("scalar", lambda e, sc=sc, hl=hl, KT=KT: e.activation(
                    KT[sc], pb2[:, sc * 64:sc * 64 + 64].bitcast(BF16), AF.Copy,
                    scale=KDEC[:, hl * 2 + sc:hl * 2 + sc + 1]),
                    r=[pb2[:, sc * 64:sc * 64 + 64], KDEC[:, hl * 2 + sc:hl * 2 + sc + 1]],
                    w=[KT[sc]])
                P.op("scalar", lambda e, sc=sc, VT=VT: e.copy(
                    VT[sc], pb3[:, sc * 64:sc * 64 + 64].bitcast(BF16)),
                    r=[pb3[:, sc * 64:sc * 64 + 64]], w=[VT[sc]])

        def b_consume(h, hl, hp, blk):
            base = blk * 256
            bp = blk % 2
            VT, KT, QKP = VTD[bp], KTD[bp], QKPD[bp]
            P.op("tensor", lambda e, VT=VT, QKP=QKP: e.matmul(
                pb6[:, 0:256], VT[0], QKP[0],
                start=True, stop=False, skip_group_check=True),
                r=[VT[0], QKP[0]], w=[pb6[:, 0:256]])
            P.op("tensor", lambda e, blk=blk, VT=VT, QKP=QKP: e.matmul(
                pb6[:, 0:256], VT[1], QKP[1],
                start=False, stop=(blk == 0), skip_group_check=True),
                r=[VT[1], QKP[1]], w=[pb6[:, 0:256]])
            if blk > 0:
                P.op("tensor", lambda e, h=h, base=base: e.matmul(
                    pb6[:, 0:256], KVB, QDl[:, h, base:base + 256],
                    start=False, stop=True, skip_group_check=True),
                    r=[KVB, QDl[:, h, base:base + 256]], w=[pb6[:, 0:256]])
            P.op("tensor", lambda e, KT=KT, VT=VT: e.matmul(
                pb7[:, 0:128], KT[0], VT[0],
                start=True, stop=False, skip_group_check=True),
                r=[KT[0], VT[0]], w=[pb7[:, 0:128]])
            P.op("tensor", lambda e, KT=KT, VT=VT: e.matmul(
                pb7[:, 0:128], KT[1], VT[1],
                start=False, stop=True, skip_group_check=True),
                r=[KT[1], VT[1]], w=[pb7[:, 0:128]])
            if blk == 0:
                P.op("vector", lambda e, h=h: e.tensor_copy(KV[:, h, :], pb7[:, 0:128]),
                     r=[pb7[:, 0:128]], w=[KV[:, h, :]])
            else:
                P.op("vector", lambda e, h=h: e.tensor_scalar(
                    KV[:, h, :], KV[:, h, :], BDP[:, h * 4 + 1:h * 4 + 2], None, ALU.mult),
                    r=[KV[:, h, :], BDP[:, h * 4 + 1:h * 4 + 2]], w=[KV[:, h, :]])
                P.op("vector", lambda e, h=h: e.tensor_tensor(
                    KV[:, h, :], KV[:, h, :], pb7[:, 0:128], ALU.add),
                    r=[KV[:, h, :], pb7[:, 0:128]], w=[KV[:, h, :]])
            if blk < NBLK - 1:
                P.op("vector", lambda e, h=h: e.tensor_copy(KVB, KV[:, h, :]),
                     r=[KV[:, h, :]], w=[KVB])
            P.op("scalar", lambda e, h=h, base=base: e.copy(
                O_[:, h, base:base + 256], pb6[:, 0:256]),
                r=[pb6[:, 0:256]], w=[O_[:, h, base:base + 256]])

        emit_masks(0)
        load_stripe(d_qkvu[0], WQB, eng="scalar")
        load_stripe(d_qkvu[16], WKB, eng="scalar")
        load_stripe(d_qkvu[32], WVB, eng="scalar")
        ln_half(0)
        proj_group(WQB, QS2[0], True, 0)
        proj_group(WKB, KS2[0], True, 0)
        ln_half(1)
        proj_group(WVB, VS2[0], False, 0)
        proj_group(WQB, QS2[0], True, 1)
        proj_group(WKB, KS2[0], True, 1)
        proj_group(WVB, VS2[0], False, 1)

        for h in range(NH):
            hl = h % 4
            hp = h % 2
            if hl == 1 and h // 4 + 1 < 4:
                emit_masks(h // 4 + 1)
            nxt = []
            if h + 1 < NH:
                nxt = [(d_qkvu[h + 1], WQB, QS2[1 - hp], True),
                       (d_qkvu[16 + h + 1], WKB, KS2[1 - hp], True),
                       (d_qkvu[32 + h + 1], WVB, VS2[1 - hp], False)]
            b_produce(h, hl, hp, 0)
            b_produce(h, hl, hp, 1)
            b_consume(h, hl, hp, 0)
            if nxt:
                load_stripe(nxt[0][0], nxt[0][1], eng="scalar")
            b_consume(h, hl, hp, 1)
            if nxt:
                proj_group(nxt[0][1], nxt[0][2], nxt[0][3], 0)
            b_produce(h, hl, hp, 2)
            if nxt:
                load_stripe(nxt[1][0], nxt[1][1], eng="scalar")
                proj_group(nxt[0][1], nxt[0][2], nxt[0][3], 1)
            b_consume(h, hl, hp, 2)
            if nxt:
                proj_group(nxt[1][1], nxt[1][2], nxt[1][3], 0)
            b_produce(h, hl, hp, 3)
            if nxt:
                load_stripe(nxt[2][0], nxt[2][1])
                proj_group(nxt[1][1], nxt[1][2], nxt[1][3], 1)
            b_consume(h, hl, hp, 3)
            if nxt:
                proj_group(nxt[2][1], nxt[2][2], nxt[2][3], 0)
                proj_group(nxt[2][1], nxt[2][2], nxt[2][3], 1)

            if dbg and h == 1:
                dbg_n[0] += 1; P.dma("scalar", f"dbg{dbg_n[0]}", dbg_o["dbg_q"], QS2[1], track_write=False)
                dbg_n[0] += 1; P.dma("scalar", f"dbg{dbg_n[0]}", dbg_o["dbg_k"], KS2[1], track_write=False)
                dbg_n[0] += 1; P.dma("scalar", f"dbg{dbg_n[0]}", dbg_o["dbg_v"], VS2[1], track_write=False)

        P.dma("gpsimd", "sl", d_slocal, KVINf)
        m_cc = P.op("gpsimd", lambda e: e.collective_compute(
            "AllGather", ALU.bypass, replica_groups=[[0, 1, 2, 3, 4, 5, 6, 7]],
            ins=[d_slocal.opt()], outs=[d_sall.opt()]),
            r=[d_slocal], w=[d_sall], mark_sem="CC", mark_amt=None)

        for nt in range(NT):
            P.gop("tensor", [
                (lambda e, dc=dc, nt=nt: e.matmul(
                    pb5[:, 0:512], ALH[:, dc, :], XN[:, dc, nt * 512:(nt + 1) * 512],
                    start=(dc == 0), stop=(dc == DC - 1), skip_group_check=True),
                 [ALH[:, dc, :], XN[:, dc, nt * 512:(nt + 1) * 512]], [pb5[:, 0:512]])
                for dc in range(DC)])
            P.op("scalar", lambda e, nt=nt: e.copy(HSB[:, nt * 512:(nt + 1) * 512], pb5[:, 0:512]),
                 r=[pb5[:, 0:512]], w=[HSB[:, nt * 512:(nt + 1) * 512]])
        for nt in range(NT):
            P.gop("tensor", [
                (lambda e, dc=dc, nt=nt: e.matmul(
                    pb3[0:8, 0:512], RTR[:, dc, :], XN[:, dc, nt * 512:(nt + 1) * 512],
                    start=(dc == 0), stop=(dc == DC - 1), skip_group_check=True),
                 [RTR[:, dc, :], XN[:, dc, nt * 512:(nt + 1) * 512]], [pb3[0:8, 0:512]])
                for dc in range(DC)])
            P.op("vector", lambda e, nt=nt: e.tensor_copy(
                LTF[:, nt * 512:(nt + 1) * 512], pb3[0:8, 0:512]),
                r=[pb3[0:8, 0:512]], w=[LTF[:, nt * 512:(nt + 1) * 512]])
        for i in range(8):
            P.op("tensor", lambda e, i=i: e.transpose(
                pb2[:, 0:8], LTF[:, i * 128:(i + 1) * 128], IDF8),
                r=[LTF[:, i * 128:(i + 1) * 128], IDF8], w=[pb2[:, 0:8]])
            P.op("vector", lambda e: e.tensor_copy(LTT, pb2[:, 0:8]),
                 r=[pb2[:, 0:8]], w=[LTT])
            P.op("vector", lambda e: e.tensor_reduce(MXC, LTT, axis=AX.X, op=ALU.max),
                 r=[LTT], w=[MXC])
            P.op("vector", lambda e: e.tensor_scalar(OHT, LTT, MXC, None, ALU.is_equal),
                 r=[LTT, MXC], w=[OHT])
            P.op("tensor", lambda e: e.transpose(
                pb2[0:8, 128:192].bitcast(BF16), OHT, IDN),
                r=[OHT, IDN], w=[pb2[0:8, 128:192]])
            P.op("vector", lambda e, i=i: e.tensor_copy(
                MFB[:, i * 128:(i + 1) * 128], pb2[0:8, 128:192].bitcast(BF16)),
                r=[pb2[0:8, 128:192]], w=[MFB[:, i * 128:(i + 1) * 128]])
        for nt in range(NT):
            P.op("tensor", lambda e, nt=nt: e.matmul(
                pb6[:, 0:512], E2C, MFB[:, nt * 512:(nt + 1) * 512],
                start=True, stop=True, skip_group_check=True),
                r=[E2C, MFB[:, nt * 512:(nt + 1) * 512]], w=[pb6[:, 0:512]])
            P.op("vector", lambda e, nt=nt: e.tensor_tensor(
                HMSK[:, nt * 512:(nt + 1) * 512], HSB[:, nt * 512:(nt + 1) * 512],
                pb6[:, 0:512], ALU.mult),
                r=[HSB[:, nt * 512:(nt + 1) * 512], pb6[:, 0:512]],
                w=[HMSK[:, nt * 512:(nt + 1) * 512]])

        if dbg:
            dbg_n[0] += 1; P.dma("scalar", f"dbg{dbg_n[0]}", dbg_o["dbg_hm"], HMSK, track_write=False)


        WPAR = [WBB, WQB]
        for oc in range(16):
            WBc = WPAR[oc % 2]
            load_stripe(d_wbase[oc], WBc)
            for nt in range(NT):
                ti = oc * 2 + nt
                bank = pb[ti % 2]
                tsl = slice(nt * 512, (nt + 1) * 512)
                P.gop("tensor", [
                    (lambda e, dc=dc, nt=nt, bank=bank, WBc=WBc: e.matmul(
                        bank[:, 0:512], WBc[:, dc, :], XN[:, dc, nt * 512:(nt + 1) * 512],
                        start=(dc == 0), stop=False, skip_group_check=True),
                     [WBc[:, dc, :], XN[:, dc, nt * 512:(nt + 1) * 512]], [bank[:, 0:512]])
                    for dc in range(8)])
                items = [
                    (lambda e, dc=dc, nt=nt, bank=bank, WBc=WBc: e.matmul(
                        bank[:, 0:512], WBc[:, dc, :], XN[:, dc, nt * 512:(nt + 1) * 512],
                        start=False, stop=False, skip_group_check=True),
                     [WBc[:, dc, :], XN[:, dc, nt * 512:(nt + 1) * 512]], [bank[:, 0:512]])
                    for dc in range(8, DC)]
                items.append(
                    (lambda e, oc=oc, nt=nt, bank=bank: e.matmul(
                        bank[:, 0:512], BWL[:, oc * 128:(oc + 1) * 128],
                        HMSK[:, nt * 512:(nt + 1) * 512],
                        start=False, stop=True, skip_group_check=True),
                     [BWL[:, oc * 128:(oc + 1) * 128], HMSK[:, nt * 512:(nt + 1) * 512]],
                     [bank[:, 0:512]]))
                P.gop("tensor", items)
                psl = ti % 2
                P.op("scalar", lambda e, oc=oc, psl=psl, bank=bank: e.activation(
                    PST[psl], bank[:, 0:512], AF.Identity, bias=BBT[:, oc:oc + 1]),
                    r=[bank[:, 0:512], BBT[:, oc:oc + 1]], w=[PST[psl]])
                P.dma("scalar", f"pp{psl}", d_part[oc * 128:(oc + 1) * 128, tsl], PST[psl])

        GSC = c(o_d, 2048)
        GBH = [c(o_qs + i * 1024, 1024) for i in range(2)]
        for cidx in range(8):
            for hf in range(2):
                gbb = GBH[hf]
                csl = slice(hf * 1024, (hf + 1) * 1024)
                P.dma("sync", f"gb{hf}", gbb, d_sall[cidx * 128:(cidx + 1) * 128, csl])
                if cidx == 0:
                    P.op("vector", lambda e, gbb=gbb, csl=csl: e.tensor_scalar(
                        KVINf[:, csl], gbb, MSK8[:, 0:1], None, ALU.mult),
                        r=[gbb, MSK8[:, 0:1]], w=[KVINf[:, csl]])
                else:
                    P.op("vector", lambda e, cidx=cidx, gbb=gbb, csl=csl: e.tensor_scalar(
                        GSC[:, csl], gbb, MSK8[:, cidx:cidx + 1], None, ALU.mult),
                        r=[gbb, MSK8[:, cidx:cidx + 1]], w=[GSC[:, csl]])
                    P.op("vector", lambda e, csl=csl: e.tensor_tensor(
                        KVINf[:, csl], KVINf[:, csl], GSC[:, csl], ALU.add),
                        r=[KVINf[:, csl], GSC[:, csl]], w=[KVINf[:, csl]])

        if dbg:
            dbg_n[0] += 1; P.dma("scalar", f"dbg{dbg_n[0]}", dbg_o["dbg_kvin"], KVINf, track_write=False)

        RSB = c(o_qs + 1792, 256)

        def emit_umult(blk, heads):
            base = blk * 256
            for h in heads:
                eng = "vector" if h % 4 < 2 else "gpsimd"
                P.op(eng, lambda e, h=h, base=base: e.tensor_tensor(
                    O_[:, h, base:base + 256], O_[:, h, base:base + 256],
                    U_[:, h, base:base + 256], ALU.mult),
                    r=[O_[:, h, base:base + 256], U_[:, h, base:base + 256]],
                    w=[O_[:, h, base:base + 256]])

        def emit_rmult(blk):
            base = blk * 256
            P.op("tensor", lambda e, base=base: e.matmul(
                pb3[:, 0:256], ONER, RROW4[0:1, base:base + 256],
                start=True, stop=True, skip_group_check=True),
                r=[ONER, RROW4[0:1, base:base + 256]], w=[pb3[:, 0:256]])
            P.op("scalar", lambda e: e.copy(RSB, pb3[:, 0:256]),
                 r=[pb3[:, 0:256]], w=[RSB])
            for h in range(NH):
                eng = "vector" if h % 4 < 2 else "gpsimd"
                P.op(eng, lambda e, h=h, base=base: e.tensor_tensor(
                    O_[:, h, base:base + 256], O_[:, h, base:base + 256], RSB,
                    ALU.mult),
                    r=[O_[:, h, base:base + 256], RSB], w=[O_[:, h, base:base + 256]])

        ugen = []

        def u_head(h):
            WUc = [WQB, WBB][h % 2]
            load_stripe(d_qkvu[48 + h], WUc, eng="scalar")
            for nt in range(NT):
                bank = pb[nt]
                P.gop("tensor", mm_chain(bank, WUc, nt, 0))
                P.gop("tensor", mm_chain(bank, WUc, nt, 1))
                P.op("scalar", lambda e, nt=nt, h=h, bank=bank: e.copy(
                    U_[:, h, nt * 512:(nt + 1) * 512], bank[:, 0:512]),
                    r=[bank[:, 0:512]], w=[U_[:, h, nt * 512:(nt + 1) * 512]])

        cbank = [pb2, pb5, pb6, pb7]
        uh = [0]
        pend_um = []
        rm_done = [0]

        def flush_sched():
            keep = []
            for b, heads in pend_um:
                if max(heads) < uh[0]:
                    emit_umult(b, heads)
                else:
                    keep.append((b, heads))
            pend_um[:] = keep

        while uh[0] < 3:
            u_head(uh[0]); uh[0] += 1
        for blk in range(NBLK):
            base = blk * 256
            for h0 in range(0, NH, 4):
                if blk > 0 and h0 == 4 and rm_done[0] < blk:
                    emit_rmult(rm_done[0]); rm_done[0] += 1
                if uh[0] < NH:
                    u_head(uh[0]); uh[0] += 1
                for i in range(4):
                    h = h0 + i
                    P.op("vector", lambda e, h=h, blk=blk, i=i: e.tensor_scalar(
                        KVS4[i], KVINf[:, h * 128:(h + 1) * 128],
                        BDP[:, h * 4 + blk:h * 4 + blk + 1], None, ALU.mult),
                        r=[KVINf[:, h * 128:(h + 1) * 128], BDP[:, h * 4 + blk:h * 4 + blk + 1]],
                        w=[KVS4[i]])
                for i in range(4):
                    h = h0 + i
                    P.op("tensor", lambda e, h=h, base=base, i=i: e.matmul(
                        cbank[i][:, 0:256], KVS4[i], QDl[:, h, base:base + 256],
                        start=True, stop=True, skip_group_check=True),
                        r=[KVS4[i], QDl[:, h, base:base + 256]], w=[cbank[i][:, 0:256]])
                for i in range(4):
                    h = h0 + i
                    P.op("vector", lambda e, h=h, base=base, i=i: e.tensor_tensor(
                        OSC4[i], O_[:, h, base:base + 256], cbank[i][:, 0:256], ALU.add),
                        r=[O_[:, h, base:base + 256], cbank[i][:, 0:256]], w=[OSC4[i]])
                for i in range(4):
                    P.op("scalar", lambda e, i=i: e.activation(SQB4[i], OSC4[i], AF.Square),
                         r=[OSC4[i]], w=[SQB4[i]])
                for i in range(4):
                    h = h0 + i
                    P.op("tensor", lambda e, h=h, i=i: e.matmul(
                        pb4[0:1, 0:256], ONEB[:, 0:1], SQB4[i],
                        start=(h == 0), stop=(h == NH - 1), skip_group_check=True),
                        r=[ONEB[:, 0:1], SQB4[i]], w=[pb4[0:1, 0:256]])
                for i in range(4):
                    h = h0 + i
                    P.op("vector", lambda e, h=h, base=base, i=i: e.tensor_copy(
                        O_[:, h, base:base + 256], OSC4[i]),
                        r=[OSC4[i]], w=[O_[:, h, base:base + 256]])
                pend_um.append((blk, list(range(h0, h0 + 4))))
                flush_sched()
            P.op("vector", lambda e: e.tensor_scalar(R2ROW[0:1, 0:256], pb4[0:1, 0:256],
                                                     1.0 / HID, RMS_EPS, ALU.mult, ALU.add),
                 r=[pb4[0:1, 0:256]], w=[R2ROW[0:1, 0:256]])
            P.op("scalar", lambda e: e.activation(RROW[0:1, 0:256], R2ROW[0:1, 0:256], AF.Sqrt),
                 r=[R2ROW[0:1, 0:256]], w=[RROW[0:1, 0:256]])
            P.op("vector", lambda e, base=base: e.reciprocal(
                RROW4[0:1, base:base + 256], RROW[0:1, 0:256]),
                r=[RROW[0:1, 0:256]], w=[RROW4[0:1, base:base + 256]])
        while uh[0] < NH:
            u_head(uh[0]); uh[0] += 1
            flush_sched()
        flush_sched()
        assert not pend_um
        while rm_done[0] < NBLK:
            emit_rmult(rm_done[0]); rm_done[0] += 1
        if dbg:
            dbg_n[0] += 1; P.dma("scalar", f"dbg{dbg_n[0]}", dbg_o["dbg_u"], U_.rearrange("p h t -> p (h t)"), track_write=False)

        if dbg:
            dbg_n[0] += 1; P.dma("scalar", f"dbg{dbg_n[0]}", dbg_o["dbg_g"], O_.rearrange("p h t -> p (h t)"), track_write=False)

        for oc in range(16):
            WOc = [WOB, WBB][oc % 2]
            load_stripe(d_wout[oc], WOc, eng="scalar")
            for nt in range(NT):
                ti = oc * 2 + nt
                bank = pb[ti % 2]
                tsl = slice(nt * 512, (nt + 1) * 512)
                xsl = ti % 3
                P.dma("sync", f"xr{xsl}", XRS[xsl], d_x[oc * 128:(oc + 1) * 128, tsl],
                      track_read=False)
                prs = ti % 3
                P.dma("sync", f"pr{prs}", PRD[prs], d_part[oc * 128:(oc + 1) * 128, tsl])
                for half in range(2):
                    P.gop("tensor", [
                        (lambda e, hc=hc, nt=nt, bank=bank, WOc=WOc: e.matmul(
                            bank[:, 0:512], WOc[:, hc, :], O_[:, hc, nt * 512:(nt + 1) * 512],
                            start=(hc == 0), stop=(hc == 15), skip_group_check=True),
                         [WOc[:, hc, :], O_[:, hc, nt * 512:(nt + 1) * 512]], [bank[:, 0:512]])
                        for hc in range(half * 8, half * 8 + 8)])
                osl = ti % 2
                P.op("scalar", lambda e, osl=osl, bank=bank: e.copy(OST[osl], bank[:, 0:512]),
                     r=[bank[:, 0:512]], w=[OST[osl]])
                P.op("vector", lambda e, osl=osl, prs=prs: e.tensor_tensor(
                    OST[osl], OST[osl], PRD[prs], ALU.add),
                    r=[OST[osl], PRD[prs]], w=[OST[osl]])
                P.op("vector", lambda e, osl=osl, xsl=xsl: e.tensor_tensor(
                    OST[osl], OST[osl], XRS[xsl], ALU.add),
                    r=[OST[osl], XRS[xsl]], w=[OST[osl]])
                P.dma("scalar", f"y{osl}", d_y[oc * 128:(oc + 1) * 128, tsl], OST[osl],
                      track_write=False)

        final_waits = [(s, P.cnt[s]) for s in P.cnt
                       if s.startswith("dma_y") or s.startswith("dma_dbg")]
        P.barrier_wait("scalar", final_waits)

        sem_names = ["PE", "DV", "AC", "PL", "CC"] + sorted(
            s for s in P.cnt if s.startswith("dma_"))
        sems = {n: ctx.enter_context(nc.semaphore(n)) for n in sem_names}
        block = ctx.enter_context(nc.Block())

        @block.tensor
        def _(eng):
            P.replay("tensor", eng, sems)

        @block.vector
        def _(eng):
            P.replay("vector", eng, sems)

        @block.scalar
        def _(eng):
            P.replay("scalar", eng, sems)

        @block.gpsimd
        def _(eng):
            P.replay("gpsimd", eng, sems)

        @block.sync
        def _(eng):
            P.replay("sync", eng, sems)

    return nc


def _host_constants():
    QDI = np.broadcast_to(-(np.arange(256, dtype=np.float32) + 1.0), (128, 256)).copy()
    DDI = np.empty((128, 512), np.float32)
    m = np.arange(256, dtype=np.float32)
    for sc in range(2):
        for p in range(128):
            s = sc * 128 + p
            DDI[p, sc * 256:(sc + 1) * 256] = np.where(m >= s, -(m - s), DDBIG)
    KDI = np.empty((128, 2), np.float32)
    for j in range(2):
        KDI[:, j] = -(255.0 - (j * 128 + np.arange(128, dtype=np.float32)))
    BDIc = np.broadcast_to(np.array([0.0, -256.0, -512.0, -768.0], np.float32), (128, 4)).copy()
    identB = np.eye(128, dtype=BF)
    E2 = np.zeros((8, 128), BF)
    for e in range(8):
        E2[e, e * 16:(e + 1) * 16] = BF(SCALING)
    IDF8 = np.eye(8, dtype=np.float32)
    return QDI, DDI, KDI, BDIc, identB, E2, IDF8


def kernel(L0_attn_x, L0_attn_slope_rate, L0_attn_W_qkvu, L0_attn_W_out,
           L0_ffn_W_base, L0_ffn_b_base, L0_ffn_router_W, L0_ffn_A, L0_ffn_Bw):
    x = np.ascontiguousarray(np.asarray(L0_attn_x, dtype=np.float32)).reshape(8192, D)
    Wqkvu = np.asarray(L0_attn_W_qkvu, np.float32)
    Wout = np.asarray(L0_attn_W_out, np.float32)
    Wbase = np.asarray(L0_ffn_W_base, np.float32)
    bbase = np.asarray(L0_ffn_b_base, np.float32)
    rW = np.asarray(L0_ffn_router_W, np.float32)
    A = np.asarray(L0_ffn_A, np.float32)
    Bw = np.asarray(L0_ffn_Bw, np.float32)
    slope = np.asarray(L0_attn_slope_rate, np.float32).reshape(NH)

    qkvuS = np.ascontiguousarray(Wqkvu.reshape(64, 128, DC, 128).transpose(0, 3, 2, 1))
    woutS = np.ascontiguousarray(Wout.reshape(16, 128, DC, 128).transpose(0, 3, 2, 1))
    wbaseS = np.ascontiguousarray(Wbase.reshape(16, 128, DC, 128).transpose(0, 3, 2, 1))
    aT = np.ascontiguousarray(A.reshape(E * RK, DC, 128).transpose(2, 1, 0))
    rT = np.ascontiguousarray(rW.reshape(E, DC, 128).transpose(2, 1, 0))
    bwF = np.ascontiguousarray(Bw.transpose(0, 2, 1).reshape(E * RK, HID))
    bT = np.ascontiguousarray(bbase.reshape(16, 128).T)
    slopeB = np.broadcast_to(slope.reshape(1, NH), (128, NH)).copy()
    QDI, DDI, KDI, BDIc, identB, E2, IDF8 = _host_constants()

    shared = dict(qkvuS=qkvuS, woutS=woutS, wbaseS=wbaseS, aT=aT, rT=rT, bwF=bwF,
                  bT=bT, slopeB=slopeB, QDI=QDI, DDI=DDI, KDI=KDI, BDIc=BDIc,
                  identB=identB, E2=E2, IDF8=IDF8)
    in_maps = []
    for cc in range(8):
        m8 = np.zeros((128, 8), np.float32)
        if cc % 2 == 1:
            m8[:, cc - 1] = 1.0
        xc = np.ascontiguousarray(x[cc * T:(cc + 1) * T].T)
        in_maps.append(dict(x=xc, maskv8=m8, **shared))

    nc = build_nc()
    res = run_bass_kernel_spmd(nc, in_maps, core_ids=list(range(8)))
    out = np.empty((8192, D), np.float32)
    for cc in range(8):
        out[cc * T:(cc + 1) * T] = res.results[cc]["y"].T
    return out.reshape(4, 2048, D)


if __name__ == "__main__":
    import reference as _r
    inp = _r.setup_inputs()
    got = kernel(**{k: np.asarray(v) for k, v in inp.items()})
    want = np.asarray(_r.reference(**inp))
    num = float(((got - want) ** 2).mean())
    den = float((want ** 2).mean())
    print("res-var ratio:", num / den)


# version 22
# speedup vs baseline: 1.2946x; 1.1941x over previous
import numpy as np
import ml_dtypes
import concourse.bass as bass
import concourse.mybir as mybir
from concourse.bass_utils import run_bass_kernel_spmd

F32 = mybir.dt.float32
BF16 = mybir.dt.bfloat16
AF = mybir.ActivationFunctionType
ALU = mybir.AluOpType
AX = mybir.AxisListType

BF = ml_dtypes.bfloat16

D = 2048
HID = 2048
NH = 16
T = 1024
NT = 2
NBLK = 4
DC = 16
E = 8
RK = 16
SCALING = 2.0
LN_EPS = 1e-5
RMS_EPS = 1e-6
DDBIG = -1.0e5

ENGS = ("tensor", "vector", "scalar", "gpsimd", "sync")
ESEM = {"tensor": "PE", "vector": "DV", "scalar": "AC", "gpsimd": "PL", "sync": None}
PAGE = 8192


def _ap_ranges(ap):
    t = ap.tensor
    if t.name.startswith("pb"):
        return (t.name, 0, 2048)
    esz = mybir.dt.size(ap.dtype)
    rowbytes = None
    shp = list(t.shape)
    if len(shp) == 2 and shp[0] <= 128 and t.name.startswith("arena"):
        rowbytes = shp[1] * mybir.dt.size(t.dtype)
    lo = ap.offset * esz
    span = 0
    for step, cnt in ap.ap:
        sb = step * esz
        if rowbytes is not None and cnt <= 128 and sb == rowbytes:
            continue
        span += sb * (cnt - 1)
    return (t.name, lo, lo + span + esz)


class Prog:

    def __init__(self):
        self.ops = {e: [] for e in ENGS}
        self.cnt = {}
        self.K = {e: {} for e in ENGS}
        self.SNAP = {}
        self.pages = {}
        self.hist = {}
        self.hidx = 0
        self.dma_slots = set()

    def _entries_overlapping(self, name, lo, hi):
        out = set()
        for pg in range(lo // PAGE, hi // PAGE + 1):
            for i in self.pages.get((name, pg), ()):
                ent = self.hist[i]
                if ent[5] and ent[1] < hi and lo < ent[2]:
                    out.add(i)
        return out

    def _record(self, name, lo, hi, mark, is_write):
        i = self.hidx
        self.hidx += 1
        self.hist[i] = [name, lo, hi, mark, is_write, True]
        for pg in range(lo // PAGE, hi // PAGE + 1):
            self.pages.setdefault((name, pg), set()).add(i)
        return i

    def _auto_waits(self, eng, r, w):
        need = {}
        kill = []
        for ap in r:
            name, lo, hi = _ap_ranges(ap)
            for i in self._entries_overlapping(name, lo, hi):
                ent = self.hist[i]
                if ent[4]:
                    s, v = ent[3]
                    if need.get(s, 0) < v:
                        need[s] = v
        for ap in w:
            name, lo, hi = _ap_ranges(ap)
            for i in self._entries_overlapping(name, lo, hi):
                ent = self.hist[i]
                s, v = ent[3]
                if need.get(s, 0) < v:
                    need[s] = v
                if ent[1] >= lo and ent[2] <= hi:
                    kill.append(i)
        for i in kill:
            self.hist[i][5] = False
        own = ESEM[eng]
        out = []
        for s, v in need.items():
            if eng == "tensor" and s == "PE":
                continue
            if self.K[eng].get(s, 0) >= v:
                continue
            out.append((s, v))
        return out

    def _apply_waits(self, eng, waits):
        K = self.K[eng]
        for s, v in waits:
            if K.get(s, 0) >= v:
                continue
            snap = self.SNAP.get((s, v))
            if snap:
                for s2, v2 in snap.items():
                    if K.get(s2, 0) < v2:
                        K[s2] = v2
            if K.get(s, 0) < v:
                K[s] = v

    def _push(self, eng, fn, waits, incs):
        self.ops[eng].append({"waits": waits, "fn": fn, "incs": incs})

    def op(self, eng, fn, r=(), w=(), extra=(), mark_sem=None, mark_amt=1):
        ws = dict()
        for m in extra:
            if m is None:
                continue
            s, v = m
            if v > 0 and ws.get(s, 0) < v:
                ws[s] = v
        for s, v in self._auto_waits(eng, r, w):
            if ws.get(s, 0) < v:
                ws[s] = v
        waits = [(s, v) for s, v in ws.items() if not (eng == "tensor" and s == "PE")
                 and self.K[eng].get(s, 0) < v]
        self._apply_waits(eng, waits)
        if mark_sem is None:
            mark_sem = ESEM[eng]
        self.cnt[mark_sem] = self.cnt.get(mark_sem, 0) + (1 if mark_amt is None else mark_amt)
        mark = (mark_sem, self.cnt[mark_sem])
        self._push(eng, fn, waits, [(mark_sem, mark_amt)])
        snap = dict(self.K[eng])
        if ESEM[eng] is not None and mark_sem == ESEM[eng]:
            snap[mark_sem] = mark[1]
        else:
            snap[mark_sem] = mark[1]
        self.SNAP[mark] = snap
        for ap in r:
            name, lo, hi = _ap_ranges(ap)
            self._record(name, lo, hi, mark, False)
        for ap in w:
            name, lo, hi = _ap_ranges(ap)
            self._record(name, lo, hi, mark, True)
        return mark

    def gop(self, eng, items, extra=()):
        ws = dict()
        for m in extra:
            if m is None:
                continue
            s, v = m
            if v > 0 and ws.get(s, 0) < v:
                ws[s] = v
        for fn, r, w in items:
            for s, v in self._auto_waits(eng, r, w):
                if ws.get(s, 0) < v:
                    ws[s] = v
        waits = [(s, v) for s, v in ws.items() if not (eng == "tensor" and s == "PE")
                 and self.K[eng].get(s, 0) < v]
        self._apply_waits(eng, waits)
        sem = ESEM[eng]
        self.cnt[sem] = self.cnt.get(sem, 0) + 1
        mark = (sem, self.cnt[sem])
        for i, (fn, r, w) in enumerate(items):
            self._push(eng, fn, waits if i == 0 else [],
                       [(sem, 1)] if i == len(items) - 1 else [])
        snap = dict(self.K[eng])
        snap[sem] = mark[1]
        self.SNAP[mark] = snap
        for fn, r, w in items:
            for ap in r:
                name, lo, hi = _ap_ranges(ap)
                self._record(name, lo, hi, mark, False)
            for ap in w:
                name, lo, hi = _ap_ranges(ap)
                self._record(name, lo, hi, mark, True)
        return mark

    def dma(self, eng, slot, out, in_, extra=(), track_read=True, track_write=True):
        sem = "dma_" + slot
        self.dma_slots.add(sem)
        r = [in_] if track_read else []
        w = [out] if track_write else []
        return self.op(eng, lambda e, o=out, i=in_: e.dma_start(out=o, in_=i),
                       r=r, w=w, extra=extra, mark_sem=sem, mark_amt=16)

    def barrier_wait(self, eng, extra):
        self.op(eng, lambda e: e.nop(), extra=extra)

    def replay(self, eng_name, engine, sems):
        hi = {}
        for ent in self.ops[eng_name]:
            for s, v in ent["waits"]:
                if hi.get(s, 0) >= v:
                    continue
                engine.wait_ge(sems[s], v)
                hi[s] = v
            inst = ent["fn"](engine)
            for s, a in ent["incs"]:
                if a is None:
                    inst.then_inc(sems[s])
                else:
                    inst.then_inc(sems[s], a)


def build_nc(dbg=False):
    nc = bass.Bass("TRN2", target_bir_lowering=False, debug=False)

    d_x = nc.dram_tensor("x", [D, T], F32, kind="ExternalInput").ap()
    d_qkvu = nc.dram_tensor("qkvuS", [64, 128, DC, 128], F32, kind="ExternalInput").ap()
    d_wout = nc.dram_tensor("woutS", [16, 128, DC, 128], F32, kind="ExternalInput").ap()
    d_wbase = nc.dram_tensor("wbaseS", [16, 128, DC, 128], F32, kind="ExternalInput").ap()
    d_aT = nc.dram_tensor("aT", [128, DC, 128], F32, kind="ExternalInput").ap()
    d_rT = nc.dram_tensor("rT", [128, DC, 8], F32, kind="ExternalInput").ap()
    d_bw = nc.dram_tensor("bwF", [128, HID], F32, kind="ExternalInput").ap()
    d_bT = nc.dram_tensor("bT", [128, 16], F32, kind="ExternalInput").ap()
    d_slope = nc.dram_tensor("slopeB", [128, 16], F32, kind="ExternalInput").ap()
    d_mask8 = nc.dram_tensor("maskv8", [128, 8], F32, kind="ExternalInput").ap()
    d_qdi = nc.dram_tensor("QDI", [128, 256], F32, kind="ExternalInput").ap()
    d_ddi = nc.dram_tensor("DDI", [128, 512], F32, kind="ExternalInput").ap()
    d_kdi = nc.dram_tensor("KDI", [128, 2], F32, kind="ExternalInput").ap()
    d_bdi = nc.dram_tensor("BDIc", [128, 4], F32, kind="ExternalInput").ap()
    d_idb = nc.dram_tensor("identB", [128, 128], BF16, kind="ExternalInput").ap()
    d_e2 = nc.dram_tensor("E2", [8, 128], BF16, kind="ExternalInput").ap()
    d_idf8 = nc.dram_tensor("IDF8", [8, 8], F32, kind="ExternalInput").ap()
    d_y = nc.dram_tensor("y", [D, T], F32, kind="ExternalOutput").ap()

    d_slocal = nc.dram_tensor("s_local", [128, NH * 128], F32).ap()
    d_sall = nc.dram_tensor("s_all", [2 * 128, NH * 128], F32).ap()

    dbg_o = {}
    dbg_n = [0]
    if dbg:
        for nm, shp, dt in [("dbg_xn", [128, DC * T], BF16), ("dbg_q", [128, T], BF16),
                            ("dbg_k", [128, T], BF16), ("dbg_v", [128, T], BF16),
                            ("dbg_o", [128, NH * T], BF16), ("dbg_kv", [128, NH * 128], F32),
                            ("dbg_u", [128, NH * T], BF16), ("dbg_g", [128, NH * T], BF16),
                            ("dbg_hm", [128, T], BF16), ("dbg_kvin", [128, NH * 128], F32),
                            ("dbg_qd", [128, NH * T], BF16)]:
            dbg_o[nm] = nc.dram_tensor(nm, shp, dt, kind="ExternalOutput").ap()

    P = Prog()

    from contextlib import ExitStack
    with ExitStack() as ctx:
        AR = ctx.enter_context(nc.sbuf_tensor("arena", [128, 53196], F32))
        pb = [ctx.enter_context(nc.psum_tensor(f"pb{i}", [128, 512], F32)) for i in range(8)]
        pb0, pb1, pb2, pb3, pb4, pb5, pb6, pb7 = pb

        pos = [0]

        def alloc(n, align=64):
            p = (pos[0] + align - 1) // align * align
            pos[0] = p + n
            assert pos[0] <= 53196, f"arena overflow at {pos[0]}"
            return p

        def c(off, n):
            return AR[:, off:off + n]

        def cb(off, n):
            return AR[:, off:off + n].bitcast(BF16)

        o_xn = alloc(8192)
        XN = cb(o_xn, 8192).rearrange("p (c t) -> p c t", c=DC)
        o_o = alloc(8192)
        O_ = cb(o_o, 8192).rearrange("p (h t) -> p h t", h=NH)
        o_qd = alloc(8192)
        QDl = cb(o_qd, 8192).rearrange("p (h t) -> p h t", h=NH)
        Xf = c(o_o, 16384).rearrange("p (c t) -> p c t", c=DC)
        o_u = alloc(8192)
        U_ = cb(o_u, 8192).rearrange("p (h t) -> p h t", h=NH)
        o_qdec = alloc(512)
        o_ddec = alloc(1024)
        o_kdec = alloc(8)
        QDEC = None
        BDP = c(alloc(64), 64)
        SLP = c(alloc(16), 16)
        o_qdi = alloc(256)
        QDI = c(o_qdi, 256)
        o_ddi = alloc(512)
        DDI = c(o_ddi, 512)
        KDI = c(alloc(2), 2)
        BDIc = c(alloc(4), 4)
        HMSK = cb(alloc(512), 512)
        HSB = cb(o_qdec, 512)
        o_kv = alloc(2048)
        KV = c(o_kv, 2048).rearrange("p (h e) -> p h e", h=NH)
        KVINf = c(o_kv, 2048)
        IDN = cb(alloc(64), 64)
        o_e2 = alloc(64)
        E2C = AR[0:8, o_e2:o_e2 + 64].bitcast(BF16)
        o_idf = alloc(8)
        IDF8 = AR[0:8, o_idf:o_idf + 8]
        ALH = cb(alloc(1024), 1024).rearrange("p (c j) -> p c j", c=DC)
        RTR = cb(alloc(64), 64).rearrange("p (c j) -> p c j", c=DC)
        BWL = cb(alloc(1024), 1024)
        BBT = c(alloc(16), 16)
        MSK8 = c(alloc(8), 8)
        ONE1 = c(alloc(1), 1)
        ONEB = cb(alloc(1), 1)
        o_oner = alloc(128)
        ONER = AR[0:1, o_oner:o_oner + 128]
        RROW = c(alloc(512), 512)
        R2ROW = c(alloc(512), 512)
        NSTG = 2
        o_stg = alloc(NSTG * 1024)
        STG = [c(o_stg + i * 1024, 1024).rearrange("p (c j) -> p c j", c=8) for i in range(NSTG)]
        o_wq = alloc(1024)
        WQB = cb(o_wq, 1024).rearrange("p (c j) -> p c j", c=DC)
        o_wk = alloc(1024)
        WKB = cb(o_wk, 1024).rearrange("p (c j) -> p c j", c=DC)
        o_wv = alloc(1024)
        WVB = cb(o_wv, 1024).rearrange("p (c j) -> p c j", c=DC)
        o_qs = alloc(512)
        QS2 = [cb(o_qs, 512)]
        o_ks = alloc(512)
        KS2 = [cb(o_ks, 512)]
        o_vs = alloc(512)
        VS2 = [cb(o_vs, 512)]
        o_qs1 = alloc(512)
        QS2.append(cb(o_qs1, 512))
        o_ks1 = alloc(512)
        KS2.append(cb(o_ks1, 512))
        o_vs1 = alloc(512)
        VS2.append(cb(o_vs1, 512))
        XB4 = [cb(o_qs1 + i * 256, 256) for i in range(4)]
        o_ov = alloc(576)
        KVB = cb(o_ov + 512, 64)
        OSC4 = [c(o_qs + i * 256, 256) for i in range(4)]
        SQB4 = [cb(o_qs + 1024 + i * 128, 128) for i in range(4)]
        KVS4 = [cb(o_qs + 1536 + i * 64, 64) for i in range(4)]
        o_d = alloc(2560)
        MROW = c(o_d, 512)
        XRS = [c(o_d + i * 512, 512) for i in range(3)]
        OST = [c(o_d + 1536 + i * 512, 512) for i in range(2)]
        GBUF = c(o_wk, 2048)
        GTMP = c(o_qs, 2048)
        LTF = AR[0:8, o_stg:o_stg + 1024]
        LTT = c(o_d, 8)
        MXC = c(o_d + 16, 1)
        OHT = cb(o_d + 24, 4)
        o_mfb = o_d + 64
        MFB = AR[0:8, o_mfb:o_mfb + 512].bitcast(BF16)
        WOB = cb(o_wq, 1024).rearrange("p (c j) -> p c j", c=DC)
        WBB = cb(o_wk, 1024).rearrange("p (c j) -> p c j", c=DC)
        WB2 = cb(o_wv, 1024).rearrange("p (c j) -> p c j", c=DC)
        WB3 = cb(o_stg, 1024).rearrange("p (c j) -> p c j", c=DC)
        RROW4 = c(o_d + 1536, 1024)
        QDEC2 = [cb(o_qdec, 512).rearrange("p (l t) -> p l t", l=4),
                 cb(o_d + 576, 512).rearrange("p (l t) -> p l t", l=4)]
        DDEC2 = [cb(o_ddec, 1024).rearrange("p (l t) -> p l t", l=4),
                 cb(o_d + 1088, 1024).rearrange("p (l t) -> p l t", l=4)]
        KDEC2 = [c(o_kdec, 8), c(o_d + 2112, 8)]
        VTD = [[cb(b + i * 64, 64) for i in range(2)] for b in (o_ov, o_d)]
        KTD = [[cb(b + 128 + i * 64, 64) for i in range(2)] for b in (o_ov, o_d)]
        QKPD = [[cb(b + 256 + i * 128, 128) for i in range(2)] for b in (o_ov, o_d)]


        xq = ["sync", "scalar", "sync", "scalar"]
        for xh in range(2):
            for ch in range(2):
                P.dma(xq[xh * 2 + ch], f"x{xh}{ch}",
                      Xf[:, ch * 8:(ch + 1) * 8, xh * 512:(xh + 1) * 512],
                      d_x.rearrange("(c p) t -> p c t", p=128)[:, ch * 8:(ch + 1) * 8,
                                                              xh * 512:(xh + 1) * 512],
                      track_read=False)
        for key, dst, src in [("QDI", QDI, d_qdi), ("DDI", DDI, d_ddi), ("KDI", KDI, d_kdi),
                              ("BDI", BDIc, d_bdi), ("SLP", SLP, d_slope), ("IDN", IDN, d_idb),
                              ("E2", E2C, d_e2), ("IDF8", IDF8, d_idf8),
                              ("BBT", BBT, d_bT), ("MSK8", MSK8, d_mask8)]:
            P.dma("sync", "c_" + key, dst, src, track_read=False)
        P.op("vector", lambda e: e.memset(ONE1, 1.0), w=[ONE1])
        P.op("vector", lambda e: e.memset(ONEB[:, 0:1], 1.0), w=[ONEB[:, 0:1]])
        P.op("vector", lambda e: e.memset(ONER, 1.0), w=[ONER])

        def ln_half(nt):
            tsl = slice(nt * 512, (nt + 1) * 512)
            for dc in range(DC):
                sl = dc % 4
                if dc % 2 == 0:
                    P.op("scalar", lambda e, dc=dc, tsl=tsl, sl=sl: e.activation(
                        XB4[sl], Xf[:, dc, tsl], AF.Square),
                        r=[Xf[:, dc, tsl]], w=[XB4[sl]])
                else:
                    P.op("vector", lambda e, dc=dc, tsl=tsl, sl=sl: e.tensor_tensor(
                        XB4[sl], Xf[:, dc, tsl], Xf[:, dc, tsl], ALU.mult),
                        r=[Xf[:, dc, tsl]], w=[XB4[sl]])
                P.op("tensor", lambda e, dc=dc, tsl=tsl: e.matmul(
                    pb4[0:1, 0:512], ONE1, Xf[:, dc, tsl],
                    start=(dc == 0), stop=(dc == DC - 1), skip_group_check=True),
                    r=[ONE1, Xf[:, dc, tsl]], w=[pb4[0:1, 0:512]])
                P.op("tensor", lambda e, dc=dc, sl=sl: e.matmul(
                    pb5[0:1, 0:512], ONEB[:, 0:1], XB4[sl],
                    start=(dc == 0), stop=(dc == DC - 1), skip_group_check=True),
                    r=[ONEB[:, 0:1], XB4[sl]], w=[pb5[0:1, 0:512]])
            P.op("vector", lambda e: e.tensor_scalar(MROW[0:1, 0:512], pb4[0:1, 0:512],
                                                     1.0 / D, None, ALU.mult),
                 r=[pb4[0:1, 0:512]], w=[MROW[0:1, 0:512]])
            P.op("vector", lambda e: e.tensor_scalar(R2ROW[0:1, 0:512], pb5[0:1, 0:512],
                                                     1.0 / D, LN_EPS, ALU.mult, ALU.add),
                 r=[pb5[0:1, 0:512]], w=[R2ROW[0:1, 0:512]])
            P.op("vector", lambda e: e.tensor_tensor(RROW[0:1, 0:512], MROW[0:1, 0:512],
                                                     MROW[0:1, 0:512], ALU.mult),
                 r=[MROW[0:1, 0:512]], w=[RROW[0:1, 0:512]])
            P.op("vector", lambda e: e.tensor_tensor(R2ROW[0:1, 0:512], R2ROW[0:1, 0:512],
                                                     RROW[0:1, 0:512], ALU.subtract),
                 r=[R2ROW[0:1, 0:512], RROW[0:1, 0:512]], w=[R2ROW[0:1, 0:512]])
            P.op("scalar", lambda e: e.activation(RROW[0:1, 0:512], R2ROW[0:1, 0:512], AF.Sqrt),
                 r=[R2ROW[0:1, 0:512]], w=[RROW[0:1, 0:512]])
            P.op("vector", lambda e: e.reciprocal(R2ROW[0:1, 0:512], RROW[0:1, 0:512]),
                 r=[RROW[0:1, 0:512]], w=[R2ROW[0:1, 0:512]])
            P.op("tensor", lambda e: e.matmul(pb3[:, 0:512], ONER, MROW[0:1, 0:512],
                                              start=True, stop=True, skip_group_check=True),
                 r=[ONER, MROW[0:1, 0:512]], w=[pb3[:, 0:512]])
            P.op("tensor", lambda e: e.matmul(pb2[:, 0:512], ONER, R2ROW[0:1, 0:512],
                                              start=True, stop=True, skip_group_check=True),
                 r=[ONER, R2ROW[0:1, 0:512]], w=[pb2[:, 0:512]])
            for dc in range(DC):
                P.op("vector", lambda e, dc=dc, tsl=tsl: e.tensor_tensor(
                    Xf[:, dc, tsl], Xf[:, dc, tsl], pb3[:, 0:512], ALU.subtract),
                    r=[Xf[:, dc, tsl], pb3[:, 0:512]], w=[Xf[:, dc, tsl]])
                P.op("vector", lambda e, dc=dc, tsl=tsl: e.tensor_tensor(
                    XN[:, dc, tsl], Xf[:, dc, tsl], pb2[:, 0:512], ALU.mult),
                    r=[Xf[:, dc, tsl], pb2[:, 0:512]], w=[XN[:, dc, tsl]])
        def load_stripe(dram_ap, dstW, eng=None):
            return P.dma("gpsimd", f"wst{dstW.offset}", dstW, dram_ap,
                         track_read=False)

        def mm_chain(bank, dstW, nt, half):
            return [
                (lambda e, dc=dc, nt=nt, bank=bank, dstW=dstW: e.matmul(
                    bank[:, 0:512], dstW[:, dc, :], XN[:, dc, nt * 512:(nt + 1) * 512],
                    start=(dc == 0), stop=(dc == DC - 1), skip_group_check=True),
                 [dstW[:, dc, :], XN[:, dc, nt * 512:(nt + 1) * 512]],
                 [bank[:, 0:512]])
                for dc in range(half * 8, half * 8 + 8)]

        def proj_group(dstW, out_sb, silu, nt):
            bank = pb[nt]
            P.gop("tensor", mm_chain(bank, dstW, nt, 0))
            P.gop("tensor", mm_chain(bank, dstW, nt, 1))
            if silu:
                P.op("scalar", lambda e, nt=nt, bank=bank, out_sb=out_sb: e.activation(
                    out_sb[:, nt * 512:(nt + 1) * 512], bank[:, 0:512], AF.Silu),
                    r=[bank[:, 0:512]], w=[out_sb[:, nt * 512:(nt + 1) * 512]])
            else:
                P.op("scalar", lambda e, nt=nt, bank=bank, out_sb=out_sb: e.copy(
                    out_sb[:, nt * 512:(nt + 1) * 512], bank[:, 0:512]),
                    r=[bank[:, 0:512]], w=[out_sb[:, nt * 512:(nt + 1) * 512]])

        def emit_masks(g, hls=range(4)):
            par = g % 2
            QDEC, DDEC, KDEC = QDEC2[par], DDEC2[par], KDEC2[par]
            for hl2 in hls:
                h2 = g * 4 + hl2
                P.op("scalar", lambda e, hl2=hl2, h2=h2, QDEC=QDEC: e.activation(
                    QDEC[:, hl2, :], QDI, AF.Exp, scale=SLP[:, h2:h2 + 1]),
                    r=[QDI, SLP[:, h2:h2 + 1]], w=[QDEC[:, hl2, :]])
                P.op("scalar", lambda e, hl2=hl2, h2=h2, DDEC=DDEC: e.activation(
                    DDEC[:, hl2, :], DDI, AF.Exp, scale=SLP[:, h2:h2 + 1]),
                    r=[DDI, SLP[:, h2:h2 + 1]], w=[DDEC[:, hl2, :]])
                P.op("scalar", lambda e, hl2=hl2, h2=h2, KDEC=KDEC: e.activation(
                    KDEC[:, hl2 * 2:hl2 * 2 + 2], KDI, AF.Exp, scale=SLP[:, h2:h2 + 1]),
                    r=[KDI, SLP[:, h2:h2 + 1]], w=[KDEC[:, hl2 * 2:hl2 * 2 + 2]])
                P.op("scalar", lambda e, h2=h2: e.activation(
                    BDP[:, h2 * 4:(h2 + 1) * 4], BDIc, AF.Exp, scale=SLP[:, h2:h2 + 1]),
                    r=[BDIc, SLP[:, h2:h2 + 1]], w=[BDP[:, h2 * 4:(h2 + 1) * 4]])

        def b_produce(h, hl, hp, blk):
            base = blk * 256
            bp = blk % 2
            mpar = (h // 4) % 2
            QDEC, DDEC, KDEC = QDEC2[mpar], DDEC2[mpar], KDEC2[mpar]
            VT, KT, QKP = VTD[bp], KTD[bp], QKPD[bp]
            QS, KS, VS = QS2[hp], KS2[hp], VS2[hp]
            for sc in range(2):
                P.op("tensor", lambda e, sc=sc, base=base, KS=KS, QS=QS: e.matmul(
                    pb[4 + sc][:, 0:256], KS[:, base + sc * 128:base + sc * 128 + 128],
                    QS[:, base:base + 256], start=True, stop=True, skip_group_check=True),
                    r=[KS[:, base + sc * 128:base + sc * 128 + 128], QS[:, base:base + 256]],
                    w=[pb[4 + sc][:, 0:256]])
            for sc in range(2):
                P.op("tensor", lambda e, sc=sc, base=base, KS=KS: e.transpose(
                    pb2[:, sc * 64:sc * 64 + 64].bitcast(BF16),
                    KS[:, base + sc * 128:base + sc * 128 + 128], IDN),
                    r=[KS[:, base + sc * 128:base + sc * 128 + 128], IDN],
                    w=[pb2[:, sc * 64:sc * 64 + 64]])
                P.op("tensor", lambda e, sc=sc, base=base, VS=VS: e.transpose(
                    pb3[:, sc * 64:sc * 64 + 64].bitcast(BF16),
                    VS[:, base + sc * 128:base + sc * 128 + 128], IDN),
                    r=[VS[:, base + sc * 128:base + sc * 128 + 128], IDN],
                    w=[pb3[:, sc * 64:sc * 64 + 64]])
            for sc in range(2):
                P.op("vector", lambda e, sc=sc, hl=hl, QKP=QKP: e.tensor_tensor(
                    QKP[sc], pb[4 + sc][:, 0:256], DDEC[:, hl, sc * 256:(sc + 1) * 256],
                    ALU.mult),
                    r=[pb[4 + sc][:, 0:256], DDEC[:, hl, sc * 256:(sc + 1) * 256]],
                    w=[QKP[sc]])
            P.op("vector", lambda e, h=h, hl=hl, base=base, QS=QS: e.tensor_tensor(
                QDl[:, h, base:base + 256], QS[:, base:base + 256], QDEC[:, hl, :],
                ALU.mult),
                r=[QS[:, base:base + 256], QDEC[:, hl, :]], w=[QDl[:, h, base:base + 256]])
            for sc in range(2):
                P.op("scalar", lambda e, sc=sc, hl=hl, KT=KT: e.activation(
                    KT[sc], pb2[:, sc * 64:sc * 64 + 64].bitcast(BF16), AF.Copy,
                    scale=KDEC[:, hl * 2 + sc:hl * 2 + sc + 1]),
                    r=[pb2[:, sc * 64:sc * 64 + 64], KDEC[:, hl * 2 + sc:hl * 2 + sc + 1]],
                    w=[KT[sc]])
                P.op("scalar", lambda e, sc=sc, VT=VT: e.copy(
                    VT[sc], pb3[:, sc * 64:sc * 64 + 64].bitcast(BF16)),
                    r=[pb3[:, sc * 64:sc * 64 + 64]], w=[VT[sc]])

        def b_consume(h, hl, hp, blk):
            base = blk * 256
            bp = blk % 2
            VT, KT, QKP = VTD[bp], KTD[bp], QKPD[bp]
            P.op("tensor", lambda e, VT=VT, QKP=QKP: e.matmul(
                pb6[:, 0:256], VT[0], QKP[0],
                start=True, stop=False, skip_group_check=True),
                r=[VT[0], QKP[0]], w=[pb6[:, 0:256]])
            P.op("tensor", lambda e, blk=blk, VT=VT, QKP=QKP: e.matmul(
                pb6[:, 0:256], VT[1], QKP[1],
                start=False, stop=(blk == 0), skip_group_check=True),
                r=[VT[1], QKP[1]], w=[pb6[:, 0:256]])
            if blk > 0:
                P.op("tensor", lambda e, h=h, base=base: e.matmul(
                    pb6[:, 0:256], KVB, QDl[:, h, base:base + 256],
                    start=False, stop=True, skip_group_check=True),
                    r=[KVB, QDl[:, h, base:base + 256]], w=[pb6[:, 0:256]])
            P.op("tensor", lambda e, KT=KT, VT=VT: e.matmul(
                pb7[:, 0:128], KT[0], VT[0],
                start=True, stop=False, skip_group_check=True),
                r=[KT[0], VT[0]], w=[pb7[:, 0:128]])
            P.op("tensor", lambda e, KT=KT, VT=VT: e.matmul(
                pb7[:, 0:128], KT[1], VT[1],
                start=False, stop=True, skip_group_check=True),
                r=[KT[1], VT[1]], w=[pb7[:, 0:128]])
            if blk == 0:
                P.op("vector", lambda e, h=h: e.tensor_copy(KV[:, h, :], pb7[:, 0:128]),
                     r=[pb7[:, 0:128]], w=[KV[:, h, :]])
            else:
                P.op("vector", lambda e, h=h: e.tensor_scalar(
                    KV[:, h, :], KV[:, h, :], BDP[:, h * 4 + 1:h * 4 + 2], None, ALU.mult),
                    r=[KV[:, h, :], BDP[:, h * 4 + 1:h * 4 + 2]], w=[KV[:, h, :]])
                P.op("vector", lambda e, h=h: e.tensor_tensor(
                    KV[:, h, :], KV[:, h, :], pb7[:, 0:128], ALU.add),
                    r=[KV[:, h, :], pb7[:, 0:128]], w=[KV[:, h, :]])
            if blk < NBLK - 1:
                P.op("vector", lambda e, h=h: e.tensor_copy(KVB, KV[:, h, :]),
                     r=[KV[:, h, :]], w=[KVB])
            P.op("scalar", lambda e, h=h, base=base: e.copy(
                O_[:, h, base:base + 256], pb6[:, 0:256]),
                r=[pb6[:, 0:256]], w=[O_[:, h, base:base + 256]])

        emit_masks(0)
        load_stripe(d_qkvu[0], WQB, eng="scalar")
        load_stripe(d_qkvu[16], WKB, eng="scalar")
        load_stripe(d_qkvu[32], WVB, eng="scalar")
        ln_half(0)
        proj_group(WQB, QS2[0], True, 0)
        proj_group(WKB, KS2[0], True, 0)
        ln_half(1)
        proj_group(WVB, VS2[0], False, 0)
        proj_group(WQB, QS2[0], True, 1)
        proj_group(WKB, KS2[0], True, 1)
        proj_group(WVB, VS2[0], False, 1)

        for h in range(NH):
            hl = h % 4
            hp = h % 2
            if h + 4 < NH:
                emit_masks(h // 4 + 1, [hl])
            if h == 2:
                P.dma("gpsimd", "bw", BWL, d_bw, track_read=False)
                P.dma("gpsimd", "aT", ALH, d_aT, track_read=False)
                P.dma("gpsimd", "rT", RTR, d_rT, track_read=False)
            nxt = []
            if h + 1 < NH:
                nxt = [(d_qkvu[h + 1], WQB, QS2[1 - hp], True),
                       (d_qkvu[16 + h + 1], WKB, KS2[1 - hp], True),
                       (d_qkvu[32 + h + 1], WVB, VS2[1 - hp], False)]
            b_produce(h, hl, hp, 0)
            b_produce(h, hl, hp, 1)
            b_consume(h, hl, hp, 0)
            if nxt:
                load_stripe(nxt[0][0], nxt[0][1], eng="scalar")
            b_consume(h, hl, hp, 1)
            if nxt:
                proj_group(nxt[0][1], nxt[0][2], nxt[0][3], 0)
            b_produce(h, hl, hp, 2)
            if nxt:
                load_stripe(nxt[1][0], nxt[1][1], eng="scalar")
                proj_group(nxt[0][1], nxt[0][2], nxt[0][3], 1)
            b_consume(h, hl, hp, 2)
            if nxt:
                proj_group(nxt[1][1], nxt[1][2], nxt[1][3], 0)
            b_produce(h, hl, hp, 3)
            if nxt:
                load_stripe(nxt[2][0], nxt[2][1])
                proj_group(nxt[1][1], nxt[1][2], nxt[1][3], 1)
            b_consume(h, hl, hp, 3)
            if nxt:
                proj_group(nxt[2][1], nxt[2][2], nxt[2][3], 0)
                proj_group(nxt[2][1], nxt[2][2], nxt[2][3], 1)

            if dbg and h == 1:
                dbg_n[0] += 1; P.dma("scalar", f"dbg{dbg_n[0]}", dbg_o["dbg_q"], QS2[1], track_write=False)
                dbg_n[0] += 1; P.dma("scalar", f"dbg{dbg_n[0]}", dbg_o["dbg_k"], KS2[1], track_write=False)
                dbg_n[0] += 1; P.dma("scalar", f"dbg{dbg_n[0]}", dbg_o["dbg_v"], VS2[1], track_write=False)

        P.dma("gpsimd", "sl", d_slocal, KVINf)
        m_cc = P.op("gpsimd", lambda e: e.collective_compute(
            "AllGather", ALU.bypass, replica_groups=[[0, 1], [2, 3], [4, 5], [6, 7]],
            ins=[d_slocal.opt()], outs=[d_sall.opt()]),
            r=[d_slocal], w=[d_sall], mark_sem="CC", mark_amt=None)

        for nt in range(NT):
            P.gop("tensor", [
                (lambda e, dc=dc, nt=nt: e.matmul(
                    pb5[:, 0:512], ALH[:, dc, :], XN[:, dc, nt * 512:(nt + 1) * 512],
                    start=(dc == 0), stop=(dc == DC - 1), skip_group_check=True),
                 [ALH[:, dc, :], XN[:, dc, nt * 512:(nt + 1) * 512]], [pb5[:, 0:512]])
                for dc in range(DC)])
            P.op("scalar", lambda e, nt=nt: e.copy(HSB[:, nt * 512:(nt + 1) * 512], pb5[:, 0:512]),
                 r=[pb5[:, 0:512]], w=[HSB[:, nt * 512:(nt + 1) * 512]])
        for nt in range(NT):
            P.gop("tensor", [
                (lambda e, dc=dc, nt=nt: e.matmul(
                    pb3[0:8, 0:512], RTR[:, dc, :], XN[:, dc, nt * 512:(nt + 1) * 512],
                    start=(dc == 0), stop=(dc == DC - 1), skip_group_check=True),
                 [RTR[:, dc, :], XN[:, dc, nt * 512:(nt + 1) * 512]], [pb3[0:8, 0:512]])
                for dc in range(DC)])
            P.op("vector", lambda e, nt=nt: e.tensor_copy(
                LTF[:, nt * 512:(nt + 1) * 512], pb3[0:8, 0:512]),
                r=[pb3[0:8, 0:512]], w=[LTF[:, nt * 512:(nt + 1) * 512]])
        for i in range(8):
            P.op("tensor", lambda e, i=i: e.transpose(
                pb2[:, 0:8], LTF[:, i * 128:(i + 1) * 128], IDF8),
                r=[LTF[:, i * 128:(i + 1) * 128], IDF8], w=[pb2[:, 0:8]])
            P.op("vector", lambda e: e.tensor_copy(LTT, pb2[:, 0:8]),
                 r=[pb2[:, 0:8]], w=[LTT])
            P.op("vector", lambda e: e.tensor_reduce(MXC, LTT, axis=AX.X, op=ALU.max),
                 r=[LTT], w=[MXC])
            P.op("vector", lambda e: e.tensor_scalar(OHT, LTT, MXC, None, ALU.is_equal),
                 r=[LTT, MXC], w=[OHT])
            P.op("tensor", lambda e: e.transpose(
                pb2[0:8, 128:192].bitcast(BF16), OHT, IDN),
                r=[OHT, IDN], w=[pb2[0:8, 128:192]])
            P.op("vector", lambda e, i=i: e.tensor_copy(
                MFB[:, i * 128:(i + 1) * 128], pb2[0:8, 128:192].bitcast(BF16)),
                r=[pb2[0:8, 128:192]], w=[MFB[:, i * 128:(i + 1) * 128]])
        for nt in range(NT):
            P.op("tensor", lambda e, nt=nt: e.matmul(
                pb6[:, 0:512], E2C, MFB[:, nt * 512:(nt + 1) * 512],
                start=True, stop=True, skip_group_check=True),
                r=[E2C, MFB[:, nt * 512:(nt + 1) * 512]], w=[pb6[:, 0:512]])
            P.op("vector", lambda e, nt=nt: e.tensor_tensor(
                HMSK[:, nt * 512:(nt + 1) * 512], HSB[:, nt * 512:(nt + 1) * 512],
                pb6[:, 0:512], ALU.mult),
                r=[HSB[:, nt * 512:(nt + 1) * 512], pb6[:, 0:512]],
                w=[HMSK[:, nt * 512:(nt + 1) * 512]])

        if dbg:
            dbg_n[0] += 1; P.dma("scalar", f"dbg{dbg_n[0]}", dbg_o["dbg_hm"], HMSK, track_write=False)


        GBH = [c(o_qs + i * 1024, 1024) for i in range(2)]
        for hf in range(2):
            gbb = GBH[hf]
            csl = slice(hf * 1024, (hf + 1) * 1024)
            P.dma("sync", f"gb{hf}", gbb, d_sall[0:128, csl])
            P.op("vector", lambda e, gbb=gbb, csl=csl: e.tensor_scalar(
                KVINf[:, csl], gbb, MSK8[:, 0:1], None, ALU.mult),
                r=[gbb, MSK8[:, 0:1]], w=[KVINf[:, csl]])

        if dbg:
            dbg_n[0] += 1; P.dma("scalar", f"dbg{dbg_n[0]}", dbg_o["dbg_kvin"], KVINf, track_write=False)

        RSB = c(o_qs + 1792, 256)

        def emit_umult(blk, heads):
            base = blk * 256
            for h in heads:
                eng = "vector"
                P.op(eng, lambda e, h=h, base=base: e.tensor_tensor(
                    O_[:, h, base:base + 256], O_[:, h, base:base + 256],
                    U_[:, h, base:base + 256], ALU.mult),
                    r=[O_[:, h, base:base + 256], U_[:, h, base:base + 256]],
                    w=[O_[:, h, base:base + 256]])

        def emit_rmult(blk):
            base = blk * 256
            P.op("tensor", lambda e, base=base: e.matmul(
                pb3[:, 0:256], ONER, RROW4[0:1, base:base + 256],
                start=True, stop=True, skip_group_check=True),
                r=[ONER, RROW4[0:1, base:base + 256]], w=[pb3[:, 0:256]])
            P.op("scalar", lambda e: e.copy(RSB, pb3[:, 0:256]),
                 r=[pb3[:, 0:256]], w=[RSB])
            for h in range(NH):
                eng = "vector"
                P.op(eng, lambda e, h=h, base=base: e.tensor_tensor(
                    O_[:, h, base:base + 256], O_[:, h, base:base + 256], RSB,
                    ALU.mult),
                    r=[O_[:, h, base:base + 256], RSB], w=[O_[:, h, base:base + 256]])

        ugen = []

        u_loaded = [False] * (NH + 1)

        def u_load(h):
            if h < NH and not u_loaded[h]:
                load_stripe(d_qkvu[48 + h], [WQB, WBB][h % 2])
                u_loaded[h] = True

        def u_head(h):
            u_load(h)
            u_load(h + 1)
            WUc = [WQB, WBB][h % 2]
            for nt in range(NT):
                bank = pb[nt]
                P.gop("tensor", mm_chain(bank, WUc, nt, 0))
                P.gop("tensor", mm_chain(bank, WUc, nt, 1))
                P.op("scalar", lambda e, nt=nt, h=h, bank=bank: e.copy(
                    U_[:, h, nt * 512:(nt + 1) * 512], bank[:, 0:512]),
                    r=[bank[:, 0:512]], w=[U_[:, h, nt * 512:(nt + 1) * 512]])

        cbank = [pb2, pb5, pb6, pb7]
        uh = [0]
        pend_um = []
        rm_done = [0]

        def flush_sched():
            keep = []
            for b, heads in pend_um:
                if max(heads) < uh[0]:
                    emit_umult(b, heads)
                else:
                    keep.append((b, heads))
            pend_um[:] = keep

        while uh[0] < 3:
            u_head(uh[0]); uh[0] += 1
        for blk in range(NBLK):
            base = blk * 256
            for h0 in range(0, NH, 4):
                if blk > 0 and h0 == 4 and rm_done[0] < blk:
                    emit_rmult(rm_done[0]); rm_done[0] += 1
                if uh[0] < NH:
                    u_head(uh[0]); uh[0] += 1
                for i in range(4):
                    h = h0 + i
                    P.op("vector", lambda e, h=h, blk=blk, i=i: e.tensor_scalar(
                        KVS4[i], KVINf[:, h * 128:(h + 1) * 128],
                        BDP[:, h * 4 + blk:h * 4 + blk + 1], None, ALU.mult),
                        r=[KVINf[:, h * 128:(h + 1) * 128], BDP[:, h * 4 + blk:h * 4 + blk + 1]],
                        w=[KVS4[i]])
                for i in range(4):
                    h = h0 + i
                    P.op("tensor", lambda e, h=h, base=base, i=i: e.matmul(
                        cbank[i][:, 0:256], KVS4[i], QDl[:, h, base:base + 256],
                        start=True, stop=True, skip_group_check=True),
                        r=[KVS4[i], QDl[:, h, base:base + 256]], w=[cbank[i][:, 0:256]])
                for i in range(4):
                    h = h0 + i
                    P.op("vector", lambda e, h=h, base=base, i=i: e.tensor_tensor(
                        OSC4[i], O_[:, h, base:base + 256], cbank[i][:, 0:256], ALU.add),
                        r=[O_[:, h, base:base + 256], cbank[i][:, 0:256]], w=[OSC4[i]])
                for i in range(4):
                    P.op("scalar", lambda e, i=i: e.activation(SQB4[i], OSC4[i], AF.Square),
                         r=[OSC4[i]], w=[SQB4[i]])
                for i in range(4):
                    h = h0 + i
                    P.op("tensor", lambda e, h=h, i=i: e.matmul(
                        pb4[0:1, 0:256], ONEB[:, 0:1], SQB4[i],
                        start=(h == 0), stop=(h == NH - 1), skip_group_check=True),
                        r=[ONEB[:, 0:1], SQB4[i]], w=[pb4[0:1, 0:256]])
                for i in range(4):
                    h = h0 + i
                    P.op("vector", lambda e, h=h, base=base, i=i: e.tensor_copy(
                        O_[:, h, base:base + 256], OSC4[i]),
                        r=[OSC4[i]], w=[O_[:, h, base:base + 256]])
                pend_um.append((blk, list(range(h0, h0 + 4))))
                flush_sched()
            P.op("vector", lambda e: e.tensor_scalar(R2ROW[0:1, 0:256], pb4[0:1, 0:256],
                                                     1.0 / HID, RMS_EPS, ALU.mult, ALU.add),
                 r=[pb4[0:1, 0:256]], w=[R2ROW[0:1, 0:256]])
            P.op("scalar", lambda e: e.activation(RROW[0:1, 0:256], R2ROW[0:1, 0:256], AF.Sqrt),
                 r=[R2ROW[0:1, 0:256]], w=[RROW[0:1, 0:256]])
            P.op("vector", lambda e, base=base: e.reciprocal(
                RROW4[0:1, base:base + 256], RROW[0:1, 0:256]),
                r=[RROW[0:1, 0:256]], w=[RROW4[0:1, base:base + 256]])
        while uh[0] < NH:
            u_head(uh[0]); uh[0] += 1
            flush_sched()
        flush_sched()
        assert not pend_um
        while rm_done[0] < NBLK:
            emit_rmult(rm_done[0]); rm_done[0] += 1
        if dbg:
            dbg_n[0] += 1; P.dma("scalar", f"dbg{dbg_n[0]}", dbg_o["dbg_u"], U_.rearrange("p h t -> p (h t)"), track_write=False)

        if dbg:
            dbg_n[0] += 1; P.dma("scalar", f"dbg{dbg_n[0]}", dbg_o["dbg_g"], O_.rearrange("p h t -> p (h t)"), track_write=False)

        WOUTB = [WOB, WB2]
        WBASB = [WBB, WB3]
        load_stripe(d_wout[0], WOUTB[0])
        load_stripe(d_wbase[0], WBASB[0])
        for oc in range(16):
            if oc + 1 < 16:
                load_stripe(d_wout[oc + 1], WOUTB[(oc + 1) % 2])
                load_stripe(d_wbase[oc + 1], WBASB[(oc + 1) % 2])
            WOc, WBc = WOUTB[oc % 2], WBASB[oc % 2]
            for nt in range(NT):
                ti = oc * 2 + nt
                bank = pb[ti % 4]
                tsl = slice(nt * 512, (nt + 1) * 512)
                xsl = ti % 3
                P.dma("sync", f"xr{xsl}", XRS[xsl], d_x[oc * 128:(oc + 1) * 128, tsl],
                      track_read=False)
                for half in range(2):
                    P.gop("tensor", [
                        (lambda e, dc=dc, nt=nt, bank=bank, WBc=WBc: e.matmul(
                            bank[:, 0:512], WBc[:, dc, :], XN[:, dc, nt * 512:(nt + 1) * 512],
                            start=(dc == 0), stop=False, skip_group_check=True),
                         [WBc[:, dc, :], XN[:, dc, nt * 512:(nt + 1) * 512]], [bank[:, 0:512]])
                        for dc in range(half * 8, half * 8 + 8)])
                for half in range(2):
                    items = [
                        (lambda e, hc=hc, nt=nt, bank=bank, WOc=WOc: e.matmul(
                            bank[:, 0:512], WOc[:, hc, :], O_[:, hc, nt * 512:(nt + 1) * 512],
                            start=False, stop=False, skip_group_check=True),
                         [WOc[:, hc, :], O_[:, hc, nt * 512:(nt + 1) * 512]], [bank[:, 0:512]])
                        for hc in range(half * 8, half * 8 + 8)]
                    if half == 1:
                        items.append(
                            (lambda e, oc=oc, nt=nt, bank=bank: e.matmul(
                                bank[:, 0:512], BWL[:, oc * 128:(oc + 1) * 128],
                                HMSK[:, nt * 512:(nt + 1) * 512],
                                start=False, stop=True, skip_group_check=True),
                             [BWL[:, oc * 128:(oc + 1) * 128],
                              HMSK[:, nt * 512:(nt + 1) * 512]], [bank[:, 0:512]]))
                    P.gop("tensor", items)
                osl = ti % 2
                P.op("scalar", lambda e, oc=oc, osl=osl, bank=bank: e.activation(
                    OST[osl], bank[:, 0:512], AF.Identity, bias=BBT[:, oc:oc + 1]),
                    r=[bank[:, 0:512], BBT[:, oc:oc + 1]], w=[OST[osl]])
                P.op("vector", lambda e, osl=osl, xsl=xsl: e.tensor_tensor(
                    OST[osl], OST[osl], XRS[xsl], ALU.add),
                    r=[OST[osl], XRS[xsl]], w=[OST[osl]])
                P.dma("scalar", f"y{osl}", d_y[oc * 128:(oc + 1) * 128, tsl], OST[osl],
                      track_write=False)

        final_waits = [(s, P.cnt[s]) for s in P.cnt
                       if s.startswith("dma_y") or s.startswith("dma_dbg")]
        P.barrier_wait("scalar", final_waits)

        sem_names = ["PE", "DV", "AC", "PL", "CC"] + sorted(
            s for s in P.cnt if s.startswith("dma_"))
        sems = {n: ctx.enter_context(nc.semaphore(n)) for n in sem_names}
        block = ctx.enter_context(nc.Block())

        @block.tensor
        def _(eng):
            P.replay("tensor", eng, sems)

        @block.vector
        def _(eng):
            P.replay("vector", eng, sems)

        @block.scalar
        def _(eng):
            P.replay("scalar", eng, sems)

        @block.gpsimd
        def _(eng):
            P.replay("gpsimd", eng, sems)

        @block.sync
        def _(eng):
            P.replay("sync", eng, sems)

    return nc


def _host_constants():
    QDI = np.broadcast_to(-(np.arange(256, dtype=np.float32) + 1.0), (128, 256)).copy()
    DDI = np.empty((128, 512), np.float32)
    m = np.arange(256, dtype=np.float32)
    for sc in range(2):
        for p in range(128):
            s = sc * 128 + p
            DDI[p, sc * 256:(sc + 1) * 256] = np.where(m >= s, -(m - s), DDBIG)
    KDI = np.empty((128, 2), np.float32)
    for j in range(2):
        KDI[:, j] = -(255.0 - (j * 128 + np.arange(128, dtype=np.float32)))
    BDIc = np.broadcast_to(np.array([0.0, -256.0, -512.0, -768.0], np.float32), (128, 4)).copy()
    identB = np.eye(128, dtype=BF)
    E2 = np.zeros((8, 128), BF)
    for e in range(8):
        E2[e, e * 16:(e + 1) * 16] = BF(SCALING)
    IDF8 = np.eye(8, dtype=np.float32)
    return QDI, DDI, KDI, BDIc, identB, E2, IDF8


def kernel(L0_attn_x, L0_attn_slope_rate, L0_attn_W_qkvu, L0_attn_W_out,
           L0_ffn_W_base, L0_ffn_b_base, L0_ffn_router_W, L0_ffn_A, L0_ffn_Bw):
    x = np.ascontiguousarray(np.asarray(L0_attn_x, dtype=np.float32)).reshape(8192, D)
    Wqkvu = np.asarray(L0_attn_W_qkvu, np.float32)
    Wout = np.asarray(L0_attn_W_out, np.float32)
    Wbase = np.asarray(L0_ffn_W_base, np.float32)
    bbase = np.asarray(L0_ffn_b_base, np.float32)
    rW = np.asarray(L0_ffn_router_W, np.float32)
    A = np.asarray(L0_ffn_A, np.float32)
    Bw = np.asarray(L0_ffn_Bw, np.float32)
    slope = np.asarray(L0_attn_slope_rate, np.float32).reshape(NH)

    qkvuS = np.ascontiguousarray(Wqkvu.reshape(64, 128, DC, 128).transpose(0, 3, 2, 1))
    woutS = np.ascontiguousarray(Wout.reshape(16, 128, DC, 128).transpose(0, 3, 2, 1))
    wbaseS = np.ascontiguousarray(Wbase.reshape(16, 128, DC, 128).transpose(0, 3, 2, 1))
    aT = np.ascontiguousarray(A.reshape(E * RK, DC, 128).transpose(2, 1, 0))
    rT = np.ascontiguousarray(rW.reshape(E, DC, 128).transpose(2, 1, 0))
    bwF = np.ascontiguousarray(Bw.transpose(0, 2, 1).reshape(E * RK, HID))
    bT = np.ascontiguousarray(bbase.reshape(16, 128).T)
    slopeB = np.broadcast_to(slope.reshape(1, NH), (128, NH)).copy()
    QDI, DDI, KDI, BDIc, identB, E2, IDF8 = _host_constants()

    shared = dict(qkvuS=qkvuS, woutS=woutS, wbaseS=wbaseS, aT=aT, rT=rT, bwF=bwF,
                  bT=bT, slopeB=slopeB, QDI=QDI, DDI=DDI, KDI=KDI, BDIc=BDIc,
                  identB=identB, E2=E2, IDF8=IDF8)
    in_maps = []
    for cc in range(8):
        m8 = np.zeros((128, 8), np.float32)
        if cc % 2 == 1:
            m8[:, 0] = 1.0
        xc = np.ascontiguousarray(x[cc * T:(cc + 1) * T].T)
        in_maps.append(dict(x=xc, maskv8=m8, **shared))

    nc = build_nc()
    res = run_bass_kernel_spmd(nc, in_maps, core_ids=list(range(8)))
    out = np.empty((8192, D), np.float32)
    for cc in range(8):
        out[cc * T:(cc + 1) * T] = res.results[cc]["y"].T
    return out.reshape(4, 2048, D)


if __name__ == "__main__":
    import reference as _r
    inp = _r.setup_inputs()
    got = kernel(**{k: np.asarray(v) for k, v in inp.items()})
    want = np.asarray(_r.reference(**inp))
    num = float(((got - want) ** 2).mean())
    den = float((want ** 2).mean())
    print("res-var ratio:", num / den)


# version 23
# speedup vs baseline: 1.3706x; 1.0587x over previous
import numpy as np
import ml_dtypes
import concourse.bass as bass
import concourse.mybir as mybir
from concourse.bass_utils import run_bass_kernel_spmd

F32 = mybir.dt.float32
BF16 = mybir.dt.bfloat16
AF = mybir.ActivationFunctionType
ALU = mybir.AluOpType
AX = mybir.AxisListType

BF = ml_dtypes.bfloat16

D = 2048
HID = 2048
NH = 16
T = 1024
NT = 2
NBLK = 4
DC = 16
E = 8
RK = 16
SCALING = 2.0
LN_EPS = 1e-5
RMS_EPS = 1e-6
DDBIG = -1.0e5

ENGS = ("tensor", "vector", "scalar", "gpsimd", "sync")
ESEM = {"tensor": "PE", "vector": "DV", "scalar": "AC", "gpsimd": "PL", "sync": None}
PAGE = 8192


def _ap_ranges(ap):
    t = ap.tensor
    if t.name.startswith("pb"):
        return (t.name, 0, 2048)
    esz = mybir.dt.size(ap.dtype)
    rowbytes = None
    shp = list(t.shape)
    if len(shp) == 2 and shp[0] <= 128 and t.name.startswith("arena"):
        rowbytes = shp[1] * mybir.dt.size(t.dtype)
    lo = ap.offset * esz
    span = 0
    for step, cnt in ap.ap:
        sb = step * esz
        if rowbytes is not None and cnt <= 128 and sb == rowbytes:
            continue
        span += sb * (cnt - 1)
    return (t.name, lo, lo + span + esz)


class Prog:

    def __init__(self):
        self.ops = {e: [] for e in ENGS}
        self.cnt = {}
        self.K = {e: {} for e in ENGS}
        self.SNAP = {}
        self.pages = {}
        self.hist = {}
        self.hidx = 0
        self.dma_slots = set()

    def _entries_overlapping(self, name, lo, hi):
        out = set()
        for pg in range(lo // PAGE, hi // PAGE + 1):
            for i in self.pages.get((name, pg), ()):
                ent = self.hist[i]
                if ent[5] and ent[1] < hi and lo < ent[2]:
                    out.add(i)
        return out

    def _record(self, name, lo, hi, mark, is_write):
        i = self.hidx
        self.hidx += 1
        self.hist[i] = [name, lo, hi, mark, is_write, True]
        for pg in range(lo // PAGE, hi // PAGE + 1):
            self.pages.setdefault((name, pg), set()).add(i)
        return i

    def _auto_waits(self, eng, r, w):
        need = {}
        kill = []
        for ap in r:
            name, lo, hi = _ap_ranges(ap)
            for i in self._entries_overlapping(name, lo, hi):
                ent = self.hist[i]
                if ent[4]:
                    s, v = ent[3]
                    if need.get(s, 0) < v:
                        need[s] = v
        for ap in w:
            name, lo, hi = _ap_ranges(ap)
            for i in self._entries_overlapping(name, lo, hi):
                ent = self.hist[i]
                s, v = ent[3]
                if need.get(s, 0) < v:
                    need[s] = v
                if ent[1] >= lo and ent[2] <= hi:
                    kill.append(i)
        for i in kill:
            self.hist[i][5] = False
        own = ESEM[eng]
        out = []
        for s, v in need.items():
            if eng == "tensor" and s == "PE":
                continue
            if self.K[eng].get(s, 0) >= v:
                continue
            out.append((s, v))
        return out

    def _apply_waits(self, eng, waits):
        K = self.K[eng]
        for s, v in waits:
            if K.get(s, 0) >= v:
                continue
            snap = self.SNAP.get((s, v))
            if snap:
                for s2, v2 in snap.items():
                    if K.get(s2, 0) < v2:
                        K[s2] = v2
            if K.get(s, 0) < v:
                K[s] = v

    def _push(self, eng, fn, waits, incs):
        self.ops[eng].append({"waits": waits, "fn": fn, "incs": incs})

    def op(self, eng, fn, r=(), w=(), extra=(), mark_sem=None, mark_amt=1):
        ws = dict()
        for m in extra:
            if m is None:
                continue
            s, v = m
            if v > 0 and ws.get(s, 0) < v:
                ws[s] = v
        for s, v in self._auto_waits(eng, r, w):
            if ws.get(s, 0) < v:
                ws[s] = v
        waits = [(s, v) for s, v in ws.items() if not (eng == "tensor" and s == "PE")
                 and self.K[eng].get(s, 0) < v]
        self._apply_waits(eng, waits)
        if mark_sem is None:
            mark_sem = ESEM[eng]
        self.cnt[mark_sem] = self.cnt.get(mark_sem, 0) + (1 if mark_amt is None else mark_amt)
        mark = (mark_sem, self.cnt[mark_sem])
        self._push(eng, fn, waits, [(mark_sem, mark_amt)])
        snap = dict(self.K[eng])
        if ESEM[eng] is not None and mark_sem == ESEM[eng]:
            snap[mark_sem] = mark[1]
        else:
            snap[mark_sem] = mark[1]
        self.SNAP[mark] = snap
        for ap in r:
            name, lo, hi = _ap_ranges(ap)
            self._record(name, lo, hi, mark, False)
        for ap in w:
            name, lo, hi = _ap_ranges(ap)
            self._record(name, lo, hi, mark, True)
        return mark

    def gop(self, eng, items, extra=()):
        ws = dict()
        for m in extra:
            if m is None:
                continue
            s, v = m
            if v > 0 and ws.get(s, 0) < v:
                ws[s] = v
        for fn, r, w in items:
            for s, v in self._auto_waits(eng, r, w):
                if ws.get(s, 0) < v:
                    ws[s] = v
        waits = [(s, v) for s, v in ws.items() if not (eng == "tensor" and s == "PE")
                 and self.K[eng].get(s, 0) < v]
        self._apply_waits(eng, waits)
        sem = ESEM[eng]
        self.cnt[sem] = self.cnt.get(sem, 0) + 1
        mark = (sem, self.cnt[sem])
        for i, (fn, r, w) in enumerate(items):
            self._push(eng, fn, waits if i == 0 else [],
                       [(sem, 1)] if i == len(items) - 1 else [])
        snap = dict(self.K[eng])
        snap[sem] = mark[1]
        self.SNAP[mark] = snap
        for fn, r, w in items:
            for ap in r:
                name, lo, hi = _ap_ranges(ap)
                self._record(name, lo, hi, mark, False)
            for ap in w:
                name, lo, hi = _ap_ranges(ap)
                self._record(name, lo, hi, mark, True)
        return mark

    def dma(self, eng, slot, out, in_, extra=(), track_read=True, track_write=True):
        sem = "dma_" + slot
        self.dma_slots.add(sem)
        r = [in_] if track_read else []
        w = [out] if track_write else []
        return self.op(eng, lambda e, o=out, i=in_: e.dma_start(out=o, in_=i),
                       r=r, w=w, extra=extra, mark_sem=sem, mark_amt=16)

    def barrier_wait(self, eng, extra):
        self.op(eng, lambda e: e.nop(), extra=extra)

    def replay(self, eng_name, engine, sems):
        hi = {}
        for ent in self.ops[eng_name]:
            for s, v in ent["waits"]:
                if hi.get(s, 0) >= v:
                    continue
                engine.wait_ge(sems[s], v)
                hi[s] = v
            inst = ent["fn"](engine)
            for s, a in ent["incs"]:
                if a is None:
                    inst.then_inc(sems[s])
                else:
                    inst.then_inc(sems[s], a)


def build_nc(dbg=False):
    nc = bass.Bass("TRN2", target_bir_lowering=False, debug=False)

    d_x = nc.dram_tensor("x", [D, T], F32, kind="ExternalInput").ap()
    d_qkvu = nc.dram_tensor("qkvuS", [64, 128, DC, 128], F32, kind="ExternalInput").ap()
    d_wout = nc.dram_tensor("woutS", [16, 128, DC, 128], F32, kind="ExternalInput").ap()
    d_wbase = nc.dram_tensor("wbaseS", [16, 128, DC, 128], F32, kind="ExternalInput").ap()
    d_aT = nc.dram_tensor("aT", [128, DC, 128], F32, kind="ExternalInput").ap()
    d_rT = nc.dram_tensor("rT", [128, DC, 8], F32, kind="ExternalInput").ap()
    d_bw = nc.dram_tensor("bwF", [128, HID], F32, kind="ExternalInput").ap()
    d_bT = nc.dram_tensor("bT", [128, 16], F32, kind="ExternalInput").ap()
    d_slope = nc.dram_tensor("slopeB", [128, 16], F32, kind="ExternalInput").ap()
    d_mask8 = nc.dram_tensor("maskv8", [128, 8], F32, kind="ExternalInput").ap()
    d_qdi = nc.dram_tensor("QDI", [128, 256], F32, kind="ExternalInput").ap()
    d_ddi = nc.dram_tensor("DDI", [128, 512], F32, kind="ExternalInput").ap()
    d_kdi = nc.dram_tensor("KDI", [128, 2], F32, kind="ExternalInput").ap()
    d_bdi = nc.dram_tensor("BDIc", [128, 4], F32, kind="ExternalInput").ap()
    d_idb = nc.dram_tensor("identB", [128, 128], BF16, kind="ExternalInput").ap()
    d_e2 = nc.dram_tensor("E2", [8, 128], BF16, kind="ExternalInput").ap()
    d_idf8 = nc.dram_tensor("IDF8", [8, 8], F32, kind="ExternalInput").ap()
    d_y = nc.dram_tensor("y", [D, T], F32, kind="ExternalOutput").ap()

    d_slocal = nc.dram_tensor("s_local", [128, NH * 128], F32).ap()
    d_sall = nc.dram_tensor("s_all", [2 * 128, NH * 128], F32).ap()

    dbg_o = {}
    dbg_n = [0]
    if dbg:
        for nm, shp, dt in [("dbg_xn", [128, DC * T], BF16), ("dbg_q", [128, T], BF16),
                            ("dbg_k", [128, T], BF16), ("dbg_v", [128, T], BF16),
                            ("dbg_o", [128, NH * T], BF16), ("dbg_kv", [128, NH * 128], F32),
                            ("dbg_u", [128, NH * T], BF16), ("dbg_g", [128, NH * T], BF16),
                            ("dbg_hm", [128, T], BF16), ("dbg_kvin", [128, NH * 128], F32),
                            ("dbg_qd", [128, NH * T], BF16)]:
            dbg_o[nm] = nc.dram_tensor(nm, shp, dt, kind="ExternalOutput").ap()

    P = Prog()

    from contextlib import ExitStack
    with ExitStack() as ctx:
        AR = ctx.enter_context(nc.sbuf_tensor("arena", [128, 53196], F32))
        pb = [ctx.enter_context(nc.psum_tensor(f"pb{i}", [128, 512], F32)) for i in range(8)]
        pb0, pb1, pb2, pb3, pb4, pb5, pb6, pb7 = pb

        pos = [0]

        def alloc(n, align=64):
            p = (pos[0] + align - 1) // align * align
            pos[0] = p + n
            assert pos[0] <= 53196, f"arena overflow at {pos[0]}"
            return p

        def c(off, n):
            return AR[:, off:off + n]

        def cb(off, n):
            return AR[:, off:off + n].bitcast(BF16)

        o_xn = alloc(8192)
        XN = cb(o_xn, 8192).rearrange("p (c t) -> p c t", c=DC)
        o_o = alloc(8192)
        O_ = cb(o_o, 8192).rearrange("p (h t) -> p h t", h=NH)
        o_qd = alloc(8192)
        QDl = cb(o_qd, 8192).rearrange("p (h t) -> p h t", h=NH)
        Xf = c(o_o, 16384).rearrange("p (c t) -> p c t", c=DC)
        o_u = alloc(8192)
        U_ = cb(o_u, 8192).rearrange("p (h t) -> p h t", h=NH)
        o_qdec = alloc(512)
        o_ddec = alloc(1024)
        o_kdec = alloc(8)
        QDEC = None
        BDP = c(alloc(64), 64)
        SLP = c(alloc(16), 16)
        o_qdi = alloc(256)
        QDI = c(o_qdi, 256)
        o_ddi = alloc(512)
        DDI = c(o_ddi, 512)
        KDI = c(alloc(2), 2)
        BDIc = c(alloc(4), 4)
        HMSK = cb(alloc(512), 512)
        HSB = cb(o_qdec, 512)
        o_kv = alloc(2048)
        KV = c(o_kv, 2048).rearrange("p (h e) -> p h e", h=NH)
        KVINf = c(o_kv, 2048)
        IDN = cb(alloc(64), 64)
        o_e2 = alloc(64)
        E2C = AR[0:8, o_e2:o_e2 + 64].bitcast(BF16)
        o_idf = alloc(8)
        IDF8 = AR[0:8, o_idf:o_idf + 8]
        ALH = cb(alloc(1024), 1024).rearrange("p (c j) -> p c j", c=DC)
        RTR = cb(alloc(64), 64).rearrange("p (c j) -> p c j", c=DC)
        BWL = cb(alloc(1024), 1024)
        BBT = c(alloc(16), 16)
        MSK8 = c(alloc(8), 8)
        ONE1 = c(alloc(1), 1)
        ONEB = cb(alloc(1), 1)
        o_oner = alloc(128)
        ONER = AR[0:1, o_oner:o_oner + 128]
        RROW = c(alloc(512), 512)
        R2ROW = c(alloc(512), 512)
        NSTG = 2
        o_stg = alloc(NSTG * 1024)
        STG = [c(o_stg + i * 1024, 1024).rearrange("p (c j) -> p c j", c=8) for i in range(NSTG)]
        o_wq = alloc(1024)
        WQB = cb(o_wq, 1024).rearrange("p (c j) -> p c j", c=DC)
        o_wk = alloc(1024)
        WKB = cb(o_wk, 1024).rearrange("p (c j) -> p c j", c=DC)
        o_wv = alloc(1024)
        WVB = cb(o_wv, 1024).rearrange("p (c j) -> p c j", c=DC)
        o_qs = alloc(512)
        QS2 = [cb(o_qs, 512)]
        o_ks = alloc(512)
        KS2 = [cb(o_ks, 512)]
        o_vs = alloc(512)
        VS2 = [cb(o_vs, 512)]
        o_qs1 = alloc(512)
        QS2.append(cb(o_qs1, 512))
        o_ks1 = alloc(512)
        KS2.append(cb(o_ks1, 512))
        o_vs1 = alloc(512)
        VS2.append(cb(o_vs1, 512))
        XB4 = [cb(o_qs1 + i * 256, 256) for i in range(4)]
        o_ov = alloc(576)
        KVB = cb(o_ov + 512, 64)
        OSC4 = [c(o_qs + i * 256, 256) for i in range(4)]
        SQB4 = [cb(o_qs + 1024 + i * 128, 128) for i in range(4)]
        KVS4 = [cb(o_qs + 1536 + i * 64, 64) for i in range(4)]
        o_d = alloc(2560)
        MROW = c(o_d, 512)
        XRS = [c(o_d + i * 512, 512) for i in range(3)]
        OST = [c(o_d + 1536 + i * 512, 512) for i in range(2)]
        GBUF = c(o_wk, 2048)
        GTMP = c(o_qs, 2048)
        LTF = AR[0:8, o_stg:o_stg + 1024]
        LTT = c(o_d, 8)
        MXC = c(o_d + 16, 1)
        OHT = cb(o_d + 24, 4)
        o_mfb = o_d + 64
        MFB = AR[0:8, o_mfb:o_mfb + 512].bitcast(BF16)
        WOB = cb(o_wq, 1024).rearrange("p (c j) -> p c j", c=DC)
        WBB = cb(o_wk, 1024).rearrange("p (c j) -> p c j", c=DC)
        WB2 = cb(o_wv, 1024).rearrange("p (c j) -> p c j", c=DC)
        WB3 = cb(o_stg, 1024).rearrange("p (c j) -> p c j", c=DC)
        RROW4 = c(o_d + 1536, 1024)
        QDEC2 = [cb(o_qdec, 512).rearrange("p (l t) -> p l t", l=4),
                 cb(o_d + 576, 512).rearrange("p (l t) -> p l t", l=4)]
        DDEC2 = [cb(o_ddec, 1024).rearrange("p (l t) -> p l t", l=4),
                 cb(o_d + 1088, 1024).rearrange("p (l t) -> p l t", l=4)]
        KDEC2 = [c(o_kdec, 8), c(o_d + 2112, 8)]
        VTD = [[cb(b + i * 64, 64) for i in range(2)] for b in (o_ov, o_d)]
        KTD = [[cb(b + 128 + i * 64, 64) for i in range(2)] for b in (o_ov, o_d)]
        QKPD = [[cb(b + 256 + i * 128, 128) for i in range(2)] for b in (o_ov, o_d)]


        for key, dst, src in [("SLP", SLP, d_slope), ("QDI", QDI, d_qdi), ("DDI", DDI, d_ddi),
                              ("KDI", KDI, d_kdi), ("BDI", BDIc, d_bdi), ("IDN", IDN, d_idb),
                              ("E2", E2C, d_e2), ("IDF8", IDF8, d_idf8),
                              ("BBT", BBT, d_bT), ("MSK8", MSK8, d_mask8)]:
            P.dma("sync", "c_" + key, dst, src, track_read=False)
        xq = ["sync", "scalar", "sync", "scalar"]
        for xh in range(2):
            for ch in range(2):
                P.dma(xq[xh * 2 + ch], f"x{xh}{ch}",
                      Xf[:, ch * 8:(ch + 1) * 8, xh * 512:(xh + 1) * 512],
                      d_x.rearrange("(c p) t -> p c t", p=128)[:, ch * 8:(ch + 1) * 8,
                                                              xh * 512:(xh + 1) * 512],
                      track_read=False)
        P.op("vector", lambda e: e.memset(ONE1, 1.0), w=[ONE1])
        P.op("vector", lambda e: e.memset(ONEB[:, 0:1], 1.0), w=[ONEB[:, 0:1]])
        P.op("vector", lambda e: e.memset(ONER, 1.0), w=[ONER])

        def ln_half(nt):
            tsl = slice(nt * 512, (nt + 1) * 512)
            for dc in range(DC):
                sl = dc % 4
                if dc % 2 == 0:
                    P.op("scalar", lambda e, dc=dc, tsl=tsl, sl=sl: e.activation(
                        XB4[sl], Xf[:, dc, tsl], AF.Square),
                        r=[Xf[:, dc, tsl]], w=[XB4[sl]])
                else:
                    P.op("vector", lambda e, dc=dc, tsl=tsl, sl=sl: e.tensor_tensor(
                        XB4[sl], Xf[:, dc, tsl], Xf[:, dc, tsl], ALU.mult),
                        r=[Xf[:, dc, tsl]], w=[XB4[sl]])
                P.op("tensor", lambda e, dc=dc, tsl=tsl: e.matmul(
                    pb4[0:1, 0:512], ONE1, Xf[:, dc, tsl],
                    start=(dc == 0), stop=(dc == DC - 1), skip_group_check=True),
                    r=[ONE1, Xf[:, dc, tsl]], w=[pb4[0:1, 0:512]])
                P.op("tensor", lambda e, dc=dc, sl=sl: e.matmul(
                    pb5[0:1, 0:512], ONEB[:, 0:1], XB4[sl],
                    start=(dc == 0), stop=(dc == DC - 1), skip_group_check=True),
                    r=[ONEB[:, 0:1], XB4[sl]], w=[pb5[0:1, 0:512]])
            P.op("vector", lambda e: e.tensor_scalar(MROW[0:1, 0:512], pb4[0:1, 0:512],
                                                     1.0 / D, None, ALU.mult),
                 r=[pb4[0:1, 0:512]], w=[MROW[0:1, 0:512]])
            P.op("vector", lambda e: e.tensor_scalar(R2ROW[0:1, 0:512], pb5[0:1, 0:512],
                                                     1.0 / D, LN_EPS, ALU.mult, ALU.add),
                 r=[pb5[0:1, 0:512]], w=[R2ROW[0:1, 0:512]])
            P.op("vector", lambda e: e.tensor_tensor(RROW[0:1, 0:512], MROW[0:1, 0:512],
                                                     MROW[0:1, 0:512], ALU.mult),
                 r=[MROW[0:1, 0:512]], w=[RROW[0:1, 0:512]])
            P.op("vector", lambda e: e.tensor_tensor(R2ROW[0:1, 0:512], R2ROW[0:1, 0:512],
                                                     RROW[0:1, 0:512], ALU.subtract),
                 r=[R2ROW[0:1, 0:512], RROW[0:1, 0:512]], w=[R2ROW[0:1, 0:512]])
            P.op("scalar", lambda e: e.activation(RROW[0:1, 0:512], R2ROW[0:1, 0:512], AF.Sqrt),
                 r=[R2ROW[0:1, 0:512]], w=[RROW[0:1, 0:512]])
            P.op("vector", lambda e: e.reciprocal(R2ROW[0:1, 0:512], RROW[0:1, 0:512]),
                 r=[RROW[0:1, 0:512]], w=[R2ROW[0:1, 0:512]])
            P.op("tensor", lambda e: e.matmul(pb3[:, 0:512], ONER, MROW[0:1, 0:512],
                                              start=True, stop=True, skip_group_check=True),
                 r=[ONER, MROW[0:1, 0:512]], w=[pb3[:, 0:512]])
            P.op("tensor", lambda e: e.matmul(pb2[:, 0:512], ONER, R2ROW[0:1, 0:512],
                                              start=True, stop=True, skip_group_check=True),
                 r=[ONER, R2ROW[0:1, 0:512]], w=[pb2[:, 0:512]])
            for dc in range(DC):
                P.op("vector", lambda e, dc=dc, tsl=tsl: e.tensor_tensor(
                    Xf[:, dc, tsl], Xf[:, dc, tsl], pb3[:, 0:512], ALU.subtract),
                    r=[Xf[:, dc, tsl], pb3[:, 0:512]], w=[Xf[:, dc, tsl]])
                P.op("vector", lambda e, dc=dc, tsl=tsl: e.tensor_tensor(
                    XN[:, dc, tsl], Xf[:, dc, tsl], pb2[:, 0:512], ALU.mult),
                    r=[Xf[:, dc, tsl], pb2[:, 0:512]], w=[XN[:, dc, tsl]])
        def load_stripe(dram_ap, dstW, eng=None):
            return P.dma("gpsimd", f"wst{dstW.offset}", dstW, dram_ap,
                         track_read=False)

        def mm_chain(bank, dstW, nt, half):
            return [
                (lambda e, dc=dc, nt=nt, bank=bank, dstW=dstW: e.matmul(
                    bank[:, 0:512], dstW[:, dc, :], XN[:, dc, nt * 512:(nt + 1) * 512],
                    start=(dc == 0), stop=(dc == DC - 1), skip_group_check=True),
                 [dstW[:, dc, :], XN[:, dc, nt * 512:(nt + 1) * 512]],
                 [bank[:, 0:512]])
                for dc in range(half * 8, half * 8 + 8)]

        def proj_group(dstW, out_sb, silu, nt):
            bank = pb[nt]
            P.gop("tensor", mm_chain(bank, dstW, nt, 0))
            P.gop("tensor", mm_chain(bank, dstW, nt, 1))
            if silu:
                P.op("scalar", lambda e, nt=nt, bank=bank, out_sb=out_sb: e.activation(
                    out_sb[:, nt * 512:(nt + 1) * 512], bank[:, 0:512], AF.Silu),
                    r=[bank[:, 0:512]], w=[out_sb[:, nt * 512:(nt + 1) * 512]])
            else:
                P.op("scalar", lambda e, nt=nt, bank=bank, out_sb=out_sb: e.copy(
                    out_sb[:, nt * 512:(nt + 1) * 512], bank[:, 0:512]),
                    r=[bank[:, 0:512]], w=[out_sb[:, nt * 512:(nt + 1) * 512]])

        def emit_masks(g, hls=range(4)):
            par = g % 2
            QDEC, DDEC, KDEC = QDEC2[par], DDEC2[par], KDEC2[par]
            for hl2 in hls:
                h2 = g * 4 + hl2
                P.op("scalar", lambda e, hl2=hl2, h2=h2, QDEC=QDEC: e.activation(
                    QDEC[:, hl2, :], QDI, AF.Exp, scale=SLP[:, h2:h2 + 1]),
                    r=[QDI, SLP[:, h2:h2 + 1]], w=[QDEC[:, hl2, :]])
                P.op("scalar", lambda e, hl2=hl2, h2=h2, DDEC=DDEC: e.activation(
                    DDEC[:, hl2, :], DDI, AF.Exp, scale=SLP[:, h2:h2 + 1]),
                    r=[DDI, SLP[:, h2:h2 + 1]], w=[DDEC[:, hl2, :]])
                P.op("scalar", lambda e, hl2=hl2, h2=h2, KDEC=KDEC: e.activation(
                    KDEC[:, hl2 * 2:hl2 * 2 + 2], KDI, AF.Exp, scale=SLP[:, h2:h2 + 1]),
                    r=[KDI, SLP[:, h2:h2 + 1]], w=[KDEC[:, hl2 * 2:hl2 * 2 + 2]])
                P.op("scalar", lambda e, h2=h2: e.activation(
                    BDP[:, h2 * 4:(h2 + 1) * 4], BDIc, AF.Exp, scale=SLP[:, h2:h2 + 1]),
                    r=[BDIc, SLP[:, h2:h2 + 1]], w=[BDP[:, h2 * 4:(h2 + 1) * 4]])

        def b_produce(h, hl, hp, blk):
            base = blk * 256
            bp = blk % 2
            mpar = (h // 4) % 2
            QDEC, DDEC, KDEC = QDEC2[mpar], DDEC2[mpar], KDEC2[mpar]
            VT, KT, QKP = VTD[bp], KTD[bp], QKPD[bp]
            QS, KS, VS = QS2[hp], KS2[hp], VS2[hp]
            for sc in range(2):
                P.op("tensor", lambda e, sc=sc, base=base, KS=KS, QS=QS: e.matmul(
                    pb[4 + sc][:, 0:256], KS[:, base + sc * 128:base + sc * 128 + 128],
                    QS[:, base:base + 256], start=True, stop=True, skip_group_check=True),
                    r=[KS[:, base + sc * 128:base + sc * 128 + 128], QS[:, base:base + 256]],
                    w=[pb[4 + sc][:, 0:256]])
            for sc in range(2):
                P.op("tensor", lambda e, sc=sc, base=base, KS=KS: e.transpose(
                    pb2[:, sc * 64:sc * 64 + 64].bitcast(BF16),
                    KS[:, base + sc * 128:base + sc * 128 + 128], IDN),
                    r=[KS[:, base + sc * 128:base + sc * 128 + 128], IDN],
                    w=[pb2[:, sc * 64:sc * 64 + 64]])
                P.op("tensor", lambda e, sc=sc, base=base, VS=VS: e.transpose(
                    pb3[:, sc * 64:sc * 64 + 64].bitcast(BF16),
                    VS[:, base + sc * 128:base + sc * 128 + 128], IDN),
                    r=[VS[:, base + sc * 128:base + sc * 128 + 128], IDN],
                    w=[pb3[:, sc * 64:sc * 64 + 64]])
            for sc in range(2):
                P.op("vector", lambda e, sc=sc, hl=hl, QKP=QKP: e.tensor_tensor(
                    QKP[sc], pb[4 + sc][:, 0:256], DDEC[:, hl, sc * 256:(sc + 1) * 256],
                    ALU.mult),
                    r=[pb[4 + sc][:, 0:256], DDEC[:, hl, sc * 256:(sc + 1) * 256]],
                    w=[QKP[sc]])
            P.op("vector", lambda e, h=h, hl=hl, base=base, QS=QS: e.tensor_tensor(
                QDl[:, h, base:base + 256], QS[:, base:base + 256], QDEC[:, hl, :],
                ALU.mult),
                r=[QS[:, base:base + 256], QDEC[:, hl, :]], w=[QDl[:, h, base:base + 256]])
            for sc in range(2):
                P.op("vector", lambda e, sc=sc, hl=hl, KT=KT: e.tensor_scalar(
                    KT[sc], pb2[:, sc * 64:sc * 64 + 64].bitcast(BF16),
                    KDEC[:, hl * 2 + sc:hl * 2 + sc + 1], None, ALU.mult),
                    r=[pb2[:, sc * 64:sc * 64 + 64], KDEC[:, hl * 2 + sc:hl * 2 + sc + 1]],
                    w=[KT[sc]])
                P.op("vector", lambda e, sc=sc, VT=VT: e.tensor_copy(
                    VT[sc], pb3[:, sc * 64:sc * 64 + 64].bitcast(BF16)),
                    r=[pb3[:, sc * 64:sc * 64 + 64]], w=[VT[sc]])

        def b_consume(h, hl, hp, blk):
            base = blk * 256
            bp = blk % 2
            VT, KT, QKP = VTD[bp], KTD[bp], QKPD[bp]
            P.op("tensor", lambda e, VT=VT, QKP=QKP: e.matmul(
                pb6[:, 0:256], VT[0], QKP[0],
                start=True, stop=False, skip_group_check=True),
                r=[VT[0], QKP[0]], w=[pb6[:, 0:256]])
            P.op("tensor", lambda e, blk=blk, VT=VT, QKP=QKP: e.matmul(
                pb6[:, 0:256], VT[1], QKP[1],
                start=False, stop=(blk == 0), skip_group_check=True),
                r=[VT[1], QKP[1]], w=[pb6[:, 0:256]])
            if blk > 0:
                P.op("tensor", lambda e, h=h, base=base: e.matmul(
                    pb6[:, 0:256], KVB, QDl[:, h, base:base + 256],
                    start=False, stop=True, skip_group_check=True),
                    r=[KVB, QDl[:, h, base:base + 256]], w=[pb6[:, 0:256]])
            P.op("tensor", lambda e, KT=KT, VT=VT: e.matmul(
                pb7[:, 0:128], KT[0], VT[0],
                start=True, stop=False, skip_group_check=True),
                r=[KT[0], VT[0]], w=[pb7[:, 0:128]])
            P.op("tensor", lambda e, KT=KT, VT=VT: e.matmul(
                pb7[:, 0:128], KT[1], VT[1],
                start=False, stop=True, skip_group_check=True),
                r=[KT[1], VT[1]], w=[pb7[:, 0:128]])
            if blk == 0:
                P.op("vector", lambda e, h=h: e.tensor_copy(KV[:, h, :], pb7[:, 0:128]),
                     r=[pb7[:, 0:128]], w=[KV[:, h, :]])
            else:
                P.op("vector", lambda e, h=h: e.tensor_scalar(
                    KV[:, h, :], KV[:, h, :], BDP[:, h * 4 + 1:h * 4 + 2], None, ALU.mult),
                    r=[KV[:, h, :], BDP[:, h * 4 + 1:h * 4 + 2]], w=[KV[:, h, :]])
                P.op("vector", lambda e, h=h: e.tensor_tensor(
                    KV[:, h, :], KV[:, h, :], pb7[:, 0:128], ALU.add),
                    r=[KV[:, h, :], pb7[:, 0:128]], w=[KV[:, h, :]])
            if blk < NBLK - 1:
                P.op("vector", lambda e, h=h: e.tensor_copy(KVB, KV[:, h, :]),
                     r=[KV[:, h, :]], w=[KVB])
            P.op("scalar", lambda e, h=h, base=base: e.copy(
                O_[:, h, base:base + 256], pb6[:, 0:256]),
                r=[pb6[:, 0:256]], w=[O_[:, h, base:base + 256]])

        emit_masks(0)
        load_stripe(d_qkvu[0], WQB, eng="scalar")
        load_stripe(d_qkvu[16], WKB, eng="scalar")
        load_stripe(d_qkvu[32], WVB, eng="scalar")
        ln_half(0)
        proj_group(WQB, QS2[0], True, 0)
        proj_group(WKB, KS2[0], True, 0)
        ln_half(1)
        proj_group(WVB, VS2[0], False, 0)
        proj_group(WQB, QS2[0], True, 1)
        proj_group(WKB, KS2[0], True, 1)
        proj_group(WVB, VS2[0], False, 1)

        for h in range(NH):
            hl = h % 4
            hp = h % 2
            if h + 4 < NH:
                emit_masks(h // 4 + 1, [hl])
            if h == 2:
                P.dma("gpsimd", "bw", BWL, d_bw, track_read=False)
                P.dma("gpsimd", "aT", ALH, d_aT, track_read=False)
                P.dma("gpsimd", "rT", RTR, d_rT, track_read=False)
            nxt = []
            if h + 1 < NH:
                nxt = [(d_qkvu[h + 1], WQB, QS2[1 - hp], True),
                       (d_qkvu[16 + h + 1], WKB, KS2[1 - hp], True),
                       (d_qkvu[32 + h + 1], WVB, VS2[1 - hp], False)]
            b_produce(h, hl, hp, 0)
            b_produce(h, hl, hp, 1)
            b_consume(h, hl, hp, 0)
            if nxt:
                load_stripe(nxt[0][0], nxt[0][1], eng="scalar")
            b_consume(h, hl, hp, 1)
            if nxt:
                proj_group(nxt[0][1], nxt[0][2], nxt[0][3], 0)
            b_produce(h, hl, hp, 2)
            if nxt:
                load_stripe(nxt[1][0], nxt[1][1], eng="scalar")
                proj_group(nxt[0][1], nxt[0][2], nxt[0][3], 1)
            b_consume(h, hl, hp, 2)
            if nxt:
                proj_group(nxt[1][1], nxt[1][2], nxt[1][3], 0)
            b_produce(h, hl, hp, 3)
            if nxt:
                load_stripe(nxt[2][0], nxt[2][1])
                proj_group(nxt[1][1], nxt[1][2], nxt[1][3], 1)
            b_consume(h, hl, hp, 3)
            if nxt:
                proj_group(nxt[2][1], nxt[2][2], nxt[2][3], 0)
                proj_group(nxt[2][1], nxt[2][2], nxt[2][3], 1)

            if dbg and h == 1:
                dbg_n[0] += 1; P.dma("scalar", f"dbg{dbg_n[0]}", dbg_o["dbg_q"], QS2[1], track_write=False)
                dbg_n[0] += 1; P.dma("scalar", f"dbg{dbg_n[0]}", dbg_o["dbg_k"], KS2[1], track_write=False)
                dbg_n[0] += 1; P.dma("scalar", f"dbg{dbg_n[0]}", dbg_o["dbg_v"], VS2[1], track_write=False)

        u_loaded = [False] * (NH + 1)

        def u_load(h):
            if h < NH and not u_loaded[h]:
                load_stripe(d_qkvu[48 + h], [WQB, WBB][h % 2])
                u_loaded[h] = True

        u_load(0)
        u_load(1)
        P.dma("gpsimd", "sl", d_slocal, KVINf)
        m_cc = P.op("gpsimd", lambda e: e.collective_compute(
            "AllGather", ALU.bypass, replica_groups=[[0, 1], [2, 3], [4, 5], [6, 7]],
            ins=[d_slocal.opt()], outs=[d_sall.opt()]),
            r=[d_slocal], w=[d_sall], mark_sem="CC", mark_amt=None)

        for nt in range(NT):
            P.gop("tensor", [
                (lambda e, dc=dc, nt=nt: e.matmul(
                    pb5[:, 0:512], ALH[:, dc, :], XN[:, dc, nt * 512:(nt + 1) * 512],
                    start=(dc == 0), stop=(dc == DC - 1), skip_group_check=True),
                 [ALH[:, dc, :], XN[:, dc, nt * 512:(nt + 1) * 512]], [pb5[:, 0:512]])
                for dc in range(DC)])
            P.op("scalar", lambda e, nt=nt: e.copy(HSB[:, nt * 512:(nt + 1) * 512], pb5[:, 0:512]),
                 r=[pb5[:, 0:512]], w=[HSB[:, nt * 512:(nt + 1) * 512]])
        for nt in range(NT):
            P.gop("tensor", [
                (lambda e, dc=dc, nt=nt: e.matmul(
                    pb3[0:8, 0:512], RTR[:, dc, :], XN[:, dc, nt * 512:(nt + 1) * 512],
                    start=(dc == 0), stop=(dc == DC - 1), skip_group_check=True),
                 [RTR[:, dc, :], XN[:, dc, nt * 512:(nt + 1) * 512]], [pb3[0:8, 0:512]])
                for dc in range(DC)])
            P.op("vector", lambda e, nt=nt: e.tensor_copy(
                LTF[:, nt * 512:(nt + 1) * 512], pb3[0:8, 0:512]),
                r=[pb3[0:8, 0:512]], w=[LTF[:, nt * 512:(nt + 1) * 512]])
        for i in range(8):
            P.op("tensor", lambda e, i=i: e.transpose(
                pb2[:, 0:8], LTF[:, i * 128:(i + 1) * 128], IDF8),
                r=[LTF[:, i * 128:(i + 1) * 128], IDF8], w=[pb2[:, 0:8]])
            P.op("vector", lambda e: e.tensor_copy(LTT, pb2[:, 0:8]),
                 r=[pb2[:, 0:8]], w=[LTT])
            P.op("vector", lambda e: e.tensor_reduce(MXC, LTT, axis=AX.X, op=ALU.max),
                 r=[LTT], w=[MXC])
            P.op("vector", lambda e: e.tensor_scalar(OHT, LTT, MXC, None, ALU.is_equal),
                 r=[LTT, MXC], w=[OHT])
            P.op("tensor", lambda e: e.transpose(
                pb2[0:8, 128:192].bitcast(BF16), OHT, IDN),
                r=[OHT, IDN], w=[pb2[0:8, 128:192]])
            P.op("vector", lambda e, i=i: e.tensor_copy(
                MFB[:, i * 128:(i + 1) * 128], pb2[0:8, 128:192].bitcast(BF16)),
                r=[pb2[0:8, 128:192]], w=[MFB[:, i * 128:(i + 1) * 128]])
        for nt in range(NT):
            P.op("tensor", lambda e, nt=nt: e.matmul(
                pb6[:, 0:512], E2C, MFB[:, nt * 512:(nt + 1) * 512],
                start=True, stop=True, skip_group_check=True),
                r=[E2C, MFB[:, nt * 512:(nt + 1) * 512]], w=[pb6[:, 0:512]])
            P.op("vector", lambda e, nt=nt: e.tensor_tensor(
                HMSK[:, nt * 512:(nt + 1) * 512], HSB[:, nt * 512:(nt + 1) * 512],
                pb6[:, 0:512], ALU.mult),
                r=[HSB[:, nt * 512:(nt + 1) * 512], pb6[:, 0:512]],
                w=[HMSK[:, nt * 512:(nt + 1) * 512]])

        if dbg:
            dbg_n[0] += 1; P.dma("scalar", f"dbg{dbg_n[0]}", dbg_o["dbg_hm"], HMSK, track_write=False)


        GBH = [c(o_qs + i * 1024, 1024) for i in range(2)]
        for hf in range(2):
            gbb = GBH[hf]
            csl = slice(hf * 1024, (hf + 1) * 1024)
            P.dma("sync", f"gb{hf}", gbb, d_sall[0:128, csl])
            P.op("vector", lambda e, gbb=gbb, csl=csl: e.tensor_scalar(
                KVINf[:, csl], gbb, MSK8[:, 0:1], None, ALU.mult),
                r=[gbb, MSK8[:, 0:1]], w=[KVINf[:, csl]])

        if dbg:
            dbg_n[0] += 1; P.dma("scalar", f"dbg{dbg_n[0]}", dbg_o["dbg_kvin"], KVINf, track_write=False)

        RSB = c(o_qs + 1792, 256)

        def emit_umult(blk, heads):
            base = blk * 256
            for h in heads:
                eng = "vector"
                P.op(eng, lambda e, h=h, base=base: e.tensor_tensor(
                    O_[:, h, base:base + 256], O_[:, h, base:base + 256],
                    U_[:, h, base:base + 256], ALU.mult),
                    r=[O_[:, h, base:base + 256], U_[:, h, base:base + 256]],
                    w=[O_[:, h, base:base + 256]])

        def emit_rmult(blk):
            base = blk * 256
            P.op("tensor", lambda e, base=base: e.matmul(
                pb3[:, 0:256], ONER, RROW4[0:1, base:base + 256],
                start=True, stop=True, skip_group_check=True),
                r=[ONER, RROW4[0:1, base:base + 256]], w=[pb3[:, 0:256]])
            P.op("scalar", lambda e: e.copy(RSB, pb3[:, 0:256]),
                 r=[pb3[:, 0:256]], w=[RSB])
            for h in range(NH):
                eng = "vector"
                P.op(eng, lambda e, h=h, base=base: e.tensor_tensor(
                    O_[:, h, base:base + 256], O_[:, h, base:base + 256], RSB,
                    ALU.mult),
                    r=[O_[:, h, base:base + 256], RSB], w=[O_[:, h, base:base + 256]])

        ugen = []

        def u_head(h):
            u_load(h)
            u_load(h + 1)
            WUc = [WQB, WBB][h % 2]
            for nt in range(NT):
                bank = pb[nt]
                P.gop("tensor", mm_chain(bank, WUc, nt, 0))
                P.gop("tensor", mm_chain(bank, WUc, nt, 1))
                P.op("scalar", lambda e, nt=nt, h=h, bank=bank: e.copy(
                    U_[:, h, nt * 512:(nt + 1) * 512], bank[:, 0:512]),
                    r=[bank[:, 0:512]], w=[U_[:, h, nt * 512:(nt + 1) * 512]])

        cbank = [pb2, pb5, pb6, pb7]
        uh = [0]
        pend_um = []
        rm_done = [0]

        def flush_sched():
            keep = []
            for b, heads in pend_um:
                if max(heads) < uh[0]:
                    emit_umult(b, heads)
                else:
                    keep.append((b, heads))
            pend_um[:] = keep

        while uh[0] < 3:
            u_head(uh[0]); uh[0] += 1
        for blk in range(NBLK):
            base = blk * 256
            for h0 in range(0, NH, 4):
                if blk > 0 and h0 == 4 and rm_done[0] < blk:
                    emit_rmult(rm_done[0]); rm_done[0] += 1
                if uh[0] < NH:
                    u_head(uh[0]); uh[0] += 1
                for i in range(4):
                    h = h0 + i
                    P.op("vector", lambda e, h=h, blk=blk, i=i: e.tensor_scalar(
                        KVS4[i], KVINf[:, h * 128:(h + 1) * 128],
                        BDP[:, h * 4 + blk:h * 4 + blk + 1], None, ALU.mult),
                        r=[KVINf[:, h * 128:(h + 1) * 128], BDP[:, h * 4 + blk:h * 4 + blk + 1]],
                        w=[KVS4[i]])
                for i in range(4):
                    h = h0 + i
                    P.op("tensor", lambda e, h=h, base=base, i=i: e.matmul(
                        cbank[i][:, 0:256], KVS4[i], QDl[:, h, base:base + 256],
                        start=True, stop=True, skip_group_check=True),
                        r=[KVS4[i], QDl[:, h, base:base + 256]], w=[cbank[i][:, 0:256]])
                for i in range(4):
                    h = h0 + i
                    P.op("vector", lambda e, h=h, base=base, i=i: e.tensor_tensor(
                        OSC4[i], O_[:, h, base:base + 256], cbank[i][:, 0:256], ALU.add),
                        r=[O_[:, h, base:base + 256], cbank[i][:, 0:256]], w=[OSC4[i]])
                for i in range(4):
                    P.op("scalar", lambda e, i=i: e.activation(SQB4[i], OSC4[i], AF.Square),
                         r=[OSC4[i]], w=[SQB4[i]])
                for i in range(4):
                    h = h0 + i
                    P.op("tensor", lambda e, h=h, i=i: e.matmul(
                        pb4[0:1, 0:256], ONEB[:, 0:1], SQB4[i],
                        start=(h == 0), stop=(h == NH - 1), skip_group_check=True),
                        r=[ONEB[:, 0:1], SQB4[i]], w=[pb4[0:1, 0:256]])
                for i in range(4):
                    h = h0 + i
                    P.op("vector", lambda e, h=h, base=base, i=i: e.tensor_copy(
                        O_[:, h, base:base + 256], OSC4[i]),
                        r=[OSC4[i]], w=[O_[:, h, base:base + 256]])
                pend_um.append((blk, list(range(h0, h0 + 4))))
                flush_sched()
            P.op("vector", lambda e: e.tensor_scalar(R2ROW[0:1, 0:256], pb4[0:1, 0:256],
                                                     1.0 / HID, RMS_EPS, ALU.mult, ALU.add),
                 r=[pb4[0:1, 0:256]], w=[R2ROW[0:1, 0:256]])
            P.op("scalar", lambda e: e.activation(RROW[0:1, 0:256], R2ROW[0:1, 0:256], AF.Sqrt),
                 r=[R2ROW[0:1, 0:256]], w=[RROW[0:1, 0:256]])
            P.op("vector", lambda e, base=base: e.reciprocal(
                RROW4[0:1, base:base + 256], RROW[0:1, 0:256]),
                r=[RROW[0:1, 0:256]], w=[RROW4[0:1, base:base + 256]])
        while uh[0] < NH:
            u_head(uh[0]); uh[0] += 1
            flush_sched()
        flush_sched()
        assert not pend_um
        while rm_done[0] < NBLK:
            emit_rmult(rm_done[0]); rm_done[0] += 1
        if dbg:
            dbg_n[0] += 1; P.dma("scalar", f"dbg{dbg_n[0]}", dbg_o["dbg_u"], U_.rearrange("p h t -> p (h t)"), track_write=False)

        if dbg:
            dbg_n[0] += 1; P.dma("scalar", f"dbg{dbg_n[0]}", dbg_o["dbg_g"], O_.rearrange("p h t -> p (h t)"), track_write=False)

        WOUTB = [WOB, WB2]
        WBASB = [WBB, WB3]
        load_stripe(d_wout[0], WOUTB[0])
        load_stripe(d_wbase[0], WBASB[0])
        for oc in range(16):
            if oc + 1 < 16:
                load_stripe(d_wout[oc + 1], WOUTB[(oc + 1) % 2])
                load_stripe(d_wbase[oc + 1], WBASB[(oc + 1) % 2])
            WOc, WBc = WOUTB[oc % 2], WBASB[oc % 2]
            for nt in range(NT):
                ti = oc * 2 + nt
                bank = pb[ti % 4]
                tsl = slice(nt * 512, (nt + 1) * 512)
                xsl = ti % 3
                P.dma("sync", f"xr{xsl}", XRS[xsl], d_x[oc * 128:(oc + 1) * 128, tsl],
                      track_read=False)
                for half in range(2):
                    P.gop("tensor", [
                        (lambda e, dc=dc, nt=nt, bank=bank, WBc=WBc: e.matmul(
                            bank[:, 0:512], WBc[:, dc, :], XN[:, dc, nt * 512:(nt + 1) * 512],
                            start=(dc == 0), stop=False, skip_group_check=True),
                         [WBc[:, dc, :], XN[:, dc, nt * 512:(nt + 1) * 512]], [bank[:, 0:512]])
                        for dc in range(half * 8, half * 8 + 8)])
                for half in range(2):
                    items = [
                        (lambda e, hc=hc, nt=nt, bank=bank, WOc=WOc: e.matmul(
                            bank[:, 0:512], WOc[:, hc, :], O_[:, hc, nt * 512:(nt + 1) * 512],
                            start=False, stop=False, skip_group_check=True),
                         [WOc[:, hc, :], O_[:, hc, nt * 512:(nt + 1) * 512]], [bank[:, 0:512]])
                        for hc in range(half * 8, half * 8 + 8)]
                    if half == 1:
                        items.append(
                            (lambda e, oc=oc, nt=nt, bank=bank: e.matmul(
                                bank[:, 0:512], BWL[:, oc * 128:(oc + 1) * 128],
                                HMSK[:, nt * 512:(nt + 1) * 512],
                                start=False, stop=True, skip_group_check=True),
                             [BWL[:, oc * 128:(oc + 1) * 128],
                              HMSK[:, nt * 512:(nt + 1) * 512]], [bank[:, 0:512]]))
                    P.gop("tensor", items)
                osl = ti % 2
                P.op("scalar", lambda e, oc=oc, osl=osl, bank=bank: e.activation(
                    OST[osl], bank[:, 0:512], AF.Identity, bias=BBT[:, oc:oc + 1]),
                    r=[bank[:, 0:512], BBT[:, oc:oc + 1]], w=[OST[osl]])
                P.op("vector", lambda e, osl=osl, xsl=xsl: e.tensor_tensor(
                    OST[osl], OST[osl], XRS[xsl], ALU.add),
                    r=[OST[osl], XRS[xsl]], w=[OST[osl]])
                P.dma("scalar", f"y{osl}", d_y[oc * 128:(oc + 1) * 128, tsl], OST[osl],
                      track_write=False)

        final_waits = [(s, P.cnt[s]) for s in P.cnt
                       if s.startswith("dma_y") or s.startswith("dma_dbg")]
        P.barrier_wait("scalar", final_waits)

        sem_names = ["PE", "DV", "AC", "PL", "CC"] + sorted(
            s for s in P.cnt if s.startswith("dma_"))
        sems = {n: ctx.enter_context(nc.semaphore(n)) for n in sem_names}
        block = ctx.enter_context(nc.Block())

        @block.tensor
        def _(eng):
            P.replay("tensor", eng, sems)

        @block.vector
        def _(eng):
            P.replay("vector", eng, sems)

        @block.scalar
        def _(eng):
            P.replay("scalar", eng, sems)

        @block.gpsimd
        def _(eng):
            P.replay("gpsimd", eng, sems)

        @block.sync
        def _(eng):
            P.replay("sync", eng, sems)

    return nc


def _host_constants():
    QDI = np.broadcast_to(-(np.arange(256, dtype=np.float32) + 1.0), (128, 256)).copy()
    DDI = np.empty((128, 512), np.float32)
    m = np.arange(256, dtype=np.float32)
    for sc in range(2):
        for p in range(128):
            s = sc * 128 + p
            DDI[p, sc * 256:(sc + 1) * 256] = np.where(m >= s, -(m - s), DDBIG)
    KDI = np.empty((128, 2), np.float32)
    for j in range(2):
        KDI[:, j] = -(255.0 - (j * 128 + np.arange(128, dtype=np.float32)))
    BDIc = np.broadcast_to(np.array([0.0, -256.0, -512.0, -768.0], np.float32), (128, 4)).copy()
    identB = np.eye(128, dtype=BF)
    E2 = np.zeros((8, 128), BF)
    for e in range(8):
        E2[e, e * 16:(e + 1) * 16] = BF(SCALING)
    IDF8 = np.eye(8, dtype=np.float32)
    return QDI, DDI, KDI, BDIc, identB, E2, IDF8


def kernel(L0_attn_x, L0_attn_slope_rate, L0_attn_W_qkvu, L0_attn_W_out,
           L0_ffn_W_base, L0_ffn_b_base, L0_ffn_router_W, L0_ffn_A, L0_ffn_Bw):
    x = np.ascontiguousarray(np.asarray(L0_attn_x, dtype=np.float32)).reshape(8192, D)
    Wqkvu = np.asarray(L0_attn_W_qkvu, np.float32)
    Wout = np.asarray(L0_attn_W_out, np.float32)
    Wbase = np.asarray(L0_ffn_W_base, np.float32)
    bbase = np.asarray(L0_ffn_b_base, np.float32)
    rW = np.asarray(L0_ffn_router_W, np.float32)
    A = np.asarray(L0_ffn_A, np.float32)
    Bw = np.asarray(L0_ffn_Bw, np.float32)
    slope = np.asarray(L0_attn_slope_rate, np.float32).reshape(NH)

    qkvuS = np.ascontiguousarray(Wqkvu.reshape(64, 128, DC, 128).transpose(0, 3, 2, 1))
    woutS = np.ascontiguousarray(Wout.reshape(16, 128, DC, 128).transpose(0, 3, 2, 1))
    wbaseS = np.ascontiguousarray(Wbase.reshape(16, 128, DC, 128).transpose(0, 3, 2, 1))
    aT = np.ascontiguousarray(A.reshape(E * RK, DC, 128).transpose(2, 1, 0))
    rT = np.ascontiguousarray(rW.reshape(E, DC, 128).transpose(2, 1, 0))
    bwF = np.ascontiguousarray(Bw.transpose(0, 2, 1).reshape(E * RK, HID))
    bT = np.ascontiguousarray(bbase.reshape(16, 128).T)
    slopeB = np.broadcast_to(slope.reshape(1, NH), (128, NH)).copy()
    QDI, DDI, KDI, BDIc, identB, E2, IDF8 = _host_constants()

    shared = dict(qkvuS=qkvuS, woutS=woutS, wbaseS=wbaseS, aT=aT, rT=rT, bwF=bwF,
                  bT=bT, slopeB=slopeB, QDI=QDI, DDI=DDI, KDI=KDI, BDIc=BDIc,
                  identB=identB, E2=E2, IDF8=IDF8)
    in_maps = []
    for cc in range(8):
        m8 = np.zeros((128, 8), np.float32)
        if cc % 2 == 1:
            m8[:, 0] = 1.0
        xc = np.ascontiguousarray(x[cc * T:(cc + 1) * T].T)
        in_maps.append(dict(x=xc, maskv8=m8, **shared))

    nc = build_nc()
    res = run_bass_kernel_spmd(nc, in_maps, core_ids=list(range(8)))
    out = np.empty((8192, D), np.float32)
    for cc in range(8):
        out[cc * T:(cc + 1) * T] = res.results[cc]["y"].T
    return out.reshape(4, 2048, D)


if __name__ == "__main__":
    import reference as _r
    inp = _r.setup_inputs()
    got = kernel(**{k: np.asarray(v) for k, v in inp.items()})
    want = np.asarray(_r.reference(**inp))
    num = float(((got - want) ** 2).mean())
    den = float((want ** 2).mean())
    print("res-var ratio:", num / den)
